# version 08
# baseline (speedup 1.0000x reference)
import numpy as np
import ml_dtypes
from contextlib import ExitStack

import concourse.bass as bass
import concourse.mybir as mybir
from concourse.bass_utils import run_bass_kernel_spmd

F32 = mybir.dt.float32
BF16 = mybir.dt.bfloat16
AF = mybir.ActivationFunctionType
ALU = mybir.AluOpType
AX = mybir.AxisListType
BF = ml_dtypes.bfloat16

NCORES = 8
T = 1024
SEQ = 8192
D = 2048
NEG = -1.0e9

ENGINES = ("sync", "tensor", "vector", "scalar", "gpsimd")



class Prog:

    def __init__(self, nc):
        self.nc = nc
        self.q = {e: [] for e in ENGINES}
        self.semvals = {}
        self.waited = {e: {} for e in ENGINES}

    def _key(self, sem):
        return sem.name if hasattr(sem, "name") else sem

    def inc(self, sem, n=1):
        k = self._key(sem)
        self.semvals[k] = self.semvals.get(k, 0) + n
        return self.semvals[k]

    def val(self, sem):
        return self.semvals.get(self._key(sem), 0)

    def tok(self, sem):
        return (sem, self.val(sem))

    def emit(self, eng, fn):
        self.q[eng].append(fn)

    def wait(self, eng, sem, v=None):
        if v is None:
            v = self.val(sem)
        if v <= 0:
            return
        k = self._key(sem)
        if self.waited[eng].get(k, 0) >= v:
            return
        self.waited[eng][k] = v
        self.emit(eng, lambda e, s=sem, vv=v: e.wait_ge(s, vv))

    def wait_tok(self, eng, tok):
        if tok is None:
            return
        if isinstance(tok[0], tuple):
            for t in tok:
                if t is not None:
                    self.wait(eng, t[0], t[1])
            return
        self.wait(eng, tok[0], tok[1])

    def op(self, eng, method, *args, sem=None, n=1, after=(), **kwargs):
        for t in after:
            self.wait_tok(eng, t)
        v = self.inc(sem, n) if sem is not None else None

        def f(e, m=method, a=args, kw=kwargs, s=sem, nn=n):
            inst = getattr(e, m)(*a, **kw)
            if s is not None:
                inst.then_inc(s, nn)
        self.emit(eng, f)
        return (sem, v) if sem is not None else None

    def finalize(self, block):
        for name in ENGINES:
            fns = self.q[name]

            def runner(eng, fns=fns):
                for fn in fns:
                    fn(eng)
            getattr(block, name)(runner)


class Alloc:
    def __init__(self, nc):
        self.nc = nc
        self.stack = ExitStack()
        self.stage_stack = None
        self.prefix = ""

    def _st(self):
        return self.stage_stack if self.stage_stack is not None else self.stack

    def sb(self, name, shape, dtype):
        return self._st().enter_context(self.nc.sbuf_tensor(self.prefix + name, shape, dtype))

    def ps(self, name, shape, dtype=F32):
        return self._st().enter_context(self.nc.psum_tensor(self.prefix + name, shape, dtype))

    def sem(self, name):
        return self.stack.enter_context(self.nc.semaphore(name))

    def block(self):
        return self.stack.enter_context(self.nc.Block())

    def begin_stage(self, prefix):
        assert self.stage_stack is None
        self.stage_stack = ExitStack()
        self.prefix = prefix + "_"

    def end_stage(self):
        self.stage_stack.close()
        self.stage_stack = None
        self.prefix = ""

    def close(self):
        if self.stage_stack is not None:
            self.end_stage()
        self.stack.close()


class Ctx:
    def __init__(self, name):
        self.nc = bass.Bass("TRN2", target_bir_lowering=False)
        self.al = Alloc(self.nc)
        self.p = Prog(self.nc)
        self.name = name
        self.ins = {}
        self.outs = {}
        self.ld = self.al.sem("s_ld")
        self.ls = self.al.sem("s_ls")
        self.pe = self.al.sem("s_pe")
        self.dv = self.al.sem("s_dv")
        self.sc = self.al.sem("s_sc")
        self.gp = self.al.sem("s_gp")
        self.st = self.al.sem("s_st")
        self.cc = self.al.sem("s_cc")

    def din(self, name, shape, dtype=F32):
        if name in self.ins:
            return self.ins[name]
        ap = self.nc.dram_tensor(name, list(shape), dtype, kind="ExternalInput").ap()
        self.ins[name] = ap
        return ap

    def dint(self, name, shape, dtype=F32, addr_space="Local"):
        ap = self.nc.dram_tensor(name, list(shape), dtype, addr_space=addr_space).ap()
        return ap

    def barrier(self):
        p = self.p
        sems = [self.ld, self.ls, self.pe, self.dv, self.sc, self.gp, self.st]
        for eng in ENGINES:
            for sm in sems:
                p.wait(eng, sm)

    def dout(self, name, shape, dtype=F32):
        ap = self.nc.dram_tensor(name, list(shape), dtype, kind="ExternalOutput").ap()
        self.outs[name] = ap
        return ap

    def common(self):
        _st, _pf = self.al.stage_stack, self.al.prefix
        self.al.stage_stack, self.al.prefix = None, ""
        self.ident = self.al.sb("c_ident", [128, 128], BF16)
        self.ones = self.al.sb("c_ones", [128, 128], BF16)
        self.ones32 = self.al.sb("c_ones32", [1, 128], F32)
        self.al.stage_stack, self.al.prefix = _st, _pf
        self.p.op("vector", "memset", self.ones[:, :], 1.0, sem=self.dv)
        self.p.op("vector", "memset", self.ones32[:, :], 1.0, sem=self.dv)
        self.p.op("sync", "dma_start", sem=self.ls, n=16, out=self.ident[:, :],
                  in_=self.ins["ident"][:, :])

    def finish(self):
        self.p.wait("sync", self.st)
        blk = self.al.block()
        self.p.finalize(blk)
        self.al.close()
        return self.nc


class SlotPool:

    def __init__(self, tiles):
        self.tiles = [[t, None] for t in tiles]
        self.i = 0

    def take(self):
        t = self.tiles[self.i % len(self.tiles)]
        self.i += 1
        return t



def load_weight(ctx, wsb, col0, dram, fan_in, fan_out, after=()):
    nk = fan_in // 128
    return ctx.p.op("gpsimd", "dma_start", sem=ctx.ld, n=16, after=after,
                    out=wsb[:, col0:col0 + nk * fan_out].rearrange("p (k m) -> p k m", k=nk),
                    in_=dram.rearrange("(k p) m -> p k m", p=128))


def wslice(wsb, col0, fan_out, k, m, mw=128):
    return wsb[:, col0 + k * fan_out + m * 128:col0 + k * fan_out + m * 128 + mw]


def emit_ln(ctx, x_tile_ap, lnx_tm_ap, scr_sq, stats, toks, eps=1e-5):
    p = ctx.p
    prev = getattr(ctx, "ln_prev", ())
    t_sq = p.op("scalar", "activation", scr_sq[:, :], x_tile_ap, AF.Square,
                accum_out=stats["ssq"][:, :], sem=ctx.sc, after=tuple(toks) + tuple(prev))
    t_sm = p.op("vector", "reduce_sum", stats["smean"][:, :], x_tile_ap, axis=AX.X,
                sem=ctx.dv, after=tuple(toks) + tuple(prev))
    t_m = p.op("vector", "tensor_scalar_mul", stats["m"][:, :], stats["smean"][:, :],
               1.0 / D, sem=ctx.dv, after=(t_sm,))
    t_msq = p.op("vector", "tensor_mul", stats["var"][:, :], stats["m"][:, :],
                 stats["m"][:, :], sem=ctx.dv, after=(t_m,))
    t_v = p.op("vector", "scalar_tensor_tensor", stats["var"][:, :], stats["ssq"][:, :],
               1.0 / D, stats["var"][:, :], ALU.mult, ALU.subtract, sem=ctx.dv,
               after=(t_msq, t_sq))
    t_e = p.op("vector", "tensor_scalar_add", stats["var"][:, :], stats["var"][:, :],
               eps, sem=ctx.dv, after=(t_v,))
    t_s = p.op("scalar", "activation", stats["rstd"][:, :], stats["var"][:, :], AF.Sqrt,
               sem=ctx.sc, after=(t_e,))
    t_r = p.op("vector", "reciprocal", stats["rstd"][:, :], stats["rstd"][:, :],
               sem=ctx.dv, after=(t_s,))
    tn = p.op("vector", "tensor_scalar", lnx_tm_ap, x_tile_ap, stats["m"][:, :],
              stats["rstd"][:, :], ALU.subtract, ALU.mult, sem=ctx.dv, after=(t_r,))
    ctx.ln_prev = (tn, t_sq, t_s)
    return tn


def transpose_group(ctx, srcs, dst_ap, ps_slot):
    p = ctx.p
    ps, war = ps_slot
    n = len(srcs)
    if war is not None:
        p.wait_tok("tensor", war)
    tok = None
    for j, s in enumerate(srcs):
        tok = p.op("tensor", "transpose", ps[:, j * 128:(j + 1) * 128], s,
                   ctx.ident[:, :], sem=(ctx.pe if j == n - 1 else None))
    ev = p.op("vector", "tensor_copy", dst_ap, ps[:, 0:n * 128], sem=ctx.dv, after=(tok,))
    ps_slot[1] = ev
    return ev



def _bf(x):
    return np.ascontiguousarray(x).astype(BF)


def _perm_ri(n_pairs):
    return np.concatenate([np.arange(0, 2 * n_pairs, 2), np.arange(1, 2 * n_pairs, 2)])


def host_prep(inp):
    g = {}
    x = np.ascontiguousarray(inp["L0_attn_x"].reshape(SEQ, D))
    g["x"] = x
    g["w_cq"] = np.ascontiguousarray(inp["L0_attn_w_cq"])
    g["w_dq_nope"] = np.ascontiguousarray(inp["L0_attn_w_dq_nope"])
    p64 = _perm_ri(32)
    wqr = inp["L0_attn_w_dq_rope"]
    g["w_dq_rope_p"] = np.ascontiguousarray(
        wqr.reshape(1536, 16, 64)[:, :, p64].reshape(1536, 1024))
    g["w_ckv"] = np.ascontiguousarray(inp["L0_attn_w_ckv"])
    g["w_dk_nope"] = np.ascontiguousarray(inp["L0_attn_w_dk_nope"])
    g["w_dv"] = np.ascontiguousarray(inp["L0_attn_w_dv"])
    g["w_k_rope_p"] = np.ascontiguousarray(inp["L0_attn_w_k_rope"][:, p64])
    g["w_proj"] = np.ascontiguousarray(inp["L0_attn_w_proj"])
    g["qg"] = np.ascontiguousarray(inp["L0_attn_q_norm_w"].reshape(12, 128).T)
    g["kvg"] = np.ascontiguousarray(inp["L0_attn_kv_norm_w"].reshape(4, 128).T)
    g["crow"] = np.ascontiguousarray(np.tile(inp["L0_attn_freqs_cos"], (1, 2)).T)
    g["srow"] = np.ascontiguousarray(np.tile(inp["L0_attn_freqs_sin"], (1, 2)).T)
    g["ident"] = _bf(np.eye(128, dtype=np.float32))
    psw = np.zeros((128, 128), np.float32)
    for j in range(64):
        psw[j, j + 64] = 1.0
        psw[j + 64, j] = 1.0
    g["pswap"] = _bf(psw)
    g["i64"] = _bf(np.eye(64, dtype=np.float32))
    msw = np.zeros((64, 64), np.float32)
    for j in range(32):
        msw[j, j + 32] = 1.0
        msw[j + 32, j] = -1.0
    g["mswap"] = _bf(msw)
    g["W0T"] = np.ascontiguousarray(inp["L0_ffn_W0"].T)
    g["b0col"] = np.ascontiguousarray(inp["L0_ffn_bias0"].reshape(64, 128).T)
    g["W1T"] = np.ascontiguousarray(inp["L0_ffn_W1"].T)
    g["b1row"] = np.ascontiguousarray(inp["L0_ffn_bias1"].reshape(1, D))
    g["Wqkv"] = np.ascontiguousarray(inp["L1_attn_Wqkv"])
    g["qg1"] = np.ascontiguousarray(inp["L1_attn_q_norm_w"].reshape(16, 128).T)
    g["kg1"] = np.ascontiguousarray(inp["L1_attn_k_norm_w"].reshape(16, 128).T)
    g["Wo"] = np.ascontiguousarray(inp["L1_attn_Wo"])
    hd = 128
    inv = 1.0 / (10000.0 ** (np.arange(0, hd, 2, dtype=np.float32) / hd))
    pos = np.arange(SEQ, dtype=np.float32)
    fr = pos[:, None] * inv[None, :]
    ct = np.cos(fr).astype(np.float32).T
    st = np.sin(fr).astype(np.float32).T
    g["cosT"] = np.ascontiguousarray(np.concatenate([ct, ct], 0))
    g["sinT"] = np.ascontiguousarray(np.concatenate([-st, st], 0))
    g["rwT"] = np.ascontiguousarray(inp["L1_ffn_router_weight"].T)
    g["rbrow"] = np.ascontiguousarray(inp["L1_ffn_router_bias"].reshape(1, 8))
    gu = inp["L1_ffn_gate_up_proj"]
    g["gw"] = np.ascontiguousarray(gu[:, :, 0::2])
    g["uw"] = np.ascontiguousarray(gu[:, :, 1::2])
    gub = inp["L1_ffn_gate_up_proj_bias"]
    g["gbcol"] = np.ascontiguousarray(gub[:, 0::2].reshape(8, 16, 128).transpose(0, 2, 1))
    g["ubcol"] = np.ascontiguousarray(gub[:, 1::2].reshape(8, 16, 128).transpose(0, 2, 1))
    g["dw"] = np.ascontiguousarray(inp["L1_ffn_down_proj"])
    g["dbrow"] = np.ascontiguousarray(inp["L1_ffn_down_proj_bias"])
    kk = np.arange(512)[:, None]
    qq = np.arange(512)[None, :]
    mC = np.where(qq >= kk, 0.0, NEG).astype(np.float32)
    mW = np.where(qq < kk, 0.0, NEG).astype(np.float32)
    g["maskC"] = np.ascontiguousarray(mC.reshape(4, 128, 512).transpose(1, 0, 2).reshape(128, 2048))
    g["maskW"] = np.ascontiguousarray(mW.reshape(4, 128, 512).transpose(1, 0, 2).reshape(128, 2048))
    return g




def rmsnorm_post2(ctx, ps_st_region, t, strow, r128, bslot, fan_in, eps, stat_tok):
    p = ctx.p
    c0 = t * 512
    ps_bc, bwar = bslot
    t1 = p.op("vector", "tensor_scalar", strow[:, c0:c0 + 512], ps_st_region,
              1.0 / fan_in, eps, ALU.mult, ALU.add, sem=ctx.dv,
              after=(stat_tok,) + ((ctx.strowg.get(t),) if ctx.strowg.get(t) else ()))
    ctx.psg = t1
    t2 = p.op("scalar", "activation", strow[:, c0:c0 + 512], strow[:, c0:c0 + 512],
              AF.Sqrt, sem=ctx.sc, after=(t1,))
    t3 = p.op("vector", "reciprocal", strow[:, c0:c0 + 512], strow[:, c0:c0 + 512],
              sem=ctx.dv, after=(t2,))
    if bwar is not None:
        p.wait_tok("tensor", bwar)
    t4 = p.op("tensor", "matmul", ps_bc[:, :], ctx.ones32[0:1, 0:128],
              strow[0:1, c0:c0 + 512], start=True, stop=True, sem=ctx.pe, after=(t3,))
    ctx.strowg[t] = t4
    t5 = p.op("vector", "tensor_copy", r128[:, c0:c0 + 512], ps_bc[:, :],
              sem=ctx.dv, after=(t4,))
    bslot[1] = t5
    return t5


def emit_qk_heads(ctx, t, wsb, wcol, nk, rhs_fn, rope_rhs_fn, rope_lhs_fn, o_n, o_r, o_n2,
                  pool, epool, erpool, rpool, sqpool, sq64p, ps_st, str2, rot_tok,
                  rope_src_tok):
    p = ctx.p
    pending = [None]
    for h in range(16):
        slot = pool.take()
        ps, war = slot
        if war is not None:
            p.wait_tok("tensor", war)
        mm = None
        for k in range(nk):
            mm = p.op("tensor", "matmul", ps[:, :], wslice(wsb, wcol, 2048, k, h),
                      rhs_fn(k, t), start=(k == 0), stop=(k == nk - 1),
                      sem=(ctx.pe if k == nk - 1 else None))
        if pending[0] is not None:
            pending[0]()
            pending[0] = None
        es = epool.take()
        ev, ewar = es
        tev = p.op("scalar", "activation", ev[:, :], ps[:, :], AF.Identity, sem=ctx.sc,
                   after=(mm,) + ((ewar,) if ewar is not None else ()))
        es[1] = p.op("sync", "dma_start", sem=ctx.st, n=16,
                     out=o_n[h * 128:(h + 1) * 128, t * 512:(t + 1) * 512],
                     in_=ev[:, :], after=(tev,))
        sqslot = sqpool.take()
        sqa, sqwar = sqslot
        tsq1 = p.op("scalar", "activation", sqa, ps[:, :], AF.Square, sem=ctx.sc,
                    after=(mm,) + ((sqwar,) if sqwar is not None else ()))
        slot[1] = tsq1
        rslot = rpool.take()
        rps, rwar = rslot
        if rwar is not None:
            p.wait_tok("tensor", rwar)
        p.wait_tok("tensor", rot_tok)
        p.wait_tok("tensor", rope_src_tok)
        rmm = p.op("tensor", "matmul", rps, rope_lhs_fn(h),
                   rope_rhs_fn(h, t), start=True, stop=True, sem=ctx.pe)
        ers = erpool.take()
        er, erwar = ers
        trev = p.op("scalar", "activation", er, rps, AF.Identity, sem=ctx.sc,
                    after=(rmm,) + ((erwar,) if erwar is not None else ()))
        ers[1] = p.op("sync", "dma_start", sem=ctx.st, n=16,
                      out=o_r[h * 64:(h + 1) * 64, t * 512:(t + 1) * 512],
                      in_=er, after=(trev,))
        s64 = sq64p.take()
        sqb, sqbwar = s64
        tsq2 = p.op("scalar", "activation", sqb, er, AF.Square, sem=ctx.sc,
                    after=(trev,) + ((sqbwar,) if sqbwar is not None else ()))
        rslot[1] = trev

        def mk_stats(h=h, tsq1=tsq1, tsq2=tsq2, sqa=sqa, sqb=sqb,
                     sqslot=sqslot, s64=s64):
            def emit():
                ts1 = p.op("tensor", "matmul", ps_st[0:1, :], ctx.ones[:, 0:1], sqa,
                           start=True, stop=False, sem=ctx.pe,
                           after=(tsq1,) + ((ctx.psg,) if ctx.psg else ()))
                sqslot[1] = ts1
                ts2 = p.op("tensor", "matmul", ps_st[0:1, :], ctx.ones[0:64, 0:1], sqb,
                           start=False, stop=True, sem=ctx.pe, after=(tsq2,))
                s64[1] = ts2
                tse = p.op("vector", "tensor_copy", str2[0:1, :], ps_st[0:1, :], sem=ctx.dv,
                           after=(ts2,) + ((ctx.str2g,) if ctx.str2g else ()))
                ctx.psg = tse
                ctx.str2g = p.op("sync", "dma_start", sem=ctx.st, n=16,
                                 out=o_n2[h:h + 1, t * 512:(t + 1) * 512],
                                 in_=str2[0:1, :], after=(tse,))
            return emit
        pending[0] = mk_stats()
    pending[0]()


def build_l0pre(mega=None):
    ctx = mega["ctx"] if mega else Ctx("l0pre")
    ctx.al.begin_stage("a")
    ctx.psg = None
    ctx.str2g = None
    ctx.strowg = {0: None, 1: None}
    p, al = ctx.p, ctx.al
    x = ctx.din("x_sl", [T, D])
    w_cq_d = ctx.din("w_cq", [D, 1536])
    w_dqn_d = ctx.din("w_dq_nope", [1536, 2048])
    w_dqr_d = ctx.din("w_dq_rope_p", [1536, 1024])
    w_ckv_d = ctx.din("w_ckv", [D, 512])
    w_dkn_d = ctx.din("w_dk_nope", [512, 2048])
    w_dv_d = ctx.din("w_dv", [512, 2048])
    w_krope_d = ctx.din("w_k_rope_p", [D, 64])
    qg_in = ctx.din("qg", [128, 12])
    kvg_in = ctx.din("kvg", [128, 4])
    crow_in = ctx.din("crow", [64, 16])
    srow_in = ctx.din("srow", [64, 16])
    ctx.din("ident", [128, 128], BF16)
    i64_in = ctx.din("i64", [64, 64], BF16)
    mswap_in = ctx.din("mswap", [64, 64], BF16)

    if mega:
        io = mega["io"]
        o_qn, o_qr, o_kn, o_kr = io["Qn0"], io["Qr0"], io["Kn0"], io["Kr0"]
        o_v, o_qn2, o_kn2 = io["V0"], io["qn20"], io["kn20"]
    else:
        o_qn = ctx.dout("Qn", [2048, T], BF16)
        o_qr = ctx.dout("Qr", [1024, T], BF16)
        o_kn = ctx.dout("Kn", [2048, T], BF16)
        o_kr = ctx.dout("Kr", [1024, T], BF16)
        o_v = ctx.dout("V", [T, 2048], BF16)
        o_qn2 = ctx.dout("qn2", [16, T])
        o_kn2 = ctx.dout("kn2", [16, T])

    if not hasattr(ctx, "ident"):
        ctx.common()
    WA, WB = 0, 24576
    R_CKV, R_KR, R_DKN, R_DV = 0, 8192, 9216, 24576
    wsb = al.sb("wsb", [128, WB + 12288], BF16)
    lnx = al.sb("lnx", [128, 16 * T], BF16)
    lnx_tm = al.sb("lnx_tm", [128, D], BF16)
    xt = al.sb("xt", [128, D], F32)
    nq = al.sb("nq", [128, 12 * T], BF16)
    nkv = al.sb("nkv", [128, 4 * T], BF16)
    base = al.sb("base", [64, T], BF16)
    rot = al.sb("rot", [128, 16 * 64], BF16)
    big = al.sb("big", [128, 8 * T], BF16)
    ev4 = al.sb("ev4", [128, 4 * 512], BF16)
    er2 = al.sb("er2", [64, 2 * 512], BF16)
    sqs = al.sb("sqs", [128, 2 * 512], BF16)
    sq64s = al.sb("sq64s", [64, 2 * 512], BF16)
    vtm = al.sb("vtm", [128, 2 * 2048], BF16)
    strow = al.sb("strow", [1, 2 * 512], F32)
    str2 = al.sb("str2", [1, 512], F32)
    i64 = al.sb("sb_i64", [64, 64], BF16)
    mswap = al.sb("sb_mswap", [64, 64], BF16)
    crow = al.sb("sb_crow", [64, 16], F32)
    srow = al.sb("sb_srow", [64, 16], F32)
    qg = al.sb("sb_qg", [128, 12], F32)
    kvg = al.sb("sb_kvg", [128, 4], F32)
    scr_sq = al.sb("scr_sq", [128, D], BF16)
    stats = {k: al.sb("st_" + k, [128, 1], F32) for k in ["ssq", "smean", "m", "var", "rstd"]}
    r128 = al.sb("r128", [128, T], F32)
    ps_a = al.ps("ps_a", [128, 512])
    ps_b = al.ps("ps_b", [128, 512])
    ps_tr = al.ps("ps_tr", [128, 2048], BF16)
    ps_st = al.ps("ps_st", [1, 512])
    ps_bc = al.ps("ps_bc", [128, 512])
    ps_r = al.ps("ps_r", [64, 1024])
    pool = SlotPool([ps_a, ps_b])
    epool = SlotPool([ev4[:, i * 512:(i + 1) * 512] for i in range(4)])
    erpool = SlotPool([er2[:, 0:512], er2[:, 512:1024]])
    trpool = SlotPool([ps_tr[:, 0:512], ps_tr[:, 1024:1536]])
    rpool = SlotPool([ps_r[:, 0:512], ps_r[:, 512:1024]])
    sqpool = SlotPool([sqs[:, 0:512], sqs[:, 512:1024]])
    sq64p = SlotPool([sq64s[:, 0:512], sq64s[:, 512:1024]])
    bcslot = SlotPool([ps_bc])

    for dst, src in ((i64, i64_in), (mswap, mswap_in), (crow, crow_in), (srow, srow_in),
                     (qg, qg_in), (kvg, kvg_in)):
        p.op("sync", "dma_start", sem=ctx.ls, n=16, out=dst[:, :], in_=src[:, :])
    p.wait("vector", ctx.ls)
    tsrow = p.tok(ctx.dv)
    rot_tok = None
    for h in range(16):
        t1 = p.op("vector", "tensor_scalar_mul", rot[0:64, h * 64:(h + 1) * 64], i64[:, :],
                  crow[:, h:h + 1], sem=ctx.dv, after=(tsrow,))
        rot_tok = p.op("vector", "scalar_tensor_tensor", rot[0:64, h * 64:(h + 1) * 64],
                       mswap[:, :], srow[:, h:h + 1], rot[0:64, h * 64:(h + 1) * 64],
                       ALU.mult, ALU.add, sem=ctx.dv, after=(t1,))
    rot_tok = p.op("sync", "dma_start", sem=ctx.ls, n=16, out=rot[64:128, :],
                   in_=rot[0:64, :], after=(rot_tok,))

    tw_ckv = load_weight(ctx, wsb, R_CKV, w_ckv_d, D, 512)
    tw_kr = load_weight(ctx, wsb, R_KR, w_krope_d, D, 64)
    tw_dkn = load_weight(ctx, wsb, R_DKN, w_dkn_d, 512, 2048)
    tw_dv = load_weight(ctx, wsb, R_DV, w_dv_d, 512, 2048)
    xt_war = [None, None]
    tm_war = [None, None]
    lnx_ready = None
    for a in range(8):
        sl = 0
        tx = p.op("sync", "dma_start", sem=ctx.ls, n=16, out=xt[:, :],
                  in_=x[a * 128:(a + 1) * 128, :],
                  after=((xt_war[sl],) if xt_war[sl] is not None else ()))
        tn = emit_ln(ctx, xt[:, :], lnx_tm[:, :],
                     scr_sq, stats, (tx,) + ((tm_war[sl],) if tm_war[sl] is not None else ()))
        xt_war[sl] = tn
        p.wait_tok("tensor", tn)
        for g4 in range(4):
            srcs = [lnx_tm[:, (g4 * 4 + j) * 128:(g4 * 4 + j) * 128 + 128]
                    for j in range(4)]
            dst = lnx[:, :].rearrange("p (k t) -> p k t", k=16)[:, g4 * 4:(g4 + 1) * 4,
                                                               a * 128:(a + 1) * 128]
            lnx_ready = transpose_group(ctx, srcs, dst, trpool.take())
        tm_war[sl] = p.tok(ctx.pe)

    def lnx_rhs(k, t):
        ctx.p.wait_tok("tensor", lnx_ready)
        return lnx[:, k * T + t * 512:k * T + (t + 1) * 512]

    def rms_proj(w_col, fan_out, nkk, rhs_fn, dst, dst_nk, gain, wtok, fan_in_true):
        mods = []
        for t in range(2):
            stat_tok = None
            pending = None
            for m in range(dst_nk):
                slot = pool.take()
                ps, war = slot
                p.wait_tok("tensor", wtok)
                if war is not None:
                    p.wait_tok("tensor", war)
                mm = None
                for k in range(nkk):
                    mm = p.op("tensor", "matmul", ps[:, :], wslice(wsb, w_col, fan_out, k, m),
                              rhs_fn(k, t), start=(k == 0), stop=(k == nkk - 1),
                              sem=(ctx.pe if k == nkk - 1 else None))
                if pending is not None:
                    stat_tok = pending()
                tev = p.op("vector", "tensor_copy", dst[:, m * T + t * 512:m * T + (t + 1) * 512],
                           ps[:, :], sem=ctx.dv, after=(mm,))
                sqslot = sqpool.take()
                sqa, sqwar = sqslot
                tsq = p.op("scalar", "activation", sqa, ps[:, :], AF.Square, sem=ctx.sc,
                           after=(mm, tev) + ((sqwar,) if sqwar is not None else ()))
                slot[1] = (tev, tsq)

                def mk_stat(m=m, tsq=tsq, sqa=sqa, sqslot=sqslot):
                    def emit():
                        tk = p.op("tensor", "matmul", ps_st[0:1, :], ctx.ones[:, 0:1], sqa,
                                  start=(m == 0), stop=(m == dst_nk - 1), sem=ctx.pe,
                                  after=(tsq,) + ((ctx.psg,) if (m == 0 and ctx.psg) else ()))
                        sqslot[1] = tk
                        return tk
                    return emit
                pending = mk_stat()
            stat_tok = pending()
            tr = rmsnorm_post2(ctx, ps_st[0:1, :], t, strow, r128, bcslot.take(),
                               fan_in_true, 1e-5, stat_tok)
            for m in range(dst_nk):
                tg1 = p.op("vector", "tensor_scalar_mul",
                           dst[:, m * T + t * 512:m * T + (t + 1) * 512],
                           dst[:, m * T + t * 512:m * T + (t + 1) * 512],
                           gain[:, m:m + 1], sem=ctx.dv, after=(tr,))
                tg2 = p.op("vector", "tensor_mul",
                           dst[:, m * T + t * 512:m * T + (t + 1) * 512],
                           dst[:, m * T + t * 512:m * T + (t + 1) * 512],
                           r128[:, t * 512:(t + 1) * 512], sem=ctx.dv, after=(tg1,))
                mods.append(tg2)
        return mods

    nkv_mods = rms_proj(R_CKV, 512, 16, lnx_rhs, nkv, 4, kvg, tw_ckv, 512)

    def nkv_rhs(k, t):
        for tk in nkv_mods:
            ctx.p.wait_tok("tensor", tk)
        return nkv[:, k * T + t * 512:k * T + (t + 1) * 512]

    base_tok = None
    for t in range(2):
        slot = pool.take()
        ps, war = slot
        p.wait_tok("tensor", tw_kr)
        if war is not None:
            p.wait_tok("tensor", war)
        mm = None
        for k in range(16):
            mm = p.op("tensor", "matmul", ps[0:64, :], wslice(wsb, R_KR, 64, k, 0, 64),
                      lnx_rhs(k, t), start=(k == 0), stop=(k == 15),
                      sem=(ctx.pe if k == 15 else None))
        base_tok = p.op("scalar", "activation", base[:, t * 512:(t + 1) * 512], ps[0:64, :],
                        AF.Identity, sem=ctx.sc, after=(mm,))
        slot[1] = base_tok


    def k_rope_rhs(h, t):
        return base[:, t * 512:(t + 1) * 512]

    for t in range(2):
        p.wait_tok("tensor", tw_dkn)
        emit_qk_heads(ctx, t, wsb, R_DKN, 4, nkv_rhs, k_rope_rhs,
                      lambda h: rot[0:64, h * 64:(h + 1) * 64],
                      o_kn, o_kr, o_kn2,
                      pool, epool, erpool, rpool, sqpool, sq64p, ps_st, str2,
                      rot_tok, base_tok)
    k_done = p.tok(ctx.pe)
    tw_cq = load_weight(ctx, wsb, WA, w_cq_d, D, 1536, after=(k_done,))

    big_war = None
    vtm_war2 = [None, None]
    for t in range(2):
        vfm_tok = None
        for m in range(16):
            slot = pool.take()
            ps, war = slot
            p.wait_tok("tensor", tw_dv)
            if war is not None:
                p.wait_tok("tensor", war)
            mm = None
            for k in range(4):
                mm = p.op("tensor", "matmul", ps[:, :], wslice(wsb, R_DV, 2048, k, m),
                          nkv_rhs(k, t), start=(k == 0), stop=(k == 3),
                          sem=(ctx.pe if k == 3 else None))
            vfm_tok = p.op("scalar", "activation", big[:, m * 512:m * 512 + 512],
                           ps[:, :], AF.Identity, sem=ctx.sc, after=(mm, big_war))
            slot[1] = vfm_tok
        p.wait_tok("tensor", vfm_tok)
        for a in range(4):
            sl = a % 2
            if vtm_war2[sl] is not None:
                p.wait_tok("vector", vtm_war2[sl])
            for g4 in range(4):
                srcs = [big[:, (g4 * 4 + j) * 512 + a * 128:(g4 * 4 + j) * 512 + a * 128 + 128]
                        for j in range(4)]
                dst = vtm[:, sl * 2048 + g4 * 512:sl * 2048 + (g4 + 1) * 512]
                tg = transpose_group(ctx, srcs, dst, trpool.take())
            vtm_war2[sl] = p.op("sync", "dma_start", sem=ctx.st, n=16,
                                out=o_v[t * 512 + a * 128:t * 512 + (a + 1) * 128, :],
                                in_=vtm[:, sl * 2048:(sl + 1) * 2048], after=(tg,))
        big_war = p.tok(ctx.pe)
    v_done = p.tok(ctx.pe)
    if mega:
        p.wait("gpsimd", ctx.st)
        p.op("gpsimd", "collective_compute", "AllReduce", ALU.add,
             replica_groups=[list(range(NCORES))],
             ins=[mega["io"]["bar1"].opt()], outs=[mega["io"]["bar1o"].opt()],
             sem=ctx.cc, n=1)
        p.wait("gpsimd", ctx.cc)
        for src, dst in (("Kn0", "KnAG"), ("Kr0", "KrAG"), ("V0", "VAG"), ("kn20", "kn2AG")):
            p.op("gpsimd", "collective_compute", "AllGather", ALU.bypass,
                 replica_groups=[list(range(NCORES))],
                 ins=[mega["io"][src].opt()], outs=[mega["io"][dst].opt()],
                 sem=ctx.cc, n=1)
        mega["cc1"] = p.tok(ctx.cc)

    nq_mods = rms_proj(WA, 1536, 16, lnx_rhs, nq, 12, qg, tw_cq, 1536)
    nq_pe_done = p.tok(ctx.pe)

    def nq_rhs(k, t):
        for tk in nq_mods:
            ctx.p.wait_tok("tensor", tk)
        return nq[:, k * T + t * 512:k * T + (t + 1) * 512]

    tw_qr = load_weight(ctx, wsb, WB, w_dqr_d, 1536, 1024, after=(v_done,))
    for t in range(2):
        for m in range(8):
            slot = pool.take()
            ps, war = slot
            p.wait_tok("tensor", tw_qr)
            if war is not None:
                p.wait_tok("tensor", war)
            mm = None
            for k in range(12):
                mm = p.op("tensor", "matmul", ps[:, :], wslice(wsb, WB, 1024, k, m),
                          nq_rhs(k, t), start=(k == 0), stop=(k == 11),
                          sem=(ctx.pe if k == 11 else None))
            slot[1] = p.op("scalar", "activation",
                           big[:, m * T + t * 512:m * T + (t + 1) * 512],
                           ps[:, :], AF.Identity, sem=ctx.sc, after=(mm, v_done))
    prerot_done = p.tok(ctx.sc)
    prerot_pe = p.tok(ctx.pe)

    tw_qn = load_weight(ctx, wsb, WA, w_dqn_d, 1536, 2048, after=(nq_pe_done,))

    def q_rope_rhs(h, t):
        c, r0 = h // 2, (h % 2) * 64
        return big[r0:r0 + 64, c * T + t * 512:c * T + (t + 1) * 512]

    for t in range(2):
        p.wait_tok("tensor", tw_qn)
        emit_qk_heads(ctx, t, wsb, WA, 12, nq_rhs, q_rope_rhs,
                      lambda h: rot[(h % 2) * 64:(h % 2) * 64 + 64, h * 64:(h + 1) * 64],
                      o_qn, o_qr, o_qn2,
                      pool, epool, erpool, rpool, sqpool, sq64p, ps_st, str2,
                      rot_tok, prerot_done)
    q_done = p.tok(ctx.pe)

    if mega:
        ctx.barrier()
        ctx.al.end_stage()
    else:
        ctx.al.end_stage()
    return ctx



def run_l0pre(g):
    ctx = build_l0pre()
    nc = ctx.finish()
    shared = {k: g[k] for k in ["w_cq", "w_dq_nope", "w_dq_rope_p", "w_ckv", "w_dk_nope",
                                "w_dv", "w_k_rope_p", "qg", "kvg", "crow", "srow",
                                "ident", "i64", "mswap"]}
    in_maps = []
    for i in range(NCORES):
        m = dict(shared)
        m["x_sl"] = g["x"][i * T:(i + 1) * T]
        in_maps.append(m)
    res = run_bass_kernel_spmd(nc, in_maps, core_ids=list(range(NCORES)))
    return res.results


ATT_SCALE0 = 1.0 / float(np.sqrt(192.0))
VSCALE0 = 1.0 / float(np.sqrt(2048.0))


def build_l0attn(mega=None):
    ctx = mega["ctx"] if mega else Ctx("l0attn")
    ctx.al.begin_stage("b")
    p, al = ctx.p, ctx.al
    x = ctx.din("x_sl", [T, D])
    if mega:
        io = mega["io"]
        qn_d, qr_d, qn2_d = io["Qn0"], io["Qr0"], io["qn20"]
        kn0, kr0, v0 = io["Kn0"], io["Kr0"], io["V0"]
        knag, krag, vag, kn2ag = io["KnAG"], io["KrAG"], io["VAG"], io["kn2AG"]
        o_r1 = io["r1"]
        mscr = io["mscrB"]
    else:
        qn_d = ctx.din("Qn", [2048, T], BF16)
        qr_d = ctx.din("Qr", [1024, T], BF16)
        knm_d = ctx.din("KnM", [2048, 7680], BF16)
        krm_d = ctx.din("KrM", [1024, 7680], BF16)
        vm_d = ctx.din("VM", [7680, 2048], BF16)
        knd_d = ctx.din("KnD", [2, 2048, 512], BF16)
        krd_d = ctx.din("KrD", [2, 1024, 512], BF16)
        vd_d = ctx.din("VD", [2, 512, 2048], BF16)
        qn2_d = ctx.din("qn2", [16, T])
        kn2_d = ctx.din("kn2f", [16, SEQ])
        o_r1 = ctx.dout("resid1", [T, D])
        mscr = ctx.nc.dram_tensor("mscr", [16, 2], F32, kind="Internal").ap()
    km_d = ctx.din("keymask", [128, 2 * 15])
    maskc_d = ctx.din("maskC", [128, 2048])
    wproj_d = ctx.din("w_proj", [D, D])
    ctx.din("ident", [128, 128], BF16)

    if not hasattr(ctx, "ident"):
        ctx.common()
    wsb = al.sb("wsb", [128, 16 * 2048], BF16)
    qn_sb = al.sb("qn_sb", [128, 16 * T], BF16)
    qr_sb = al.sb("qr_sb", [64, 16 * T], BF16)
    pT = al.sb("pT", [128, 2 * 2048], BF16)
    attn = al.sb("attn", [128, 16 * 512], BF16)
    maskc = al.sb("sb_maskc", [128, 2048], F32)
    keymask = al.sb("sb_keymask", [128, 2 * 15], F32)
    biasC = al.sb("biasC", [128, 32 * 15], F32)
    smask = al.sb("smask", [128, 2048], F32)
    kn_t = al.sb("kn_t", [128, 3 * 512], BF16)
    kr_t = al.sb("kr_t", [64, 3 * 512], BF16)
    v_t = al.sb("v_t", [128, 3 * 512], BF16)
    lsum = al.sb("lsum", [128, 2 * 512], BF16)
    lsum2 = al.sb("lsum2", [128, 512], BF16)
    linv = al.sb("linv", [128, 512], F32)
    rl = al.sb("rl", [1, 512], F32)
    xrt = al.sb("xrt", [128, 2 * 512], F32)
    rout = al.sb("rout", [128, 2 * 512], F32)
    qn2 = al.sb("sb_qn2", [16, T], F32)
    k2r = al.sb("k2r", [16, T], F32)
    kmax2 = al.sb("kmax2", [16, 1], F32)
    kred = al.sb("kred", [16, 1], F32)
    qmax2 = al.sb("qmax2", [16, 2], F32)
    negM = al.sb("negM", [16, 2], F32)
    mrow = al.sb("mrow", [1, 32], F32)
    Mb = al.sb("Mb", [128, 32], F32)
    ps_sa = al.ps("ps_sa", [128, 1024])
    ps_sb = al.ps("ps_sb", [128, 1024])
    ps_pv = al.ps("ps_pv", [128, 512])
    ps_l = al.ps("ps_l", [1, 512])
    ps_bc = al.ps("ps_bc", [128, 512])
    ps_o = al.ps("ps_o", [128, 512])

    tw = p.op("gpsimd", "dma_start", sem=ctx.ld, n=16,
              out=wsb[:, :].rearrange("p (k m) -> p k m", k=16),
              in_=wproj_d.rearrange("(k p) m -> p k m", p=128))
    p.op("sync", "dma_start", sem=ctx.ls, n=16,
         out=qn_sb[:, :].rearrange("p (h t) -> p h t", h=16),
         in_=qn_d.rearrange("(h p) t -> p h t", p=128))
    p.op("sync", "dma_start", sem=ctx.ls, n=16,
         out=qr_sb[:, :].rearrange("p (h t) -> p h t", h=16),
         in_=qr_d.rearrange("(h p) t -> p h t", p=64))
    p.op("sync", "dma_start", sem=ctx.ls, n=16, out=maskc[:, :], in_=maskc_d[:, :])
    tkm = p.op("sync", "dma_start", sem=ctx.ls, n=16, out=keymask[:, :], in_=km_d[:, :])
    tq3 = p.op("sync", "dma_start", sem=ctx.ls, n=16, out=qn2[:, :], in_=qn2_d[:, :])
    q_ready = p.tok(ctx.ls)

    km_tok = None
    for c in range(8):
        src = kn2ag[c * 16:(c + 1) * 16, :] if mega else kn2_d[:, c * 1024:(c + 1) * 1024]
        tl = p.op("sync", "dma_start", sem=ctx.ls, n=16, out=k2r[:, :],
                  in_=src, after=((km_tok,) if km_tok else ()) +
                  ((mega["cc1"],) if (mega and "cc1" in mega) else ()))
        trd = p.op("vector", "reduce_max", kred[:, :], k2r[:, :], axis=AX.X,
                   sem=ctx.dv, after=(tl,))
        if c == 0:
            km_tok = p.op("vector", "tensor_copy", kmax2[:, :], kred[:, :], sem=ctx.dv,
                          after=(trd,))
        else:
            km_tok = p.op("vector", "tensor_max", kmax2[:, :], kmax2[:, :], kred[:, :],
                          sem=ctx.dv, after=(trd,))
    tq4 = None
    for q in range(2):
        tq4 = p.op("vector", "reduce_max", qmax2[:, q:q + 1], qn2[:, q * 512:(q + 1) * 512],
                   axis=AX.X, sem=ctx.dv, after=(tq3, km_tok))
    t5 = p.op("vector", "tensor_scalar_mul", negM[:, :], qmax2[:, :], kmax2[:, 0:1],
              sem=ctx.dv, after=(tq4,))
    t6 = p.op("scalar", "activation", negM[:, :], negM[:, :], AF.Sqrt, sem=ctx.sc, after=(t5,))
    t7 = p.op("vector", "tensor_scalar_mul", negM[:, :], negM[:, :], -ATT_SCALE0,
              sem=ctx.dv, after=(t6,))
    t8 = p.op("sync", "dma_start", sem=ctx.st, n=16, out=mscr[:, :], in_=negM[:, :], after=(t7,))
    t9 = p.op("sync", "dma_start", sem=ctx.ls, n=16, out=mrow[:, :],
              in_=mscr.rearrange("a b -> (a b)")[None, :], after=(t8,))
    t10 = p.op("tensor", "matmul", ps_bc[:, 0:32], ctx.ones32[0:1, 0:128], mrow[0:1, :],
               start=True, stop=True, sem=ctx.pe, after=(t9,))
    t11 = p.op("vector", "tensor_copy", Mb[:, :], ps_bc[:, 0:32], sem=ctx.dv, after=(t10,))
    bias_tok = None
    for h in range(16):
        for q in range(2):
            idx = h * 2 + q
            bias_tok = p.op("vector", "tensor_scalar_add",
                            biasC[:, idx * 15:(idx + 1) * 15],
                            keymask[:, q * 15:(q + 1) * 15], Mb[:, idx:idx + 1],
                            sem=ctx.dv, after=(t11, tkm))

    NSLOT = 16
    st_kbuf = {"war": [None, None, None], "exp_war": [None, None],
               "pT_war": [None, None], "smask_war": None, "psg": t11,
               "bc_war": t11, "attn_war": None}

    def emit_slot_loads(h, q, j):
        b = j % 3
        extra = ()
        if mega:
            if j < 15:
                extra = ((mega["cc1"],) if "cc1" in mega else ())
                r, c0 = j // 2, (j % 2) * 512
                kn_src = knag[r * 2048 + h * 128:r * 2048 + (h + 1) * 128, c0:c0 + 512]
                kr_src = krag[r * 1024 + h * 64:r * 1024 + (h + 1) * 64, c0:c0 + 512]
                v_src = vag[j * 512:(j + 1) * 512, h * 128:(h + 1) * 128]
            else:
                kn_src = kn0[h * 128:(h + 1) * 128, q * 512:(q + 1) * 512]
                kr_src = kr0[h * 64:(h + 1) * 64, q * 512:(q + 1) * 512]
                v_src = v0[q * 512:(q + 1) * 512, h * 128:(h + 1) * 128]
        elif j < 15:
            kn_src = knm_d[h * 128:(h + 1) * 128, j * 512:(j + 1) * 512]
            kr_src = krm_d[h * 64:(h + 1) * 64, j * 512:(j + 1) * 512]
            v_src = vm_d[j * 512:(j + 1) * 512, h * 128:(h + 1) * 128]
        else:
            kn_src = knd_d[q, h * 128:(h + 1) * 128, :]
            kr_src = krd_d[q, h * 64:(h + 1) * 64, :]
            v_src = vd_d[q, :, h * 128:(h + 1) * 128]
        war = ((st_kbuf["war"][b],) if st_kbuf["war"][b] else ()) + extra
        t1 = p.op("sync", "dma_start", sem=ctx.ls, n=16,
                  out=kn_t[:, b * 512:(b + 1) * 512], in_=kn_src, after=war)
        t2 = p.op("sync", "dma_start", sem=ctx.ls, n=16,
                  out=kr_t[:, b * 512:(b + 1) * 512], in_=kr_src, after=war)
        t3 = p.op("gpsimd", "dma_start", sem=ctx.ld, n=16,
                  out=v_t[:, b * 512:(b + 1) * 512].rearrange("p (c d) -> p c d", c=4),
                  in_=v_src.rearrange("(c p) d -> p c d", p=128), after=war)
        return (t1, t2, t3)

    def flash_head(h, q):
        hq = h * 2 + q
        qn_rhs = qn_sb[:, h * T + q * 512:h * T + (q + 1) * 512]
        qr_rhs = qr_sb[:, h * T + q * 512:h * T + (q + 1) * 512]
        prev = None
        ld_q = {0: emit_slot_loads(h, q, 0), 1: emit_slot_loads(h, q, 1),
                2: emit_slot_loads(h, q, 2)}
        for j in range(NSLOT + 1):
            if j < NSLOT:
                b = j % 3
                ld_toks = ld_q.pop(j)
                mm_a = mm_b = None
                for c in range(4):
                    sps = ps_sa if c < 2 else ps_sb
                    if c in (0, 2) and st_kbuf["exp_war"][0 if c < 2 else 1] is not None:
                        p.wait_tok("tensor", st_kbuf["exp_war"][0 if c < 2 else 1])
                    if c == 0:
                        for tt in ld_toks:
                            p.wait_tok("tensor", tt)
                        p.wait_tok("tensor", q_ready)
                    p.op("tensor", "matmul", sps[:, (c % 2) * 512:(c % 2 + 1) * 512],
                         kn_t[:, b * 512 + c * 128:b * 512 + (c + 1) * 128],
                         qn_rhs, start=True, stop=False)
                    mm = p.op("tensor", "matmul", sps[:, (c % 2) * 512:(c % 2 + 1) * 512],
                              kr_t[:, b * 512 + c * 128:b * 512 + (c + 1) * 128],
                              qr_rhs, start=False, stop=True,
                              sem=(ctx.pe if c in (1, 3) else None))
                    if c == 1:
                        mm_a = mm
                mm_b = mm
                pb = j % 2
                pw = (st_kbuf["pT_war"][pb],) if st_kbuf["pT_war"][pb] else ()
                bias_ap = (Mb[:, hq:hq + 1] if j == NSLOT - 1
                           else biasC[:, hq * 15 + j:hq * 15 + j + 1])
                if j == NSLOT - 1:
                    smw = (st_kbuf["smask_war"],) if st_kbuf["smask_war"] else ()
                    tm1 = p.op("vector", "tensor_add", smask[:, 0:1024],
                               ps_sa[:, :], maskc[:, 0:1024], sem=ctx.dv,
                               after=(mm_a, bias_tok) + smw)
                    tm2 = p.op("vector", "tensor_add", smask[:, 1024:2048],
                               ps_sb[:, :], maskc[:, 1024:2048], sem=ctx.dv,
                               after=(mm_b,))
                    e_a = p.op("scalar", "activation", pT[:, pb * 2048:pb * 2048 + 1024],
                               smask[:, 0:1024], AF.Exp, bias=bias_ap,
                               scale=ATT_SCALE0, sem=ctx.sc, after=(tm1,) + pw)
                    e_b = p.op("scalar", "activation",
                               pT[:, pb * 2048 + 1024:pb * 2048 + 2048],
                               smask[:, 1024:2048], AF.Exp, bias=bias_ap,
                               scale=ATT_SCALE0, sem=ctx.sc, after=(tm2,))
                    st_kbuf["smask_war"] = e_b
                else:
                    e_a = p.op("scalar", "activation", pT[:, pb * 2048:pb * 2048 + 1024],
                               ps_sa[:, :], AF.Exp, bias=bias_ap, scale=ATT_SCALE0,
                               sem=ctx.sc, after=(mm_a, bias_tok) + pw)
                    e_b = p.op("scalar", "activation",
                               pT[:, pb * 2048 + 1024:pb * 2048 + 2048],
                               ps_sb[:, :], AF.Exp, bias=bias_ap, scale=ATT_SCALE0,
                               sem=ctx.sc, after=(mm_b,))
                st_kbuf["exp_war"][0] = e_a
                st_kbuf["exp_war"][1] = e_b
            if prev is not None:
                jp, pbp, ea_p, eb_p = prev
                c0 = pbp * 2048
                lw = st_kbuf.get("lsum_war_%d" % pbp)
                tl1 = p.op("vector", "tensor_add", lsum[:, pbp * 512:(pbp + 1) * 512],
                           pT[:, c0:c0 + 512], pT[:, c0 + 512:c0 + 1024],
                           sem=ctx.dv, after=(ea_p,) + ((lw,) if lw else ()))
                tl2 = p.op("vector", "tensor_add", lsum2[:, :],
                           pT[:, c0 + 1024:c0 + 1536], pT[:, c0 + 1536:c0 + 2048],
                           sem=ctx.dv, after=(eb_p,))
                tl3 = p.op("vector", "tensor_add", lsum[:, pbp * 512:(pbp + 1) * 512],
                           lsum[:, pbp * 512:(pbp + 1) * 512], lsum2[:, :],
                           sem=ctx.dv, after=(tl2, tl1))
                lmm = p.op("tensor", "matmul", ps_l[0:1, :], ctx.ones[:, 0:1],
                           lsum[:, pbp * 512:(pbp + 1) * 512], start=(jp == 0),
                           stop=(jp == NSLOT - 1), sem=ctx.pe,
                           after=(tl3,) + ((st_kbuf["psg"],) if jp == 0 else ()))
                st_kbuf["lsum_war_%d" % pbp] = lmm
                pv_tok = None
                vb = jp % 3
                for c in range(4):
                    pv_tok = p.op("tensor", "matmul", ps_pv[:, :],
                                  v_t[:, vb * 512 + c * 128:vb * 512 + (c + 1) * 128],
                                  pT[:, c0 + c * 512:c0 + (c + 1) * 512],
                                  start=(jp == 0 and c == 0),
                                  stop=(jp == NSLOT - 1 and c == 3),
                                  after=((ea_p,) if c < 2 else (eb_p,)) +
                                        ((st_kbuf["psg"],) if (jp == 0 and c == 0) else ()),
                                  sem=ctx.pe)
                st_kbuf["war"][vb] = pv_tok
                st_kbuf["pT_war"][pbp] = (pv_tok, tl3)
                if jp + 3 < NSLOT:
                    ld_q[jp + 3] = emit_slot_loads(h, q, jp + 3)
            if j < NSLOT:
                prev = (j, j % 2, e_a, e_b)
        lpv_done = p.tok(ctx.pe)
        trc = p.op("vector", "reciprocal", rl[:, :], ps_l[0:1, :], sem=ctx.dv,
                   after=(lpv_done,))
        trc2 = p.op("vector", "tensor_scalar_mul", rl[:, :], rl[:, :], VSCALE0,
                    sem=ctx.dv, after=(trc,))
        tb = p.op("tensor", "matmul", ps_bc[:, :], ctx.ones32[0:1, 0:128], rl[0:1, :],
                  start=True, stop=True, sem=ctx.pe,
                  after=(trc2,) + ((st_kbuf["bc_war"],) if st_kbuf["bc_war"] else ()))
        tc = p.op("vector", "tensor_copy", linv[:, :], ps_bc[:, :], sem=ctx.dv, after=(tb,))
        st_kbuf["bc_war"] = tc
        aw = (st_kbuf["attn_war"],) if st_kbuf["attn_war"] else ()
        ta = p.op("vector", "tensor_mul", attn[:, h * 512:(h + 1) * 512],
                  ps_pv[:, :], linv[:, :], sem=ctx.dv, after=(tc,) + aw)
        st_kbuf["psg"] = ta
        return ta

    for q in range(2):
        attn_tok = None
        for h in range(16):
            attn_tok = flash_head(h, q)
        po_war = None
        for mq in range(4):
            for n in range(4):
                mm = None
                p.wait_tok("tensor", tw)
                p.wait_tok("tensor", attn_tok)
                if po_war is not None:
                    p.wait_tok("tensor", po_war)
                for k in range(16):
                    mm = p.op("tensor", "matmul", ps_o[:, :],
                              attn[:, k * 512 + mq * 128:k * 512 + (mq + 1) * 128],
                              wsb[:, k * 2048 + n * 512:k * 2048 + (n + 1) * 512],
                              start=(k == 0), stop=(k == 15),
                              sem=(ctx.pe if k == 15 else None))
                sl = (mq * 4 + n) % 2
                xa = xrt[:, sl * 512:(sl + 1) * 512]
                ra = rout[:, sl * 512:(sl + 1) * 512]
                war = st_kbuf.get("oslot_war_%d" % sl)
                tx = p.op("sync", "dma_start", sem=ctx.ls, n=16, out=xa,
                          in_=x[q * 512 + mq * 128:q * 512 + (mq + 1) * 128,
                                n * 512:(n + 1) * 512],
                          after=((war,) if war else ()))
                tadd = p.op("vector", "tensor_add", ra, ps_o[:, :], xa, sem=ctx.dv,
                            after=(mm, tx) + ((war,) if war else ()))
                po_war = tadd
                st_kbuf["oslot_war_%d" % sl] = p.op(
                    "sync", "dma_start", sem=ctx.st, n=16,
                    out=o_r1[q * 512 + mq * 128:q * 512 + (mq + 1) * 128,
                             n * 512:(n + 1) * 512],
                    in_=ra, after=(tadd,))
        st_kbuf["attn_war"] = p.tok(ctx.pe)
    if mega:
        ctx.barrier()
    ctx.al.end_stage()
    return ctx


def run_l0attn(g, a_res):
    ctx = build_l0attn()
    nc = ctx.finish()
    Kn_all = np.concatenate([a_res[i]["Kn"] for i in range(NCORES)], axis=1)
    Kr_all = np.concatenate([a_res[i]["Kr"] for i in range(NCORES)], axis=1)
    V_all = np.concatenate([a_res[i]["V"] for i in range(NCORES)], axis=0)
    kn2f = np.concatenate([a_res[i]["kn2"] for i in range(NCORES)], axis=1)
    KnM = np.ascontiguousarray(Kn_all[:, :7680])
    KrM = np.ascontiguousarray(Kr_all[:, :7680])
    VM = np.ascontiguousarray(V_all[:7680, :])
    in_maps = []
    for i in range(NCORES):
        m = {"KnM": KnM, "KrM": KrM, "VM": VM, "kn2f": kn2f,
             "maskC": g["maskC"], "w_proj": g["w_proj"], "ident": g["ident"]}
        m["x_sl"] = g["x"][i * T:(i + 1) * T]
        m["Qn"] = a_res[i]["Qn"]
        m["Qr"] = a_res[i]["Qr"]
        m["qn2"] = a_res[i]["qn2"]
        knd = np.stack([Kn_all[:, (2 * i + q) * 512:(2 * i + q + 1) * 512] for q in range(2)])
        krd = np.stack([Kr_all[:, (2 * i + q) * 512:(2 * i + q + 1) * 512] for q in range(2)])
        vd = np.stack([V_all[(2 * i + q) * 512:(2 * i + q + 1) * 512, :] for q in range(2)])
        m["KnD"], m["KrD"], m["VD"] = (np.ascontiguousarray(knd),
                                       np.ascontiguousarray(krd),
                                       np.ascontiguousarray(vd))
        km = np.zeros((128, 2 * 15), np.float32)
        for q in range(2):
            dq = 2 * i + q
            km[:, q * 15 + dq:(q + 1) * 15] = NEG
        m["keymask"] = km
        in_maps.append(m)
    res = run_bass_kernel_spmd(nc, in_maps, core_ids=list(range(NCORES)))
    return res.results


def build_l0ffn(mega=None):
    ctx = mega["ctx"] if mega else Ctx("l0ffn")
    ctx.al.begin_stage("c")
    ctx.psg = None
    ctx.strowg = {0: None, 1: None}
    p, al = ctx.p, ctx.al
    r1 = mega["io"]["r1"] if mega else ctx.din("resid1", [T, D])
    w0_d = ctx.din("W0T", [D, 8192])
    b0_d = ctx.din("b0col", [128, 64])
    w1_d = ctx.din("W1T", [8192, D])
    b1_d = ctx.din("b1row", [1, D])
    ctx.din("ident", [128, 128], BF16)
    o_r2 = mega["io"]["r2"] if mega else ctx.dout("resid2", [T, D])

    if not hasattr(ctx, "ident"):
        ctx.common()
    W0C, W1C = 0, 16384
    wsb = al.sb("wsb", [128, 32768], BF16)
    lnx = al.sb("lnx", [128, 16 * T], BF16)
    lnx_tm = al.sb("lnx_tm", [128, D], BF16)
    xt = al.sb("xt", [128, D], F32)
    hid = al.sb("hid", [128, 16 * 512], BF16)
    eacc = al.sb("eacc", [128, 8 * 2048], F32)
    b0 = al.sb("sb_b0", [128, 64], F32)
    b1 = al.sb("sb_b1", [1, D], BF16)
    scr_sq = al.sb("scr_sq", [128, D], BF16)
    stats = {k: al.sb("st_" + k, [128, 1], F32) for k in ["ssq", "smean", "m", "var", "rstd"]}
    rt = al.sb("rt", [128, 2 * 512], F32)
    ro = al.sb("ro", [128, 2 * 512], F32)
    ps_a = al.ps("ps_a", [128, 512])
    ps_b = al.ps("ps_b", [128, 512])
    ps_tr = al.ps("ps_tr", [128, 2048], BF16)
    ps_e = al.ps("ps_e", [128, 512])
    ps_e2 = al.ps("ps_e2", [128, 512])
    pool = SlotPool([ps_a, ps_b])
    epool = SlotPool([ps_e, ps_e2])
    trpool = SlotPool([ps_tr[:, 0:512], ps_tr[:, 1024:1536]])

    p.op("sync", "dma_start", sem=ctx.ls, n=16, out=b0[:, :], in_=b0_d[:, :])
    tb1 = p.op("gpsimd", "dma_start", sem=ctx.ld, n=16, out=b1[:, :], in_=b1_d[:, :])

    xt_war = [None]
    tm_war = [None]
    lnx_ready = None
    for a in range(8):
        tx = p.op("sync", "dma_start", sem=ctx.ls, n=16, out=xt[:, :],
                  in_=r1[a * 128:(a + 1) * 128, :],
                  after=((xt_war[0],) if xt_war[0] is not None else ()))
        tn = emit_ln(ctx, xt[:, :], lnx_tm[:, :], scr_sq, stats,
                     (tx,) + ((tm_war[0],) if tm_war[0] is not None else ()))
        xt_war[0] = tn
        p.wait_tok("tensor", tn)
        for g4 in range(4):
            srcs = [lnx_tm[:, (g4 * 4 + j) * 128:(g4 * 4 + j) * 128 + 128]
                    for j in range(4)]
            dst = lnx[:, :].rearrange("p (k t) -> p k t", k=16)[:, g4 * 4:(g4 + 1) * 4,
                                                               a * 128:(a + 1) * 128]
            lnx_ready = transpose_group(ctx, srcs, dst, trpool.take())
        tm_war[0] = p.tok(ctx.pe)

    eacc_tok = {}
    w_war = None
    hid_war = None
    pool4 = SlotPool([ps_a, ps_b, ps_e, ps_e2])
    for hg in range(8):
        tw0 = p.op("gpsimd", "dma_start", sem=ctx.ld, n=16,
                   out=wsb[:, W0C:W0C + 16384].rearrange("p (k m) -> p k m", k=16),
                   in_=w0_d[:, hg * 1024:(hg + 1) * 1024].rearrange("(k p) m -> p k m", p=128),
                   after=((w_war,) if w_war else ()))
        tw1 = p.op("gpsimd", "dma_start", sem=ctx.ld, n=16,
                   out=wsb[:, W1C:W1C + 16384].rearrange("p (k m) -> p k m", k=8),
                   in_=w1_d[hg * 1024:(hg + 1) * 1024, :].rearrange("(k p) m -> p k m", p=128),
                   after=((w_war,) if w_war else ()))
        hid_toks = []
        for c in range(8):
            gm = hg * 8 + c
            s0 = pool4.take()
            s1 = pool4.take()
            p.wait_tok("tensor", tw0)
            p.wait_tok("tensor", lnx_ready)
            for sl_ in (s0, s1):
                if sl_[1] is not None:
                    p.wait_tok("tensor", sl_[1])
            mm0 = mm1 = None
            for k in range(16):
                w_ap = wslice(wsb, W0C, 1024, k, c)
                mm0 = p.op("tensor", "matmul", s0[0][:, :], w_ap,
                           lnx[:, k * T:k * T + 512],
                           start=(k == 0), stop=(k == 15),
                           sem=(ctx.pe if k == 15 else None))
                mm1 = p.op("tensor", "matmul", s1[0][:, :], w_ap,
                           lnx[:, k * T + 512:k * T + 1024],
                           start=(k == 0), stop=(k == 15),
                           sem=(ctx.pe if k == 15 else None))
            for t, sl_, mm in ((0, s0, mm0), (1, s1, mm1)):
                tg = p.op("scalar", "activation",
                          hid[:, (c * 2 + t) * 512:(c * 2 + t + 1) * 512], sl_[0][:, :],
                          AF.Gelu, bias=b0[:, gm:gm + 1], sem=ctx.sc,
                          after=(mm,) + ((hid_war,) if hid_war else ()))
                hid_toks.append(tg)
                sl_[1] = tg
        for tk in hid_toks:
            p.wait_tok("tensor", tk)
        p.wait_tok("tensor", tw1)
        last_emm = None
        for t in range(2):
            for mq in range(4):
                for n in range(4):
                    ekey = (t, mq, n)
                    if ekey in eacc_tok:
                        p.wait_tok("tensor", eacc_tok[ekey])
                    eslot = pool4.take()
                    pse, ewar = eslot
                    if ewar is not None:
                        p.wait_tok("tensor", ewar)
                    mm = None
                    for k in range(8):
                        mm = p.op("tensor", "matmul", pse[:, :],
                                  hid[:, (k * 2 + t) * 512 + mq * 128:(k * 2 + t) * 512 + (mq + 1) * 128],
                                  wsb[:, W1C + k * 2048 + n * 512:W1C + k * 2048 + (n + 1) * 512],
                                  start=(k == 0), stop=(k == 7 and hg > 0),
                                  sem=(ctx.pe if (k == 7 and hg > 0) else None))
                    if hg == 0:
                        mm = p.op("tensor", "matmul", pse[:, :], ctx.ones[0:1, 0:128],
                                  b1[0:1, n * 512:(n + 1) * 512], start=False, stop=True,
                                  sem=ctx.pe, after=(tb1,))
                    last_emm = mm
                    dst = eacc[:, (t * 4 + mq) * 2048 + n * 512:(t * 4 + mq) * 2048 + (n + 1) * 512]
                    if hg == 0:
                        tadd = p.op("vector", "tensor_copy", dst, pse[:, :], sem=ctx.dv,
                                    after=(mm,))
                    else:
                        tadd = p.op("vector", "tensor_add", dst, dst, pse[:, :],
                                    sem=ctx.dv, after=(mm,))
                    eslot[1] = tadd
                    eacc_tok[ekey] = tadd
        hid_war = last_emm
        w_war = p.tok(ctx.pe)

    oslot_war = [None, None]
    for t in range(2):
        for mq in range(4):
            for n in range(4):
                sl = (mq * 4 + n) % 2
                war = (oslot_war[sl],) if oslot_war[sl] else ()
                tx = p.op("sync", "dma_start", sem=ctx.ls, n=16,
                          out=rt[:, sl * 512:(sl + 1) * 512],
                          in_=r1[t * 512 + mq * 128:t * 512 + (mq + 1) * 128,
                                 n * 512:(n + 1) * 512], after=war)
                src = eacc[:, (t * 4 + mq) * 2048 + n * 512:(t * 4 + mq) * 2048 + (n + 1) * 512]
                tadd = p.op("vector", "tensor_add", ro[:, sl * 512:(sl + 1) * 512],
                            src, rt[:, sl * 512:(sl + 1) * 512], sem=ctx.dv,
                            after=(tx, eacc_tok[(t, mq, n)]) + war)
                oslot_war[sl] = p.op("sync", "dma_start", sem=ctx.st, n=16,
                                     out=o_r2[t * 512 + mq * 128:t * 512 + (mq + 1) * 128,
                                              n * 512:(n + 1) * 512],
                                     in_=ro[:, sl * 512:(sl + 1) * 512], after=(tadd,))
    if mega:
        ctx.barrier()
    ctx.al.end_stage()
    return ctx


def run_l0ffn(g, resid1_per_core):
    ctx = build_l0ffn()
    nc = ctx.finish()
    in_maps = []
    for i in range(NCORES):
        m = {"W0T": g["W0T"], "b0col": g["b0col"], "W1T": g["W1T"], "b1row": g["b1row"],
             "ident": g["ident"], "resid1": resid1_per_core[i]}
        in_maps.append(m)
    res = run_bass_kernel_spmd(nc, in_maps, core_ids=list(range(NCORES)))
    return res.results


def build_l1pre(stop_after=None, mega=None):
    ctx = mega["ctx"] if mega else Ctx("l1pre")
    ctx.al.begin_stage("d")
    ctx.psg = None
    ctx.str2g = None
    ctx.strowg = {0: None, 1: None}
    p, al = ctx.p, ctx.al
    r2 = mega["io"]["r2"] if mega else ctx.din("resid2", [T, D])
    wqkv_d = ctx.din("Wqkv", [D, 6144])
    qg_in = ctx.din("qg1", [128, 16])
    kg_in = ctx.din("kg1", [128, 16])
    cos_d = ctx.din("cosT_sl", [128, T], BF16)
    sin_d = ctx.din("sinT_sl", [128, T], BF16)
    psw_d = ctx.din("pswap", [128, 128], BF16)
    ctx.din("ident", [128, 128], BF16)
    if mega:
        io = mega["io"]
        o_q, o_k, o_v = io["Q1loc"], io["K1loc"], io["V1loc"]
        o_qn2, o_kn2 = io["qn21loc"], io["kn21loc"]
    else:
        o_q = ctx.dout("Q1", [2048, T], BF16)
        o_k = ctx.dout("K1", [2048, T], BF16)
        o_v = ctx.dout("V1", [T, 2048], BF16)
        o_qn2 = ctx.dout("qn21", [16, T])
        o_kn2 = ctx.dout("kn21", [16, T])

    if not hasattr(ctx, "ident"):
        ctx.common()
    wsb = al.sb("wsb", [128, 32768], BF16)
    lnx = al.sb("lnx", [128, 16 * T], BF16)
    lnx_tm = al.sb("lnx_tm", [128, D], BF16)
    xt = al.sb("xt", [128, D], F32)
    qa = al.sb("qa", [128, 16 * T], BF16)
    ev4 = al.sb("ev4", [128, 4 * 512], BF16)
    sqs = al.sb("sqs", [128, 2 * 512], BF16)
    psw = al.sb("sb_psw", [128, 128], BF16)
    uscr = al.sb("uscr", [128, 512], BF16)
    wscr = al.sb("wscr", [128, 512], BF16)
    cosT = al.sb("sb_cos", [128, T], BF16)
    sinT = al.sb("sb_sin", [128, T], BF16)
    qg = al.sb("sb_qg1", [128, 16], F32)
    kg = al.sb("sb_kg1", [128, 16], F32)
    strow = al.sb("strow", [1, 2 * 512], F32)
    str2 = al.sb("str2", [1, 512], F32)
    scr_sq = al.sb("scr_sq", [128, D], BF16)
    stats = {k: al.sb("st_" + k, [128, 1], F32) for k in ["ssq", "smean", "m", "var", "rstd"]}
    r128 = al.sb("r128", [128, T], F32)
    vtm = al.sb("vtm", [128, 2 * 2048], BF16)
    ps_a = al.ps("ps_a", [128, 512])
    ps_b = al.ps("ps_b", [128, 512])
    ps_tr = al.ps("ps_tr", [128, 2048], BF16)
    ps_st = al.ps("ps_st", [1, 512])
    ps_bc = al.ps("ps_bc", [128, 512])
    ps_s1 = al.ps("ps_s1", [128, 512])
    ps_s2 = al.ps("ps_s2", [128, 512])
    pool = SlotPool([ps_a, ps_b])
    trpool = SlotPool([ps_tr[:, 0:512], ps_tr[:, 1024:1536]])
    epool = SlotPool([ev4[:, i * 512:(i + 1) * 512] for i in range(4)])
    sqpool = SlotPool([sqs[:, 0:512], sqs[:, 512:1024]])
    swpool = SlotPool([ps_s1, ps_s2])
    bcslot = SlotPool([ps_bc])

    for dst, src in ((cosT, cos_d), (sinT, sin_d), (qg, qg_in), (kg, kg_in), (psw, psw_d)):
        p.op("sync", "dma_start", sem=ctx.ls, n=16, out=dst[:, :], in_=src[:, :])
    c_ready = p.tok(ctx.ls)

    xt_war = [None]
    tm_war = [None]
    lnx_ready = None
    for a in range(8):
        tx = p.op("sync", "dma_start", sem=ctx.ls, n=16, out=xt[:, :],
                  in_=r2[a * 128:(a + 1) * 128, :],
                  after=((xt_war[0],) if xt_war[0] is not None else ()))
        tn = emit_ln(ctx, xt[:, :], lnx_tm[:, :], scr_sq, stats,
                     (tx,) + ((tm_war[0],) if tm_war[0] is not None else ()))
        xt_war[0] = tn
        p.wait_tok("tensor", tn)
        for g4 in range(4):
            srcs = [lnx_tm[:, (g4 * 4 + j) * 128:(g4 * 4 + j) * 128 + 128]
                    for j in range(4)]
            dst = lnx[:, :].rearrange("p (k t) -> p k t", k=16)[:, g4 * 4:(g4 + 1) * 4,
                                                               a * 128:(a + 1) * 128]
            lnx_ready = transpose_group(ctx, srcs, dst, trpool.take())
        tm_war[0] = p.tok(ctx.pe)

    def lnx_rhs(k, t):
        ctx.p.wait_tok("tensor", lnx_ready)
        return lnx[:, k * T + t * 512:k * T + (t + 1) * 512]

    def qk_tensor(part, gain, o_fm, o_n2, w_war):
        last_to1 = [None]
        tw = p.op("gpsimd", "dma_start", sem=ctx.ld, n=16,
                  out=wsb[:, :].rearrange("p (k m) -> p k m", k=16),
                  in_=wqkv_d[:, part * 2048:(part + 1) * 2048].rearrange(
                      "(k p) m -> p k m", p=128),
                  after=(w_war,) if w_war else ())
        for t in range(2):
            stat_tok = None
            pending = None
            for m in range(16):
                slot = pool.take()
                ps, war = slot
                p.wait_tok("tensor", tw)
                if war is not None:
                    p.wait_tok("tensor", war)
                mm = None
                for k in range(16):
                    mm = p.op("tensor", "matmul", ps[:, :], wslice(wsb, 0, 2048, k, m),
                              lnx_rhs(k, t), start=(k == 0), stop=(k == 15),
                              sem=(ctx.pe if k == 15 else None))
                if pending is not None:
                    stat_tok = pending()
                tev = p.op("vector", "tensor_copy", qa[:, m * T + t * 512:m * T + (t + 1) * 512],
                           ps[:, :], sem=ctx.dv, after=(mm,))
                sqslot = sqpool.take()
                sqa_, sqwar = sqslot
                tsq = p.op("scalar", "activation", sqa_, ps[:, :], AF.Square, sem=ctx.sc,
                           after=(mm, tev) + ((sqwar,) if sqwar is not None else ()))
                slot[1] = (tev, tsq)

                def mk_stat(m=m, tsq=tsq, sqa_=sqa_, sqslot=sqslot):
                    def emit():
                        tk = p.op("tensor", "matmul", ps_st[0:1, :], ctx.ones[:, 0:1], sqa_,
                                  start=(m == 0), stop=(m == 15), sem=ctx.pe,
                                  after=(tsq,) + ((ctx.psg,) if (m == 0 and ctx.psg) else ()))
                        sqslot[1] = tk
                        return tk
                    return emit
                pending = mk_stat()
            stat_tok = pending()
            tr = rmsnorm_post2(ctx, ps_st[0:1, :], t, strow, r128, bcslot.take(),
                               2048, 1e-6, stat_tok)
            for m in range(16):
                ch = qa[:, m * T + t * 512:m * T + (t + 1) * 512]
                tg1 = p.op("vector", "tensor_scalar_mul", ch, ch, gain[:, m:m + 1],
                           sem=ctx.dv, after=(tr, c_ready))
                tg2 = p.op("vector", "tensor_mul", ch, ch, r128[:, t * 512:(t + 1) * 512],
                           sem=ctx.dv, after=(tg1,))
                sqslot = sqpool.take()
                sqa_, sqwar = sqslot
                tsq = p.op("scalar", "activation", sqa_, ch, AF.Square, sem=ctx.sc,
                           after=(tg2,) + ((sqwar,) if sqwar is not None else ()))
                tmm = p.op("tensor", "matmul", ps_st[0:1, :], ctx.ones[:, 0:1], sqa_,
                           start=True, stop=True, sem=ctx.pe,
                           after=(tsq,) + ((ctx.psg,) if ctx.psg else ()))
                sqslot[1] = tmm
                tse = p.op("vector", "tensor_copy", str2[0:1, :], ps_st[0:1, :], sem=ctx.dv,
                           after=(tmm,) + ((ctx.str2g,) if ctx.str2g else ()))
                ctx.psg = tse
                ctx.str2g = p.op("sync", "dma_start", sem=ctx.st, n=16,
                                 out=o_n2[m:m + 1, t * 512:(t + 1) * 512],
                                 in_=str2[0:1, :], after=(tse,))
                ch128 = qa[:, m * T + t * 512:m * T + (t + 1) * 512]
                Ct = cosT[:, t * 512:(t + 1) * 512]
                St = sinT[:, t * 512:(t + 1) * 512]
                es = epool.take()
                ev, ewar = es
                eww = (ewar,) if ewar is not None else ()
                swsl = swpool.take()
                ps_sw, swwar = swsl
                if swwar is not None:
                    p.wait_tok("tensor", swwar)
                tsw = p.op("tensor", "matmul", ps_sw[:, :], psw[:, :], ch128,
                           start=True, stop=True, sem=ctx.pe, after=(tg2, c_ready))
                tu = p.op("vector", "tensor_mul", uscr[:, :], ch128, Ct, sem=ctx.dv,
                          after=(tg2, c_ready))
                tw_ = p.op("vector", "tensor_mul", wscr[:, :], ps_sw[:, :], St, sem=ctx.dv,
                           after=(tsw,))
                swsl[1] = tw_
                to1 = p.op("vector", "tensor_add", ev[:, :], uscr[:, :], wscr[:, :],
                           sem=ctx.dv, after=(tu, tw_) + eww)
                last_to1[0] = to1
                es[1] = p.op("sync", "dma_start", sem=ctx.st, n=16,
                             out=o_fm[m * 128:(m + 1) * 128, t * 512:(t + 1) * 512],
                             in_=ev[:, :], after=(to1,))
        return p.tok(ctx.pe)

    if stop_after == "ln":
        ctx.al.end_stage()
        return ctx
    qtok = qk_tensor(0, qg, o_q, o_qn2, None)
    if stop_after == "q":
        ctx.al.end_stage()
        return ctx
    ktok = qk_tensor(1, kg, o_k, o_kn2, qtok)
    if stop_after == "k":
        ctx.al.end_stage()
        return ctx
    if mega:
        io = mega["io"]
        p.wait("gpsimd", ctx.st)
        p.op("gpsimd", "collective_compute", "AllReduce", ALU.add,
             replica_groups=[list(range(NCORES))],
             ins=[io["bar2"].opt()], outs=[io["bar2o"].opt()], sem=ctx.cc, n=1)
        p.wait("gpsimd", ctx.cc)
        for src, dst in (("K1loc", "K1AG"), ("kn21loc", "kn21AG")):
            p.op("gpsimd", "collective_compute", "AllGather", ALU.bypass,
                 replica_groups=[list(range(NCORES))],
                 ins=[io[src].opt()], outs=[io[dst].opt()], sem=ctx.cc, n=1)
        mega["ccK"] = p.tok(ctx.cc)

    tw = p.op("gpsimd", "dma_start", sem=ctx.ld, n=16,
              out=wsb[:, :].rearrange("p (k m) -> p k m", k=16),
              in_=wqkv_d[:, 4096:6144].rearrange("(k p) m -> p k m", p=128),
              after=(ktok,))
    qa_war = p.tok(ctx.gp)
    qa_war_v = p.tok(ctx.dv)
    vtm_war = [None, None]
    for t in range(2):
        vfm_tok = None
        for m in range(16):
            slot = pool.take()
            ps, war = slot
            p.wait_tok("tensor", tw)
            if war is not None:
                p.wait_tok("tensor", war)
            mm = None
            for k in range(16):
                mm = p.op("tensor", "matmul", ps[:, :], wslice(wsb, 0, 2048, k, m),
                          lnx_rhs(k, t), start=(k == 0), stop=(k == 15),
                          sem=(ctx.pe if k == 15 else None))
            vfm_tok = p.op("scalar", "activation",
                           qa[:, m * T + t * 512:m * T + (t + 1) * 512]
                           if False else qa[:, m * 512 + t * 8192:m * 512 + t * 8192 + 512],
                           ps[:, :], AF.Identity, sem=ctx.sc,
                           after=(mm, qa_war, qa_war_v))
            slot[1] = vfm_tok
        p.wait_tok("tensor", vfm_tok)
        for a in range(4):
            sl = a % 2
            if vtm_war[sl] is not None:
                p.wait_tok("vector", vtm_war[sl])
            for g4 in range(4):
                srcs = [qa[:, t * 8192 + (g4 * 4 + j) * 512 + a * 128:
                           t * 8192 + (g4 * 4 + j) * 512 + a * 128 + 128] for j in range(4)]
                dst = vtm[:, sl * 2048 + g4 * 512:sl * 2048 + (g4 + 1) * 512]
                tg = transpose_group(ctx, srcs, dst, trpool.take())
            vtm_war[sl] = p.op("sync", "dma_start", sem=ctx.st, n=16,
                               out=o_v[t * 512 + a * 128:t * 512 + (a + 1) * 128, :],
                               in_=vtm[:, sl * 2048:(sl + 1) * 2048], after=(tg,))
    if mega:
        io = mega["io"]
        p.wait("gpsimd", ctx.st)
        p.op("gpsimd", "collective_compute", "AllReduce", ALU.add,
             replica_groups=[list(range(NCORES))],
             ins=[io["bar3"].opt()], outs=[io["bar3o"].opt()], sem=ctx.cc, n=1)
        p.wait("gpsimd", ctx.cc)
        p.op("gpsimd", "collective_compute", "AllGather", ALU.bypass,
             replica_groups=[list(range(NCORES))],
             ins=[io["V1loc"].opt()], outs=[io["V1AG"].opt()], sem=ctx.cc, n=1)
        p.wait("gpsimd", ctx.cc)

        def nbcopy(g, io=io):
            pid = g.partition_id()
            o1 = g.snap((pid + 7) % 8 * 2048)
            g.dma_start(out=io["K1nb"][:, :],
                        in_=io["K1AG"][bass.ds(o1, 2048), :]).then_inc(ctx.ld, 16)
            o3 = g.snap((pid + 7) % 8 * 16)
            g.dma_start(out=io["kn21nb"][:, :],
                        in_=io["kn21AG"][bass.ds(o3, 16), :]).then_inc(ctx.ld, 16)
            o2 = g.snap((pid + 7) % 8 * 1024)
            g.dma_start(out=io["V1nb"][:, :],
                        in_=io["V1AG"][bass.ds(o2, 1024), :]).then_inc(ctx.ld, 16)
        p.emit("gpsimd", nbcopy)
        p.inc(ctx.ld, 48)
        mega["nb_tok"] = p.tok(ctx.ld)
        ctx.barrier()
    ctx.al.end_stage()
    return ctx


def run_l1pre(g, resid2_per_core):
    ctx = build_l1pre()
    nc = ctx.finish()
    cosb = _bf(g["cosT"])
    sinb = _bf(g["sinT"])
    in_maps = []
    for i in range(NCORES):
        m = {"Wqkv": g["Wqkv"], "qg1": g["qg1"], "kg1": g["kg1"], "ident": g["ident"],
             "pswap": g["pswap"], "resid2": resid2_per_core[i],
             "cosT_sl": np.ascontiguousarray(cosb[:, i * T:(i + 1) * T]),
             "sinT_sl": np.ascontiguousarray(sinb[:, i * T:(i + 1) * T])}
        in_maps.append(m)
    res = run_bass_kernel_spmd(nc, in_maps, core_ids=list(range(NCORES)))
    return res.results


ATT_SCALE1 = 1.0 / float(np.sqrt(128.0))


def build_l1attn(mega=None):
    ctx = mega["ctx"] if mega else Ctx("l1attn")
    ctx.al.begin_stage("e")
    p, al = ctx.p, ctx.al
    if mega:
        io = mega["io"]
        x = io["r2"]
        q_d, qn2_d = io["Q1loc"], io["qn21loc"]
        k1loc, v1loc, kn21loc = io["K1loc"], io["V1loc"], io["kn21loc"]
        k1nb, v1nb, kn21nb = io["K1nb"], io["V1nb"], io["kn21nb"]
        o_r3 = io["r3"]
        mscr = io["mscrE"]
    else:
        x = ctx.din("resid2", [T, D])
        q_d = ctx.din("Q1", [2048, T], BF16)
        k_d = ctx.din("K1in", [2048, 2 * 1536], BF16)
        v_d = ctx.din("V1in", [2 * 1536, 2048], BF16)
        qn2_d = ctx.din("qn21", [16, T])
        kn2_d = ctx.din("kn21in", [16, 2 * 1536])
        o_r3 = ctx.dout("resid3", [T, D])
        mscr = ctx.nc.dram_tensor("mscr", [16, 2], F32, kind="Internal").ap()
    km_d = ctx.din("keymask1", [128, 6])
    maskc_d = ctx.din("maskC", [128, 2048])
    maskw_d = ctx.din("maskW", [128, 2048])
    wo_d = ctx.din("Wo", [D, D])
    ctx.din("ident", [128, 128], BF16)

    if not hasattr(ctx, "ident"):
        ctx.common()
    wsb = al.sb("wsb", [128, 16 * 2048], BF16)
    q_sb = al.sb("q_sb", [128, 16 * T], BF16)
    pT = al.sb("pT", [128, 2 * 2048], BF16)
    attn = al.sb("attn", [128, 16 * 512], BF16)
    maskc = al.sb("sb_maskc", [128, 2048], F32)
    maskw = al.sb("sb_maskw", [128, 2048], F32)
    keymask = al.sb("sb_keymask", [128, 6], F32)
    biasC = al.sb("biasC", [128, 32 * 3], F32)
    smask = al.sb("smask", [128, 2048], F32)
    kn_t = al.sb("kn_t", [128, 3 * 512], BF16)
    v_t = al.sb("v_t", [128, 3 * 512], BF16)
    lsum = al.sb("lsum", [128, 2 * 512], BF16)
    lsum2 = al.sb("lsum2", [128, 512], BF16)
    linv = al.sb("linv", [128, 512], F32)
    rl = al.sb("rl", [1, 512], F32)
    xrt = al.sb("xrt", [128, 2 * 512], F32)
    rout = al.sb("rout", [128, 2 * 512], F32)
    qn2 = al.sb("sb_qn2", [16, T], F32)
    k2r = al.sb("k2r", [16, 2 * 1536], F32)
    kmax2 = al.sb("kmax2", [16, 2], F32)
    qmax2 = al.sb("qmax2", [16, 2], F32)
    negM = al.sb("negM", [16, 2], F32)
    mrow = al.sb("mrow", [1, 32], F32)
    Mb = al.sb("Mb", [128, 32], F32)
    ps_sa = al.ps("ps_sa", [128, 1024])
    ps_sb = al.ps("ps_sb", [128, 1024])
    ps_pv = al.ps("ps_pv", [128, 512])
    ps_l = al.ps("ps_l", [1, 512])
    ps_bc = al.ps("ps_bc", [128, 512])
    ps_o = al.ps("ps_o", [128, 512])

    tw = p.op("gpsimd", "dma_start", sem=ctx.ld, n=16,
              out=wsb[:, :].rearrange("p (k m) -> p k m", k=16),
              in_=wo_d.rearrange("(k p) m -> p k m", p=128))
    p.op("sync", "dma_start", sem=ctx.ls, n=16,
         out=q_sb[:, :].rearrange("p (h t) -> p h t", h=16),
         in_=q_d.rearrange("(h p) t -> p h t", p=128))
    p.op("sync", "dma_start", sem=ctx.ls, n=16, out=maskc[:, :], in_=maskc_d[:, :])
    p.op("sync", "dma_start", sem=ctx.ls, n=16, out=maskw[:, :], in_=maskw_d[:, :])
    tkm = p.op("sync", "dma_start", sem=ctx.ls, n=16, out=keymask[:, :], in_=km_d[:, :])
    tq3 = p.op("sync", "dma_start", sem=ctx.ls, n=16, out=qn2[:, :], in_=qn2_d[:, :])
    regs = {}
    if mega:

        tq5 = p.op("sync", "dma_start", sem=ctx.ls, n=16, out=k2r[:, 0:1024],
                   in_=kn21loc[:, :])
        tq5b = p.op("gpsimd", "dma_start", sem=ctx.ld, n=16, out=k2r[:, 1024:2048],
                    in_=kn21nb[:, :], after=(mega["nb_tok"],))
    else:
        tq5 = p.op("sync", "dma_start", sem=ctx.ls, n=16, out=k2r[:, :], in_=kn2_d[:, :])
    q_ready = p.tok(ctx.ls)

    tq4 = None
    if mega:
        ta_ = p.op("vector", "reduce_max", kmax2[:, 0:1], k2r[:, 0:1024], axis=AX.X,
                   sem=ctx.dv, after=(tq5,))
        tb_ = p.op("vector", "reduce_max", kmax2[:, 1:2], k2r[:, 1024:2048], axis=AX.X,
                   sem=ctx.dv, after=(tq5b, ta_))
        tkx = p.op("vector", "tensor_max", kmax2[:, 0:1], kmax2[:, 0:1], kmax2[:, 1:2],
                   sem=ctx.dv, after=(tb_,))
        for q in range(2):
            tq4 = p.op("vector", "reduce_max", qmax2[:, q:q + 1],
                       qn2[:, q * 512:(q + 1) * 512], axis=AX.X, sem=ctx.dv,
                       after=(tq3, tkx))
        t5 = p.op("vector", "tensor_scalar_mul", negM[:, :], qmax2[:, :], kmax2[:, 0:1],
                  sem=ctx.dv, after=(tq4,))
    else:
        for q in range(2):
            p.op("vector", "reduce_max", kmax2[:, q:q + 1], k2r[:, q * 1536:(q + 1) * 1536],
                 axis=AX.X, sem=ctx.dv, after=(tq5,))
            tq4 = p.op("vector", "reduce_max", qmax2[:, q:q + 1], qn2[:, q * 512:(q + 1) * 512],
                       axis=AX.X, sem=ctx.dv, after=(tq3,))
        t5 = p.op("vector", "tensor_mul", negM[:, :], qmax2[:, :], kmax2[:, :],
                  sem=ctx.dv, after=(tq4,))
    t6 = p.op("scalar", "activation", negM[:, :], negM[:, :], AF.Sqrt, sem=ctx.sc, after=(t5,))
    t7 = p.op("vector", "tensor_scalar_mul", negM[:, :], negM[:, :], -ATT_SCALE1,
              sem=ctx.dv, after=(t6,))
    t8 = p.op("sync", "dma_start", sem=ctx.st, n=16, out=mscr[:, :], in_=negM[:, :], after=(t7,))
    t9 = p.op("sync", "dma_start", sem=ctx.ls, n=16, out=mrow[:, :],
              in_=mscr.rearrange("a b -> (a b)")[None, :], after=(t8,))
    t10 = p.op("tensor", "matmul", ps_bc[:, 0:32], ctx.ones32[0:1, 0:128], mrow[0:1, :],
               start=True, stop=True, sem=ctx.pe, after=(t9,))
    t11 = p.op("vector", "tensor_copy", Mb[:, :], ps_bc[:, 0:32], sem=ctx.dv, after=(t10,))
    bias_tok = None
    for h in range(16):
        for q in range(2):
            idx = h * 2 + q
            bias_tok = p.op("vector", "tensor_scalar_add",
                            biasC[:, idx * 3:(idx + 1) * 3],
                            keymask[:, q * 3:(q + 1) * 3], Mb[:, idx:idx + 1],
                            sem=ctx.dv, after=(t11, tkm))

    NSLOT = 3
    st_k = {"war": [None, None, None], "exp_war": [None, None],
            "pT_war": [None, None], "smask_war": None, "psg": t11,
            "bc_war": t11, "attn_war": None}

    def emit_slot_loads(h, q, j):
        b = j
        war = (st_k["war"][b],) if st_k["war"][b] else ()
        if mega:
            rel = 512 * (q + j - 2)
            if rel < 0:
                loc = rel + 1024
                war = war + (mega["nb_tok"],)
                kn_src = k1nb[h * 128:(h + 1) * 128, loc:loc + 512]
                v_src = v1nb[loc:loc + 512, h * 128:(h + 1) * 128]
            else:
                kn_src = k1loc[h * 128:(h + 1) * 128, rel:rel + 512]
                v_src = v1loc[rel:rel + 512, h * 128:(h + 1) * 128]
        else:
            kn_src = k_d[h * 128:(h + 1) * 128, q * 1536 + j * 512:q * 1536 + (j + 1) * 512]
            v_src = v_d[q * 1536 + j * 512:q * 1536 + (j + 1) * 512, h * 128:(h + 1) * 128]
        t1 = p.op("sync", "dma_start", sem=ctx.ls, n=16,
                  out=kn_t[:, b * 512:(b + 1) * 512], in_=kn_src, after=war)
        t3 = p.op("gpsimd", "dma_start", sem=ctx.ld, n=16,
                  out=v_t[:, b * 512:(b + 1) * 512].rearrange("p (c d) -> p c d", c=4),
                  in_=v_src.rearrange("(c p) d -> p c d", p=128), after=war)
        return (t1, t3)

    SLOTS = [(q_, h_, j_) for q_ in range(2) for h_ in range(16) for j_ in range(NSLOT)]
    pump_state = {"i": 0}
    ld_q = {}

    def pump():
        i = pump_state["i"]
        if i < len(SLOTS):
            q_, h_, j_ = SLOTS[i]
            ld_q[(q_, h_, j_)] = emit_slot_loads(h_, q_, j_)
            pump_state["i"] = i + 1

    def flash_head(h, q):
        hq = h * 2 + q
        q_rhs = q_sb[:, h * T + q * 512:h * T + (q + 1) * 512]
        prev = None
        for j in range(NSLOT + 1):
            if j < NSLOT:
                ld_toks = ld_q.pop((q, h, j))
                b = j
                mm_a = mm_b = None
                for c in range(4):
                    sps = ps_sa if c < 2 else ps_sb
                    if c in (0, 2) and st_k["exp_war"][0 if c < 2 else 1] is not None:
                        p.wait_tok("tensor", st_k["exp_war"][0 if c < 2 else 1])
                    if c == 0:
                        for tt in ld_toks:
                            p.wait_tok("tensor", tt)
                        p.wait_tok("tensor", q_ready)
                    mm = p.op("tensor", "matmul", sps[:, (c % 2) * 512:(c % 2 + 1) * 512],
                              kn_t[:, b * 512 + c * 128:b * 512 + (c + 1) * 128],
                              q_rhs, start=True, stop=True,
                              sem=(ctx.pe if c in (1, 3) else None))
                    if c == 1:
                        mm_a = mm
                mm_b = mm
                pb = j % 2
                pw = (st_k["pT_war"][pb],) if st_k["pT_war"][pb] else ()
                bias_ap = biasC[:, hq * 3 + j:hq * 3 + j + 1]
                mk = maskw if j == 0 else (maskc if j == 2 else None)
                if mk is not None:
                    smw = (st_k["smask_war"],) if st_k["smask_war"] else ()
                    tm1 = p.op("vector", "tensor_add", smask[:, 0:1024],
                               ps_sa[:, :], mk[:, 0:1024], sem=ctx.dv,
                               after=(mm_a, bias_tok) + smw)
                    tm2 = p.op("vector", "tensor_add", smask[:, 1024:2048],
                               ps_sb[:, :], mk[:, 1024:2048], sem=ctx.dv,
                               after=(mm_b,))
                    e_a = p.op("scalar", "activation", pT[:, pb * 2048:pb * 2048 + 1024],
                               smask[:, 0:1024], AF.Exp, bias=bias_ap,
                               scale=ATT_SCALE1, sem=ctx.sc, after=(tm1,) + pw)
                    e_b = p.op("scalar", "activation",
                               pT[:, pb * 2048 + 1024:pb * 2048 + 2048],
                               smask[:, 1024:2048], AF.Exp, bias=bias_ap,
                               scale=ATT_SCALE1, sem=ctx.sc, after=(tm2,))
                    st_k["smask_war"] = e_b
                else:
                    e_a = p.op("scalar", "activation", pT[:, pb * 2048:pb * 2048 + 1024],
                               ps_sa[:, :], AF.Exp, bias=bias_ap, scale=ATT_SCALE1,
                               sem=ctx.sc, after=(mm_a, bias_tok) + pw)
                    e_b = p.op("scalar", "activation",
                               pT[:, pb * 2048 + 1024:pb * 2048 + 2048],
                               ps_sb[:, :], AF.Exp, bias=bias_ap, scale=ATT_SCALE1,
                               sem=ctx.sc, after=(mm_b,))
                st_k["exp_war"][0] = e_a
                st_k["exp_war"][1] = e_b
            if prev is not None:
                jp, pbp, ea_p, eb_p = prev
                c0 = pbp * 2048
                lw = st_k.get("lsum_war_%d" % pbp)
                tl1 = p.op("vector", "tensor_add", lsum[:, pbp * 512:(pbp + 1) * 512],
                           pT[:, c0:c0 + 512], pT[:, c0 + 512:c0 + 1024],
                           sem=ctx.dv, after=(ea_p,) + ((lw,) if lw else ()))
                tl2 = p.op("vector", "tensor_add", lsum2[:, :],
                           pT[:, c0 + 1024:c0 + 1536], pT[:, c0 + 1536:c0 + 2048],
                           sem=ctx.dv, after=(eb_p,))
                tl3 = p.op("vector", "tensor_add", lsum[:, pbp * 512:(pbp + 1) * 512],
                           lsum[:, pbp * 512:(pbp + 1) * 512], lsum2[:, :],
                           sem=ctx.dv, after=(tl2, tl1))
                lmm = p.op("tensor", "matmul", ps_l[0:1, :], ctx.ones[:, 0:1],
                     lsum[:, pbp * 512:(pbp + 1) * 512], start=(jp == 0),
                     stop=(jp == NSLOT - 1), sem=ctx.pe,
                     after=(tl3,) + ((st_k["psg"],) if jp == 0 else ()))
                st_k["lsum_war_%d" % pbp] = lmm
                pv_tok = None
                for c in range(4):
                    pv_tok = p.op("tensor", "matmul", ps_pv[:, :],
                                  v_t[:, jp * 512 + c * 128:jp * 512 + (c + 1) * 128],
                                  pT[:, c0 + c * 512:c0 + (c + 1) * 512],
                                  start=(jp == 0 and c == 0),
                                  stop=(jp == NSLOT - 1 and c == 3),
                                  after=((ea_p,) if c < 2 else (eb_p,)) +
                                        ((st_k["psg"],) if (jp == 0 and c == 0) else ()),
                                  sem=ctx.pe)
                st_k["war"][jp] = pv_tok
                st_k["pT_war"][pbp] = (pv_tok, tl3)
                pump()
            if j < NSLOT:
                prev = (j, j % 2, e_a, e_b)
        lpv_done = p.tok(ctx.pe)
        trc = p.op("vector", "reciprocal", rl[:, :], ps_l[0:1, :], sem=ctx.dv,
                   after=(lpv_done,))
        tb = p.op("tensor", "matmul", ps_bc[:, :], ctx.ones32[0:1, 0:128], rl[0:1, :],
                  start=True, stop=True, sem=ctx.pe,
                  after=(trc,) + ((st_k["bc_war"],) if st_k["bc_war"] else ()))
        tc = p.op("vector", "tensor_copy", linv[:, :], ps_bc[:, :], sem=ctx.dv, after=(tb,))
        st_k["bc_war"] = tc
        aw = (st_k["attn_war"],) if st_k["attn_war"] else ()
        ta = p.op("vector", "tensor_mul", attn[:, h * 512:(h + 1) * 512],
                  ps_pv[:, :], linv[:, :], sem=ctx.dv, after=(tc,) + aw)
        st_k["psg"] = ta
        return ta

    pump(); pump(); pump()
    for q in range(2):
        attn_tok = None
        for h in range(16):
            attn_tok = flash_head(h, q)
        po_war = None
        for mq in range(4):
            for n in range(4):
                mm = None
                p.wait_tok("tensor", tw)
                p.wait_tok("tensor", attn_tok)
                if po_war is not None:
                    p.wait_tok("tensor", po_war)
                for k in range(16):
                    mm = p.op("tensor", "matmul", ps_o[:, :],
                              attn[:, k * 512 + mq * 128:k * 512 + (mq + 1) * 128],
                              wsb[:, k * 2048 + n * 512:k * 2048 + (n + 1) * 512],
                              start=(k == 0), stop=(k == 15),
                              sem=(ctx.pe if k == 15 else None))
                sl = (mq * 4 + n) % 2
                xa = xrt[:, sl * 512:(sl + 1) * 512]
                ra = rout[:, sl * 512:(sl + 1) * 512]
                war = st_k.get("oslot_war_%d" % sl)
                tx = p.op("sync", "dma_start", sem=ctx.ls, n=16, out=xa,
                          in_=x[q * 512 + mq * 128:q * 512 + (mq + 1) * 128,
                                n * 512:(n + 1) * 512],
                          after=((war,) if war else ()))
                tadd = p.op("vector", "tensor_add", ra, ps_o[:, :], xa, sem=ctx.dv,
                            after=(mm, tx) + ((war,) if war else ()))
                po_war = tadd
                st_k["oslot_war_%d" % sl] = p.op(
                    "sync", "dma_start", sem=ctx.st, n=16,
                    out=o_r3[q * 512 + mq * 128:q * 512 + (mq + 1) * 128,
                             n * 512:(n + 1) * 512],
                    in_=ra, after=(tadd,))
        st_k["attn_war"] = p.tok(ctx.pe)
    if mega:
        ctx.barrier()
    ctx.al.end_stage()
    return ctx


def run_l1attn(g, r2_cores, d_res):
    ctx = build_l1attn()
    nc = ctx.finish()
    K1_all = np.concatenate([d_res[i]["K1"] for i in range(NCORES)], axis=1)
    V1_all = np.concatenate([d_res[i]["V1"] for i in range(NCORES)], axis=0)
    kn21_all = np.concatenate([d_res[i]["kn21"] for i in range(NCORES)], axis=1)
    in_maps = []
    for i in range(NCORES):
        kin = np.zeros((2048, 2 * 1536), K1_all.dtype)
        vin = np.zeros((2 * 1536, 2048), V1_all.dtype)
        k2in = np.zeros((16, 2 * 1536), np.float32)
        km = np.zeros((128, 6), np.float32)
        for q in range(2):
            q0 = 1024 * i + 512 * q
            lo = q0 - 1024
            for s in range(3):
                a, b_ = lo + s * 512, lo + (s + 1) * 512
                if b_ <= 0:
                    km[:, q * 3 + s] = NEG
                    continue
                aa = max(a, 0)
                pad = aa - a
                kin[:, q * 1536 + s * 512 + pad:q * 1536 + (s + 1) * 512] = K1_all[:, aa:b_]
                vin[q * 1536 + s * 512 + pad:q * 1536 + (s + 1) * 512, :] = V1_all[aa:b_, :]
                k2in[:, q * 1536 + s * 512 + pad:q * 1536 + (s + 1) * 512] = kn21_all[:, aa:b_]
                if pad > 0 and s != 2:
                    pass
        m = {"resid2": r2_cores[i], "Q1": d_res[i]["Q1"], "qn21": d_res[i]["qn21"],
             "K1in": kin, "V1in": vin, "kn21in": k2in, "keymask1": km,
             "maskC": g["maskC"], "maskW": g["maskW"], "Wo": g["Wo"], "ident": g["ident"]}
        in_maps.append(m)
    res = run_bass_kernel_spmd(nc, in_maps, core_ids=list(range(NCORES)))
    return res.results


def build_l1moe(mega=None):
    ctx = mega["ctx"] if mega else Ctx("l1moe")
    ctx.al.begin_stage("f")
    ctx.psg = None
    ctx.strowg = {0: None, 1: None}
    p, al = ctx.p, ctx.al
    r3 = mega["io"]["r3"] if mega else ctx.din("resid3", [T, D])
    rw_d = ctx.din("rwT", [D, 8])
    rb_d = ctx.din("rbrow", [1, 8])
    gw_d = ctx.din("gw", [8, D, D])
    uw_d = ctx.din("uw", [8, D, D])
    gb_d = ctx.din("gbcol", [8, 128, 16])
    ub_d = ctx.din("ubcol", [8, 128, 16])
    dw_d = ctx.din("dw", [8, D, D])
    db_d = ctx.din("dbrow", [8, D])
    ctx.din("ident", [128, 128], BF16)
    o_out = ctx.dout("out_sl", [T, D])

    if not hasattr(ctx, "ident"):
        ctx.common()
    resid = al.sb("resid", [128, 8 * D], F32)
    lnx = al.sb("lnx", [128, 16 * T], BF16)
    lnx_tm = al.sb("lnx_tm", [128, D], BF16)
    xt = al.sb("xt", [128, D], F32)
    WQ = 8192
    wsb = al.sb("wsb", [128, 2 * WQ], BF16)
    gsb = al.sb("gsb", [128, 16 * 512], BF16)
    act = al.sb("act", [128, 16 * 1024], BF16)
    u2s = al.sb("u2s", [128, 512], BF16)
    sgs = al.sb("sgs", [128, 512], BF16)
    rw_sb = al.sb("sb_rw", [128, 16 * 8], BF16)
    rb_sb = al.sb("sb_rb", [1, 8], BF16)
    gb = al.sb("sb_gb", [128, 16], F32)
    ub = al.sb("sb_ub", [128, 16], F32)
    db = al.sb("sb_db", [1, D], BF16)
    logits = al.sb("logits", [128, 8 * 8], F32)
    scores = al.sb("scores", [128, 8 * 8], F32)
    m1 = al.sb("sb_m1", [128, 1], F32)
    m2 = al.sb("sb_m2", [128, 1], F32)
    w1 = al.sb("sb_w1", [128, 1], F32)
    w2 = al.sb("sb_w2", [128, 1], F32)
    msk = al.sb("sb_msk", [128, 8], F32)
    msk2 = al.sb("sb_msk2", [128, 8], F32)
    scr_sq = al.sb("scr_sq", [128, D], BF16)
    stats = {k: al.sb("st_" + k, [128, 1], F32) for k in ["ssq", "smean", "m", "var", "rstd"]}
    ps_a = al.ps("ps_a", [128, 512])
    ps_b = al.ps("ps_b", [128, 512])
    ps_tr = al.ps("ps_tr", [128, 2048], BF16)
    ps_rt = al.ps("ps_rt", [128, 64])
    ps_d = al.ps("ps_d", [128, 512])
    ps_d2 = al.ps("ps_d2", [128, 512])
    pool = SlotPool([ps_a, ps_b])
    dpool = SlotPool([ps_d, ps_d2])
    trpool = SlotPool([ps_tr[:, 0:512], ps_tr[:, 1024:1536]])

    trw = p.op("gpsimd", "dma_start", sem=ctx.ld, n=16,
               out=rw_sb[:, :].rearrange("p (k e) -> p k e", k=16),
               in_=rw_d.rearrange("(k p) e -> p k e", p=128))
    trb = p.op("gpsimd", "dma_start", sem=ctx.ld, n=16, out=rb_sb[:, :], in_=rb_d[:, :])

    res_toks = []
    tm_war = [None]
    lnx_ready = None
    for a in range(8):
        tx = p.op("sync", "dma_start", sem=ctx.ls, n=16, out=resid[:, a * D:(a + 1) * D],
                  in_=r3[a * 128:(a + 1) * 128, :])
        res_toks.append(tx)
        tn = emit_ln(ctx, resid[:, a * D:(a + 1) * D], lnx_tm[:, :], scr_sq, stats,
                     (tx,) + ((tm_war[0],) if tm_war[0] is not None else ()))
        p.wait_tok("tensor", tn)
        for g4 in range(4):
            srcs = [lnx_tm[:, (g4 * 4 + j) * 128:(g4 * 4 + j) * 128 + 128]
                    for j in range(4)]
            dst = lnx[:, :].rearrange("p (k t) -> p k t", k=16)[:, g4 * 4:(g4 + 1) * 4,
                                                               a * 128:(a + 1) * 128]
            lnx_ready = transpose_group(ctx, srcs, dst, trpool.take())
        tm_war[0] = p.tok(ctx.pe)

    sc_toks = []
    rt_guard = None
    for a in range(8):
        mm = None
        p.wait_tok("tensor", trw)
        p.wait_tok("tensor", lnx_ready)
        if rt_guard is not None:
            p.wait_tok("tensor", rt_guard)
        for k in range(16):
            mm = p.op("tensor", "matmul", ps_rt[:, a * 8:(a + 1) * 8],
                      lnx[:, k * T + a * 128:k * T + (a + 1) * 128],
                      rw_sb[:, k * 8:(k + 1) * 8],
                      start=(k == 0), stop=False)
        mm = p.op("tensor", "matmul", ps_rt[:, a * 8:(a + 1) * 8], ctx.ones[0:1, 0:128],
                  rb_sb[0:1, :], start=False, stop=True, sem=ctx.pe, after=(trb,))
        la = logits[:, a * 8:(a + 1) * 8]
        sa = scores[:, a * 8:(a + 1) * 8]
        t1 = p.op("vector", "tensor_copy", la, ps_rt[:, a * 8:(a + 1) * 8], sem=ctx.dv,
                  after=(mm,))
        rt_guard = t1
        t2 = p.op("vector", "reduce_max", m1[:, :], la, axis=AX.X, sem=ctx.dv, after=(t1,))
        t3 = p.op("vector", "tensor_single_scalar", msk[:, :], la, m1[:, :],
                  ALU.is_equal, sem=ctx.dv, after=(t2,))
        t4 = p.op("vector", "scalar_tensor_tensor", msk2[:, :], msk[:, :], -1.0e30, la,
                  ALU.mult, ALU.add, sem=ctx.dv, after=(t3,))
        t5 = p.op("vector", "reduce_max", m2[:, :], msk2[:, :], axis=AX.X, sem=ctx.dv,
                  after=(t4,))
        t6 = p.op("vector", "tensor_sub", w1[:, :], m1[:, :], m2[:, :], sem=ctx.dv,
                  after=(t5,))
        t7 = p.op("scalar", "activation", w1[:, :], w1[:, :], AF.Sigmoid, sem=ctx.sc,
                  after=(t6,))
        t8 = p.op("vector", "tensor_scalar", w2[:, :], w1[:, :], -1.0, 1.0,
                  ALU.mult, ALU.add, sem=ctx.dv, after=(t7,))
        t9 = p.op("vector", "tensor_single_scalar", msk2[:, :], la, m2[:, :],
                  ALU.is_equal, sem=ctx.dv, after=(t8,))
        t10 = p.op("vector", "tensor_scalar_mul", sa, msk[:, :], w1[:, :], sem=ctx.dv,
                   after=(t9,))
        t11 = p.op("vector", "scalar_tensor_tensor", sa, msk2[:, :], w2[:, :], sa,
                   ALU.mult, ALU.add, sem=ctx.dv, after=(t10,))
        sc_toks.append(t11)

    wpool = SlotPool([wsb[:, i * 4096:(i + 1) * 4096] for i in range(4)])
    pool4 = SlotPool([ps_a, ps_b, ps_d, ps_d2])
    gch = al.sb("gch", [128, 512], BF16)
    sgch = al.sb("sgch", [128, 512], BF16)
    u2ch = al.sb("u2ch", [128, 512], BF16)
    glch = al.sb("glch", [128, 512], BF16)
    act_war = None
    bias_war = None
    db_war = None
    dpool2 = SlotPool([wsb[:, 0:8192], wsb[:, 8192:16384]])
    sc_tok = sc_toks[-1]
    res_out_tok = {}
    for e in range(8):
        tgb = p.op("sync", "dma_start", sem=ctx.ls, n=16, out=gb[:, :], in_=gb_d[e, :, :],
                   after=(bias_war,) if bias_war else ())
        tub = p.op("sync", "dma_start", sem=ctx.ls, n=16, out=ub[:, :], in_=ub_d[e, :, :],
                   after=(bias_war,) if bias_war else ())
        tdb = p.op("gpsimd", "dma_start", sem=ctx.ld, n=16, out=db[:, :],
                   in_=db_d[e:e + 1, :], after=(db_war,) if db_war else ())
        last_act = None
        last_ts = None
        for pd in range(8):
            wg_s = wpool.take()
            wg, wgw = wg_s
            twg = p.op("gpsimd", "dma_start", sem=ctx.ld, n=16,
                       out=wg[:, :].rearrange("p (k m) -> p k m", k=16),
                       in_=gw_d[e, :, pd * 256:(pd + 1) * 256].rearrange(
                           "(k p) m -> p k m", p=128),
                       after=(wgw,) if wgw else ())
            wu_s = wpool.take()
            wu, wuw = wu_s
            twu = p.op("gpsimd", "dma_start", sem=ctx.ld, n=16,
                       out=wu[:, :].rearrange("p (k m) -> p k m", k=16),
                       in_=uw_d[e, :, pd * 256:(pd + 1) * 256].rearrange(
                           "(k p) m -> p k m", p=128),
                       after=(wuw,) if wuw else ())
            for mi in range(2):
                m = pd * 2 + mi
                g0 = pool4.take()
                g1 = pool4.take()
                p.wait_tok("tensor", twg)
                p.wait_tok("tensor", lnx_ready)
                if act_war is not None:
                    p.wait_tok("tensor", act_war)
                for sl_ in (g0, g1):
                    if sl_[1] is not None:
                        p.wait_tok("tensor", sl_[1])
                gmm = [None, None]
                for k in range(16):
                    w_ap = wg[:, k * 256 + mi * 128:k * 256 + (mi + 1) * 128]
                    gmm[0] = p.op("tensor", "matmul", g0[0][:, :], w_ap,
                                  lnx[:, k * T:k * T + 512],
                                  start=(k == 0), stop=(k == 15),
                                  sem=(ctx.pe if k == 15 else None))
                    gmm[1] = p.op("tensor", "matmul", g1[0][:, :], w_ap,
                                  lnx[:, k * T + 512:k * T + 1024],
                                  start=(k == 0), stop=(k == 15),
                                  sem=(ctx.pe if k == 15 else None))
                u0 = pool4.take()
                u1 = pool4.take()
                p.wait_tok("tensor", twu)
                for sl_ in (u0, u1):
                    if sl_[1] is not None:
                        p.wait_tok("tensor", sl_[1])
                umm = [None, None]
                for k in range(16):
                    w_ap = wu[:, k * 256 + mi * 128:k * 256 + (mi + 1) * 128]
                    umm[0] = p.op("tensor", "matmul", u0[0][:, :], w_ap,
                                  lnx[:, k * T:k * T + 512],
                                  start=(k == 0), stop=(k == 15),
                                  sem=(ctx.pe if k == 15 else None))
                    umm[1] = p.op("tensor", "matmul", u1[0][:, :], w_ap,
                                  lnx[:, k * T + 512:k * T + 1024],
                                  start=(k == 0), stop=(k == 15),
                                  sem=(ctx.pe if k == 15 else None))
                for t, gsl, usl in ((0, g0, u0), (1, g1, u1)):
                    tg1 = p.op("vector", "tensor_scalar", gch[:, :], gsl[0][:, :],
                               gb[:, m:m + 1], 7.0, ALU.add, ALU.min, sem=ctx.dv,
                               after=(gmm[t], tgb) + ((last_ts,) if last_ts else ()))
                    gsl[1] = tg1
                    tsg = p.op("scalar", "activation", sgch[:, :], gch[:, :], AF.Sigmoid,
                               scale=1.702, sem=ctx.sc, after=(tg1,))
                    tu1 = p.op("vector", "tensor_scalar", u2ch[:, :], usl[0][:, :],
                               ub[:, m:m + 1], -7.0, ALU.add, ALU.max, sem=ctx.dv,
                               after=(umm[t], tub))
                    usl[1] = tu1
                    tu2 = p.op("vector", "tensor_scalar", u2ch[:, :], u2ch[:, :],
                               7.0, 1.0, ALU.min, ALU.add, sem=ctx.dv, after=(tu1,))
                    tgl = p.op("vector", "tensor_mul", glch[:, :], gch[:, :], sgch[:, :],
                               sem=ctx.dv, after=(tsg,))
                    last_act = p.op("vector", "tensor_mul",
                                    act[:, m * 1024 + t * 512:m * 1024 + (t + 1) * 512],
                                    u2ch[:, :], glch[:, :], sem=ctx.dv, after=(tgl, tu2))
                    last_ts = last_act
            piece_done = p.tok(ctx.pe)
            wg_s[1] = piece_done
            wu_s[1] = piece_done
        bias_war = last_act
        dn_last = None
        for nd in range(4):
            wd_s = dpool2.take()
            wd, wdw = wd_s
            p.wait_tok("gpsimd", last_act)
            twd = p.op("gpsimd", "dma_start", sem=ctx.ld, n=16,
                       out=wd[:, :].rearrange("p (k m) -> p k m", k=16),
                       in_=dw_d[e, :, nd * 512:(nd + 1) * 512].rearrange(
                           "(k p) m -> p k m", p=128),
                       after=((wdw,) if wdw else ()) + ((act_war,) if act_war else ()))
            for t in range(2):
                for mq in range(4):
                    dslot = pool4.take()
                    dps, dwar = dslot
                    p.wait_tok("tensor", twd)
                    p.wait_tok("tensor", last_act)
                    if dwar is not None:
                        p.wait_tok("tensor", dwar)
                    mm = None
                    for k in range(16):
                        mm = p.op("tensor", "matmul", dps[:, :],
                                  act[:, k * 1024 + t * 512 + mq * 128:
                                      k * 1024 + t * 512 + (mq + 1) * 128],
                                  wd[:, k * 512:(k + 1) * 512],
                                  start=(k == 0), stop=False)
                    mm = p.op("tensor", "matmul", dps[:, :], ctx.ones[0:1, 0:128],
                              db[0:1, nd * 512:(nd + 1) * 512], start=False, stop=True,
                              sem=ctx.pe, after=(tdb,))
                    dn_last = mm
                    a = t * 4 + mq
                    rkey = (a, nd)
                    tacc = p.op("vector", "scalar_tensor_tensor",
                                resid[:, a * D + nd * 512:a * D + (nd + 1) * 512],
                                dps[:, :], scores[:, a * 8 + e:a * 8 + e + 1],
                                resid[:, a * D + nd * 512:a * D + (nd + 1) * 512],
                                ALU.mult, ALU.add, sem=ctx.dv,
                                after=(mm, sc_tok, res_toks[a]) +
                                      ((res_out_tok.get(rkey),) if res_out_tok.get(rkey) else ()))
                    dslot[1] = tacc
                    res_out_tok[rkey] = tacc
            wd_s[1] = dn_last
        act_war = dn_last
        db_war = dn_last
        for tile in wpool.tiles:
            tile[1] = dn_last

    for a in range(8):
        toks = tuple(res_out_tok[(a, nd)] for nd in range(4))
        p.op("sync", "dma_start", sem=ctx.st, n=16, out=o_out[a * 128:(a + 1) * 128, :],
             in_=resid[:, a * D:(a + 1) * D], after=toks)
    ctx.al.end_stage()
    return ctx


def run_l1moe(g, r3_cores):
    ctx = build_l1moe()
    nc = ctx.finish()
    in_maps = []
    for i in range(NCORES):
        m = {"rwT": g["rwT"], "rbrow": g["rbrow"], "gw": g["gw"], "uw": g["uw"],
             "gbcol": g["gbcol"], "ubcol": g["ubcol"], "dw": g["dw"], "dbrow": g["dbrow"],
             "ident": g["ident"], "resid3": r3_cores[i]}
        in_maps.append(m)
    res = run_bass_kernel_spmd(nc, in_maps, core_ids=list(range(NCORES)))
    return res.results


def kernel(**inputs):
    return kernel_mega(**inputs)


def kernel_unfused(**inputs):
    g = host_prep(inputs)
    a_res = run_l0pre(g)
    b_res = run_l0attn(g, a_res)
    r1 = [b_res[i]["resid1"] for i in range(NCORES)]
    c_res = run_l0ffn(g, r1)
    r2 = [c_res[i]["resid2"] for i in range(NCORES)]
    d_res = run_l1pre(g, r2)
    e_res = run_l1attn(g, r2, d_res)
    r3 = [e_res[i]["resid3"] for i in range(NCORES)]
    f_res = run_l1moe(g, r3)
    out = np.concatenate([f_res[i]["out_sl"] for i in range(NCORES)], axis=0)
    return out.reshape(1, SEQ, D).astype(np.float32)


def build_mega(debug=False):
    ctx = Ctx("mega")
    io = {}
    dt = ctx.dint
    io["Qn0"] = dt("i_Qn0", [2048, T], BF16)
    io["Qr0"] = dt("i_Qr0", [1024, T], BF16)
    io["Kn0"] = dt("i_Kn0", [2048, T], BF16)
    io["Kr0"] = dt("i_Kr0", [1024, T], BF16)
    io["V0"] = dt("i_V0", [T, 2048], BF16)
    io["qn20"] = dt("i_qn20", [16, T], F32)
    io["kn20"] = dt("i_kn20", [16, T], F32)
    io["KnAG"] = dt("i_KnAG", [8 * 2048, T], BF16)
    io["KrAG"] = dt("i_KrAG", [8 * 1024, T], BF16)
    io["VAG"] = dt("i_VAG", [8 * T, 2048], BF16)
    io["kn2AG"] = dt("i_kn2AG", [8 * 16, T], F32)
    io["r1"] = dt("i_r1", [T, D], F32)
    io["r2"] = dt("i_r2", [T, D], F32)
    io["r3"] = dt("i_r3", [T, D], F32)
    io["Q1loc"] = dt("i_Q1loc", [2048, T], BF16)
    io["K1loc"] = dt("i_K1loc", [2048, T], BF16)
    io["V1loc"] = dt("i_V1loc", [T, 2048], BF16)
    io["qn21loc"] = dt("i_qn21loc", [16, T], F32)
    io["kn21loc"] = dt("i_kn21loc", [16, T], F32)
    io["K1AG"] = dt("i_K1AG", [8 * 2048, T], BF16)
    io["V1AG"] = dt("i_V1AG", [8 * T, 2048], BF16)
    io["kn21AG"] = dt("i_kn21AG", [8 * 16, T], F32)
    io["K1nb"] = dt("i_K1nb", [2048, T], BF16)
    io["V1nb"] = dt("i_V1nb", [T, 2048], BF16)
    io["kn21nb"] = dt("i_kn21nb", [16, T], F32)
    io["mscrB"] = dt("i_mscrB", [16, 2], F32)
    io["mscrE"] = dt("i_mscrE", [16, 2], F32)
    for bn in ("bar1", "bar1o", "bar2", "bar2o", "bar3", "bar3o"):
        io[bn] = dt("i_" + bn, [128, 16], F32)
    mega = {"ctx": ctx, "io": io}
    p = ctx.p

    build_l0pre(mega=mega)
    groups = [list(range(NCORES))]
    build_l0attn(mega=mega)
    build_l0ffn(mega=mega)
    build_l1pre(mega=mega)
    build_l1attn(mega=mega)
    build_l1moe(mega=mega)
    if debug:
        ctx.barrier()
        p = ctx.p
        for nm, shape, dt_ in (("r1", [T, D], F32), ("r2", [T, D], F32), ("r3", [T, D], F32),
                               ("Qn0", [2048, T], BF16), ("Kn0", [2048, T], BF16),
                               ("qn20", [16, T], F32), ("kn20", [16, T], F32),
                               ("kn2AG", [128, T], F32), ("K1loc", [2048, T], BF16),
                               ("K1nb", [2048, T], BF16), ("V1nb", [T, 2048], BF16),
                               ("qn21loc", [16, T], F32)):
            o = ctx.dout("dbg_" + nm, shape, dt_)
            sl = tuple(slice(None) for _ in shape)
            p.op("sync", "dma_start", sem=ctx.st, n=16, out=o[sl], in_=io[nm][sl])
        kslice = ctx.dout("dbg_KnAG", [2048, T], BF16)
        p.op("sync", "dma_start", sem=ctx.st, n=16, out=kslice[:, :], in_=io["KnAG"][2 * 2048:3 * 2048, :])
        for rr in range(8):
            vsl = ctx.dout("dbg_V1AG%d" % rr, [T, 2048], BF16)
            p.op("sync", "dma_start", sem=ctx.st, n=16, out=vsl[:, :],
                 in_=io["V1AG"][rr * T:(rr + 1) * T, :])
    return ctx


def mega_in_map(g, i):
    km = np.zeros((128, 2 * 15), np.float32)
    for q in range(2):
        dq = 2 * i + q
        km[:, q * 15 + dq:(q + 1) * 15] = NEG
    km1 = np.zeros((128, 6), np.float32)
    if i == 0:
        km1[:, 0] = NEG
        km1[:, 1] = NEG
        km1[:, 3] = NEG
    m = {
        "x_sl": g["x"][i * T:(i + 1) * T],
        "w_cq": g["w_cq"], "w_dq_nope": g["w_dq_nope"], "w_dq_rope_p": g["w_dq_rope_p"],
        "w_ckv": g["w_ckv"], "w_dk_nope": g["w_dk_nope"], "w_dv": g["w_dv"],
        "w_k_rope_p": g["w_k_rope_p"], "qg": g["qg"], "kvg": g["kvg"],
        "crow": g["crow"], "srow": g["srow"], "ident": g["ident"], "i64": g["i64"],
        "mswap": g["mswap"], "keymask": km, "maskC": g["maskC"], "maskW": g["maskW"],
        "w_proj": g["w_proj"], "W0T": g["W0T"], "b0col": g["b0col"], "W1T": g["W1T"],
        "b1row": g["b1row"], "Wqkv": g["Wqkv"], "qg1": g["qg1"], "kg1": g["kg1"],
        "pswap": g["pswap"],
        "cosT_sl": np.ascontiguousarray(_bf(g["cosT"])[:, i * T:(i + 1) * T]),
        "sinT_sl": np.ascontiguousarray(_bf(g["sinT"])[:, i * T:(i + 1) * T]),
        "keymask1": km1, "Wo": g["Wo"], "rwT": g["rwT"], "rbrow": g["rbrow"],
        "gw": g["gw"], "uw": g["uw"], "gbcol": g["gbcol"], "ubcol": g["ubcol"],
        "dw": g["dw"], "dbrow": g["dbrow"],
    }
    return m


def kernel_mega(**inputs):
    g = host_prep(inputs)
    ctx = build_mega()
    nc = ctx.finish()
    in_maps = [mega_in_map(g, i) for i in range(NCORES)]
    res = run_bass_kernel_spmd(nc, in_maps, core_ids=list(range(NCORES)))
    out = np.concatenate([res.results[i]["out_sl"] for i in range(NCORES)], axis=0)
    return out.reshape(1, SEQ, D).astype(np.float32)


# version 09
# speedup vs baseline: 1.0672x; 1.0672x over previous
import numpy as np
import ml_dtypes
from contextlib import ExitStack

import concourse.bass as bass
import concourse.mybir as mybir
from concourse.bass_utils import run_bass_kernel_spmd

F32 = mybir.dt.float32
BF16 = mybir.dt.bfloat16
AF = mybir.ActivationFunctionType
ALU = mybir.AluOpType
AX = mybir.AxisListType
BF = ml_dtypes.bfloat16

NCORES = 8
T = 1024
SEQ = 8192
D = 2048
NEG = -1.0e9

ENGINES = ("sync", "tensor", "vector", "scalar", "gpsimd")



class Prog:

    def __init__(self, nc):
        self.nc = nc
        self.q = {e: [] for e in ENGINES}
        self.semvals = {}
        self.waited = {e: {} for e in ENGINES}

    def _key(self, sem):
        return sem.name if hasattr(sem, "name") else sem

    def inc(self, sem, n=1):
        k = self._key(sem)
        self.semvals[k] = self.semvals.get(k, 0) + n
        return self.semvals[k]

    def val(self, sem):
        return self.semvals.get(self._key(sem), 0)

    def tok(self, sem):
        return (sem, self.val(sem))

    def emit(self, eng, fn):
        self.q[eng].append(fn)

    def wait(self, eng, sem, v=None):
        if v is None:
            v = self.val(sem)
        if v <= 0:
            return
        k = self._key(sem)
        if self.waited[eng].get(k, 0) >= v:
            return
        self.waited[eng][k] = v
        self.emit(eng, lambda e, s=sem, vv=v: e.wait_ge(s, vv))

    def wait_tok(self, eng, tok):
        if tok is None:
            return
        if isinstance(tok[0], tuple):
            for t in tok:
                if t is not None:
                    self.wait(eng, t[0], t[1])
            return
        self.wait(eng, tok[0], tok[1])

    def op(self, eng, method, *args, sem=None, n=1, after=(), **kwargs):
        for t in after:
            self.wait_tok(eng, t)
        v = self.inc(sem, n) if sem is not None else None

        def f(e, m=method, a=args, kw=kwargs, s=sem, nn=n):
            inst = getattr(e, m)(*a, **kw)
            if s is not None:
                inst.then_inc(s, nn)
        self.emit(eng, f)
        return (sem, v) if sem is not None else None

    def finalize(self, block):
        for name in ENGINES:
            fns = self.q[name]

            def runner(eng, fns=fns):
                for fn in fns:
                    fn(eng)
            getattr(block, name)(runner)


class Alloc:
    def __init__(self, nc):
        self.nc = nc
        self.stack = ExitStack()
        self.stage_stack = None
        self.prefix = ""

    def _st(self):
        return self.stage_stack if self.stage_stack is not None else self.stack

    def sb(self, name, shape, dtype):
        return self._st().enter_context(self.nc.sbuf_tensor(self.prefix + name, shape, dtype))

    def ps(self, name, shape, dtype=F32):
        return self._st().enter_context(self.nc.psum_tensor(self.prefix + name, shape, dtype))

    def sem(self, name):
        return self.stack.enter_context(self.nc.semaphore(name))

    def block(self):
        return self.stack.enter_context(self.nc.Block())

    def begin_stage(self, prefix):
        assert self.stage_stack is None
        self.stage_stack = ExitStack()
        self.prefix = prefix + "_"

    def end_stage(self):
        self.stage_stack.close()
        self.stage_stack = None
        self.prefix = ""

    def close(self):
        if self.stage_stack is not None:
            self.end_stage()
        self.stack.close()


class Ctx:
    def __init__(self, name):
        self.nc = bass.Bass("TRN2", target_bir_lowering=False)
        self.al = Alloc(self.nc)
        self.p = Prog(self.nc)
        self.name = name
        self.ins = {}
        self.outs = {}
        self.ld = self.al.sem("s_ld")
        self.ls = self.al.sem("s_ls")
        self.pe = self.al.sem("s_pe")
        self.dv = self.al.sem("s_dv")
        self.sc = self.al.sem("s_sc")
        self.gp = self.al.sem("s_gp")
        self.st = self.al.sem("s_st")
        self.cc = self.al.sem("s_cc")

    def din(self, name, shape, dtype=F32):
        if name in self.ins:
            return self.ins[name]
        ap = self.nc.dram_tensor(name, list(shape), dtype, kind="ExternalInput").ap()
        self.ins[name] = ap
        return ap

    def dint(self, name, shape, dtype=F32, addr_space="Local"):
        ap = self.nc.dram_tensor(name, list(shape), dtype, addr_space=addr_space).ap()
        return ap

    def barrier(self):
        p = self.p
        sems = [self.ld, self.ls, self.pe, self.dv, self.sc, self.gp, self.st]
        for eng in ENGINES:
            for sm in sems:
                p.wait(eng, sm)

    def dout(self, name, shape, dtype=F32):
        ap = self.nc.dram_tensor(name, list(shape), dtype, kind="ExternalOutput").ap()
        self.outs[name] = ap
        return ap

    def common(self):
        _st, _pf = self.al.stage_stack, self.al.prefix
        self.al.stage_stack, self.al.prefix = None, ""
        self.ident = self.al.sb("c_ident", [128, 128], BF16)
        self.ones = self.al.sb("c_ones", [128, 128], BF16)
        self.ones32 = self.al.sb("c_ones32", [1, 128], F32)
        self.al.stage_stack, self.al.prefix = _st, _pf
        self.p.op("vector", "memset", self.ones[:, :], 1.0, sem=self.dv)
        self.p.op("vector", "memset", self.ones32[:, :], 1.0, sem=self.dv)
        self.p.op("sync", "dma_start", sem=self.ls, n=16, out=self.ident[:, :],
                  in_=self.ins["ident"][:, :])

    def finish(self):
        self.p.wait("sync", self.st)
        blk = self.al.block()
        self.p.finalize(blk)
        self.al.close()
        return self.nc


class SlotPool:

    def __init__(self, tiles):
        self.tiles = [[t, None] for t in tiles]
        self.i = 0

    def take(self):
        t = self.tiles[self.i % len(self.tiles)]
        self.i += 1
        return t



def load_weight(ctx, wsb, col0, dram, fan_in, fan_out, after=()):
    nk = fan_in // 128
    return ctx.p.op("gpsimd", "dma_start", sem=ctx.ld, n=16, after=after,
                    out=wsb[:, col0:col0 + nk * fan_out].rearrange("p (k m) -> p k m", k=nk),
                    in_=dram.rearrange("(k p) m -> p k m", p=128))


def wslice(wsb, col0, fan_out, k, m, mw=128):
    return wsb[:, col0 + k * fan_out + m * 128:col0 + k * fan_out + m * 128 + mw]


def emit_ln(ctx, x_tile_ap, lnx_tm_ap, scr_sq, stats, toks, eps=1e-5):
    p = ctx.p
    prev = getattr(ctx, "ln_prev", ())
    t_sq = p.op("scalar", "activation", scr_sq[:, :], x_tile_ap, AF.Square,
                accum_out=stats["ssq"][:, :], sem=ctx.sc, after=tuple(toks) + tuple(prev))
    t_sm = p.op("vector", "reduce_sum", stats["smean"][:, :], x_tile_ap, axis=AX.X,
                sem=ctx.dv, after=tuple(toks) + tuple(prev))
    t_m = p.op("vector", "tensor_scalar_mul", stats["m"][:, :], stats["smean"][:, :],
               1.0 / D, sem=ctx.dv, after=(t_sm,))
    t_msq = p.op("vector", "tensor_mul", stats["var"][:, :], stats["m"][:, :],
                 stats["m"][:, :], sem=ctx.dv, after=(t_m,))
    t_v = p.op("vector", "scalar_tensor_tensor", stats["var"][:, :], stats["ssq"][:, :],
               1.0 / D, stats["var"][:, :], ALU.mult, ALU.subtract, sem=ctx.dv,
               after=(t_msq, t_sq))
    t_e = p.op("vector", "tensor_scalar_add", stats["var"][:, :], stats["var"][:, :],
               eps, sem=ctx.dv, after=(t_v,))
    t_s = p.op("scalar", "activation", stats["rstd"][:, :], stats["var"][:, :], AF.Sqrt,
               sem=ctx.sc, after=(t_e,))
    t_r = p.op("vector", "reciprocal", stats["rstd"][:, :], stats["rstd"][:, :],
               sem=ctx.dv, after=(t_s,))
    tn = p.op("vector", "tensor_scalar", lnx_tm_ap, x_tile_ap, stats["m"][:, :],
              stats["rstd"][:, :], ALU.subtract, ALU.mult, sem=ctx.dv, after=(t_r,))
    ctx.ln_prev = (tn, t_sq, t_s)
    return tn


def transpose_group(ctx, srcs, dst_ap, ps_slot):
    p = ctx.p
    ps, war = ps_slot
    n = len(srcs)
    if war is not None:
        p.wait_tok("tensor", war)
    tok = None
    for j, s in enumerate(srcs):
        tok = p.op("tensor", "transpose", ps[:, j * 128:(j + 1) * 128], s,
                   ctx.ident[:, :], sem=(ctx.pe if j == n - 1 else None))
    ev = p.op("vector", "tensor_copy", dst_ap, ps[:, 0:n * 128], sem=ctx.dv, after=(tok,))
    ps_slot[1] = ev
    return ev



def _bf(x):
    return np.ascontiguousarray(x).astype(BF)


def _perm_ri(n_pairs):
    return np.concatenate([np.arange(0, 2 * n_pairs, 2), np.arange(1, 2 * n_pairs, 2)])


def host_prep(inp):
    g = {}
    x = np.ascontiguousarray(inp["L0_attn_x"].reshape(SEQ, D))
    g["x"] = x
    g["w_cq"] = np.ascontiguousarray(inp["L0_attn_w_cq"])
    g["w_dq_nope"] = np.ascontiguousarray(inp["L0_attn_w_dq_nope"])
    p64 = _perm_ri(32)
    wqr = inp["L0_attn_w_dq_rope"]
    g["w_dq_rope_p"] = np.ascontiguousarray(
        wqr.reshape(1536, 16, 64)[:, :, p64].reshape(1536, 1024))
    g["w_ckv"] = np.ascontiguousarray(inp["L0_attn_w_ckv"])
    g["w_dk_nope"] = np.ascontiguousarray(inp["L0_attn_w_dk_nope"])
    g["w_dv"] = np.ascontiguousarray(inp["L0_attn_w_dv"])
    g["w_k_rope_p"] = np.ascontiguousarray(inp["L0_attn_w_k_rope"][:, p64])
    g["w_proj"] = np.ascontiguousarray(inp["L0_attn_w_proj"])
    g["qg"] = np.ascontiguousarray(inp["L0_attn_q_norm_w"].reshape(12, 128).T)
    g["kvg"] = np.ascontiguousarray(inp["L0_attn_kv_norm_w"].reshape(4, 128).T)
    g["crow"] = np.ascontiguousarray(np.tile(inp["L0_attn_freqs_cos"], (1, 2)).T)
    g["srow"] = np.ascontiguousarray(np.tile(inp["L0_attn_freqs_sin"], (1, 2)).T)
    g["ident"] = _bf(np.eye(128, dtype=np.float32))
    psw = np.zeros((128, 128), np.float32)
    for j in range(64):
        psw[j, j + 64] = 1.0
        psw[j + 64, j] = 1.0
    g["pswap"] = _bf(psw)
    g["i64"] = _bf(np.eye(64, dtype=np.float32))
    msw = np.zeros((64, 64), np.float32)
    for j in range(32):
        msw[j, j + 32] = 1.0
        msw[j + 32, j] = -1.0
    g["mswap"] = _bf(msw)
    g["W0T"] = np.ascontiguousarray(inp["L0_ffn_W0"].T)
    g["b0col"] = np.ascontiguousarray(inp["L0_ffn_bias0"].reshape(64, 128).T)
    g["W1T"] = np.ascontiguousarray(inp["L0_ffn_W1"].T)
    g["b1row"] = np.ascontiguousarray(inp["L0_ffn_bias1"].reshape(1, D))
    g["Wqkv"] = np.ascontiguousarray(inp["L1_attn_Wqkv"])
    g["qg1"] = np.ascontiguousarray(inp["L1_attn_q_norm_w"].reshape(16, 128).T)
    g["kg1"] = np.ascontiguousarray(inp["L1_attn_k_norm_w"].reshape(16, 128).T)
    g["Wo"] = np.ascontiguousarray(inp["L1_attn_Wo"])
    hd = 128
    inv = 1.0 / (10000.0 ** (np.arange(0, hd, 2, dtype=np.float32) / hd))
    pos = np.arange(SEQ, dtype=np.float32)
    fr = pos[:, None] * inv[None, :]
    ct = np.cos(fr).astype(np.float32).T
    st = np.sin(fr).astype(np.float32).T
    g["cosT"] = np.ascontiguousarray(np.concatenate([ct, ct], 0))
    g["sinT"] = np.ascontiguousarray(np.concatenate([-st, st], 0))
    g["rwT"] = np.ascontiguousarray(inp["L1_ffn_router_weight"].T)
    g["rbrow"] = np.ascontiguousarray(inp["L1_ffn_router_bias"].reshape(1, 8))
    gu = inp["L1_ffn_gate_up_proj"]
    g["gw"] = np.ascontiguousarray(gu[:, :, 0::2])
    g["uw"] = np.ascontiguousarray(gu[:, :, 1::2])
    gub = inp["L1_ffn_gate_up_proj_bias"]
    g["gbcol"] = np.ascontiguousarray(gub[:, 0::2].reshape(8, 16, 128).transpose(0, 2, 1))
    g["ubcol"] = np.ascontiguousarray(gub[:, 1::2].reshape(8, 16, 128).transpose(0, 2, 1))
    g["dw"] = np.ascontiguousarray(inp["L1_ffn_down_proj"])
    g["dbrow"] = np.ascontiguousarray(inp["L1_ffn_down_proj_bias"])
    kk = np.arange(512)[:, None]
    qq = np.arange(512)[None, :]
    mC = np.where(qq >= kk, 0.0, NEG).astype(np.float32)
    mW = np.where(qq < kk, 0.0, NEG).astype(np.float32)
    g["maskC"] = np.ascontiguousarray(mC.reshape(4, 128, 512).transpose(1, 0, 2).reshape(128, 2048))
    g["maskW"] = np.ascontiguousarray(mW.reshape(4, 128, 512).transpose(1, 0, 2).reshape(128, 2048))
    return g




def rmsnorm_post2(ctx, ps_st_region, t, strow, r128, bslot, fan_in, eps, stat_tok):
    p = ctx.p
    c0 = t * 512
    ps_bc, bwar = bslot
    t1 = p.op("vector", "tensor_scalar", strow[:, c0:c0 + 512], ps_st_region,
              1.0 / fan_in, eps, ALU.mult, ALU.add, sem=ctx.dv,
              after=(stat_tok,) + ((ctx.strowg.get(t),) if ctx.strowg.get(t) else ()))
    ctx.psg = t1
    t2 = p.op("scalar", "activation", strow[:, c0:c0 + 512], strow[:, c0:c0 + 512],
              AF.Sqrt, sem=ctx.sc, after=(t1,))
    t3 = p.op("vector", "reciprocal", strow[:, c0:c0 + 512], strow[:, c0:c0 + 512],
              sem=ctx.dv, after=(t2,))
    if bwar is not None:
        p.wait_tok("tensor", bwar)
    t4 = p.op("tensor", "matmul", ps_bc[:, :], ctx.ones32[0:1, 0:128],
              strow[0:1, c0:c0 + 512], start=True, stop=True, sem=ctx.pe, after=(t3,))
    ctx.strowg[t] = t4
    t5 = p.op("vector", "tensor_copy", r128[:, c0:c0 + 512], ps_bc[:, :],
              sem=ctx.dv, after=(t4,))
    bslot[1] = t5
    return t5


def emit_qk_heads(ctx, t, wsb, wcol, nk, rhs_fn, rope_rhs_fn, rope_lhs_fn, o_n, o_r, o_n2,
                  pool, epool, erpool, rpool, sqpool, sq64p, ps_st, str2, rot_tok,
                  rope_src_tok):
    p = ctx.p
    pending = [None]
    for h in range(16):
        slot = pool.take()
        ps, war = slot
        if war is not None:
            p.wait_tok("tensor", war)
        mm = None
        for k in range(nk):
            mm = p.op("tensor", "matmul", ps[:, :], wslice(wsb, wcol, 2048, k, h),
                      rhs_fn(k, t), start=(k == 0), stop=(k == nk - 1),
                      sem=(ctx.pe if k == nk - 1 else None))
        if pending[0] is not None:
            pending[0]()
            pending[0] = None
        es = epool.take()
        ev, ewar = es
        tev = p.op("scalar", "activation", ev[:, :], ps[:, :], AF.Identity, sem=ctx.sc,
                   after=(mm,) + ((ewar,) if ewar is not None else ()))
        es[1] = p.op("sync", "dma_start", sem=ctx.st, n=16,
                     out=o_n[h * 128:(h + 1) * 128, t * 512:(t + 1) * 512],
                     in_=ev[:, :], after=(tev,))
        sqslot = sqpool.take()
        sqa, sqwar = sqslot
        tsq1 = p.op("scalar", "activation", sqa, ps[:, :], AF.Square, sem=ctx.sc,
                    after=(mm,) + ((sqwar,) if sqwar is not None else ()))
        slot[1] = tsq1
        rslot = rpool.take()
        rps, rwar = rslot
        if rwar is not None:
            p.wait_tok("tensor", rwar)
        p.wait_tok("tensor", rot_tok)
        p.wait_tok("tensor", rope_src_tok)
        rmm = p.op("tensor", "matmul", rps, rope_lhs_fn(h),
                   rope_rhs_fn(h, t), start=True, stop=True, sem=ctx.pe)
        ers = erpool.take()
        er, erwar = ers
        trev = p.op("scalar", "activation", er, rps, AF.Identity, sem=ctx.sc,
                    after=(rmm,) + ((erwar,) if erwar is not None else ()))
        ers[1] = p.op("sync", "dma_start", sem=ctx.st, n=16,
                      out=o_r[h * 64:(h + 1) * 64, t * 512:(t + 1) * 512],
                      in_=er, after=(trev,))
        s64 = sq64p.take()
        sqb, sqbwar = s64
        tsq2 = p.op("scalar", "activation", sqb, er, AF.Square, sem=ctx.sc,
                    after=(trev,) + ((sqbwar,) if sqbwar is not None else ()))
        rslot[1] = trev

        def mk_stats(h=h, tsq1=tsq1, tsq2=tsq2, sqa=sqa, sqb=sqb,
                     sqslot=sqslot, s64=s64):
            def emit():
                ts1 = p.op("tensor", "matmul", ps_st[0:1, :], ctx.ones[:, 0:1], sqa,
                           start=True, stop=False, sem=ctx.pe,
                           after=(tsq1,) + ((ctx.psg,) if ctx.psg else ()))
                sqslot[1] = ts1
                ts2 = p.op("tensor", "matmul", ps_st[0:1, :], ctx.ones[0:64, 0:1], sqb,
                           start=False, stop=True, sem=ctx.pe, after=(tsq2,))
                s64[1] = ts2
                tse = p.op("vector", "tensor_copy", str2[0:1, :], ps_st[0:1, :], sem=ctx.dv,
                           after=(ts2,) + ((ctx.str2g,) if ctx.str2g else ()))
                ctx.psg = tse
                ctx.str2g = p.op("sync", "dma_start", sem=ctx.st, n=16,
                                 out=o_n2[h:h + 1, t * 512:(t + 1) * 512],
                                 in_=str2[0:1, :], after=(tse,))
            return emit
        pending[0] = mk_stats()
    pending[0]()


def build_l0pre(mega=None):
    ctx = mega["ctx"] if mega else Ctx("l0pre")
    ctx.al.begin_stage("a")
    ctx.psg = None
    ctx.str2g = None
    ctx.strowg = {0: None, 1: None}
    p, al = ctx.p, ctx.al
    x = ctx.din("x_sl", [T, D])
    w_cq_d = ctx.din("w_cq", [D, 1536])
    w_dqn_d = ctx.din("w_dq_nope", [1536, 2048])
    w_dqr_d = ctx.din("w_dq_rope_p", [1536, 1024])
    w_ckv_d = ctx.din("w_ckv", [D, 512])
    w_dkn_d = ctx.din("w_dk_nope", [512, 2048])
    w_dv_d = ctx.din("w_dv", [512, 2048])
    w_krope_d = ctx.din("w_k_rope_p", [D, 64])
    qg_in = ctx.din("qg", [128, 12])
    kvg_in = ctx.din("kvg", [128, 4])
    crow_in = ctx.din("crow", [64, 16])
    srow_in = ctx.din("srow", [64, 16])
    ctx.din("ident", [128, 128], BF16)
    i64_in = ctx.din("i64", [64, 64], BF16)
    mswap_in = ctx.din("mswap", [64, 64], BF16)

    if mega:
        io = mega["io"]
        o_qn, o_qr, o_kn, o_kr = io["Qn0"], io["Qr0"], io["Kn0"], io["Kr0"]
        o_v, o_qn2, o_kn2 = io["V0"], io["qn20"], io["kn20"]
    else:
        o_qn = ctx.dout("Qn", [2048, T], BF16)
        o_qr = ctx.dout("Qr", [1024, T], BF16)
        o_kn = ctx.dout("Kn", [2048, T], BF16)
        o_kr = ctx.dout("Kr", [1024, T], BF16)
        o_v = ctx.dout("V", [T, 2048], BF16)
        o_qn2 = ctx.dout("qn2", [16, T])
        o_kn2 = ctx.dout("kn2", [16, T])

    if not hasattr(ctx, "ident"):
        ctx.common()
    WA, WB = 0, 24576
    R_CKV, R_KR, R_DKN, R_DV = 0, 8192, 9216, 24576
    wsb = al.sb("wsb", [128, WB + 12288], BF16)
    lnx = al.sb("lnx", [128, 16 * T], BF16)
    lnx_tm = al.sb("lnx_tm", [128, D], BF16)
    xt = al.sb("xt", [128, D], F32)
    nq = al.sb("nq", [128, 12 * T], BF16)
    nkv = al.sb("nkv", [128, 4 * T], BF16)
    base = al.sb("base", [64, T], BF16)
    rot = al.sb("rot", [128, 16 * 64], BF16)
    big = al.sb("big", [128, 8 * T], BF16)
    ev4 = al.sb("ev4", [128, 4 * 512], BF16)
    er2 = al.sb("er2", [64, 2 * 512], BF16)
    sqs = al.sb("sqs", [128, 2 * 512], BF16)
    sq64s = al.sb("sq64s", [64, 2 * 512], BF16)
    vtm = al.sb("vtm", [128, 2 * 2048], BF16)
    strow = al.sb("strow", [1, 2 * 512], F32)
    str2 = al.sb("str2", [1, 512], F32)
    i64 = al.sb("sb_i64", [64, 64], BF16)
    mswap = al.sb("sb_mswap", [64, 64], BF16)
    crow = al.sb("sb_crow", [64, 16], F32)
    srow = al.sb("sb_srow", [64, 16], F32)
    qg = al.sb("sb_qg", [128, 12], F32)
    kvg = al.sb("sb_kvg", [128, 4], F32)
    scr_sq = al.sb("scr_sq", [128, D], BF16)
    stats = {k: al.sb("st_" + k, [128, 1], F32) for k in ["ssq", "smean", "m", "var", "rstd"]}
    r128 = al.sb("r128", [128, T], F32)
    ps_a = al.ps("ps_a", [128, 512])
    ps_b = al.ps("ps_b", [128, 512])
    ps_tr = al.ps("ps_tr", [128, 2048], BF16)
    ps_st = al.ps("ps_st", [1, 512])
    ps_bc = al.ps("ps_bc", [128, 512])
    ps_r = al.ps("ps_r", [64, 1024])
    pool = SlotPool([ps_a, ps_b])
    epool = SlotPool([ev4[:, i * 512:(i + 1) * 512] for i in range(4)])
    erpool = SlotPool([er2[:, 0:512], er2[:, 512:1024]])
    trpool = SlotPool([ps_tr[:, 0:512], ps_tr[:, 1024:1536]])
    rpool = SlotPool([ps_r[:, 0:512], ps_r[:, 512:1024]])
    sqpool = SlotPool([sqs[:, 0:512], sqs[:, 512:1024]])
    sq64p = SlotPool([sq64s[:, 0:512], sq64s[:, 512:1024]])
    bcslot = SlotPool([ps_bc])

    for dst, src in ((i64, i64_in), (mswap, mswap_in), (crow, crow_in), (srow, srow_in),
                     (qg, qg_in), (kvg, kvg_in)):
        p.op("sync", "dma_start", sem=ctx.ls, n=16, out=dst[:, :], in_=src[:, :])
    p.wait("vector", ctx.ls)
    tsrow = p.tok(ctx.dv)
    rot_tok = None
    for h in range(16):
        t1 = p.op("vector", "tensor_scalar_mul", rot[0:64, h * 64:(h + 1) * 64], i64[:, :],
                  crow[:, h:h + 1], sem=ctx.dv, after=(tsrow,))
        rot_tok = p.op("vector", "scalar_tensor_tensor", rot[0:64, h * 64:(h + 1) * 64],
                       mswap[:, :], srow[:, h:h + 1], rot[0:64, h * 64:(h + 1) * 64],
                       ALU.mult, ALU.add, sem=ctx.dv, after=(t1,))
    rot_tok = p.op("sync", "dma_start", sem=ctx.ls, n=16, out=rot[64:128, :],
                   in_=rot[0:64, :], after=(rot_tok,))

    tw_ckv = load_weight(ctx, wsb, R_CKV, w_ckv_d, D, 512)
    tw_kr = load_weight(ctx, wsb, R_KR, w_krope_d, D, 64)
    tw_dkn = load_weight(ctx, wsb, R_DKN, w_dkn_d, 512, 2048)
    tw_dv = load_weight(ctx, wsb, R_DV, w_dv_d, 512, 2048)
    xt_war = [None, None]
    tm_war = [None, None]
    lnx_ready = None
    for a in range(8):
        sl = 0
        tx = p.op("sync", "dma_start", sem=ctx.ls, n=16, out=xt[:, :],
                  in_=x[a * 128:(a + 1) * 128, :],
                  after=((xt_war[sl],) if xt_war[sl] is not None else ()))
        tn = emit_ln(ctx, xt[:, :], lnx_tm[:, :],
                     scr_sq, stats, (tx,) + ((tm_war[sl],) if tm_war[sl] is not None else ()))
        xt_war[sl] = tn
        p.wait_tok("tensor", tn)
        for g4 in range(4):
            srcs = [lnx_tm[:, (g4 * 4 + j) * 128:(g4 * 4 + j) * 128 + 128]
                    for j in range(4)]
            dst = lnx[:, :].rearrange("p (k t) -> p k t", k=16)[:, g4 * 4:(g4 + 1) * 4,
                                                               a * 128:(a + 1) * 128]
            lnx_ready = transpose_group(ctx, srcs, dst, trpool.take())
        tm_war[sl] = p.tok(ctx.pe)

    def lnx_rhs(k, t):
        ctx.p.wait_tok("tensor", lnx_ready)
        return lnx[:, k * T + t * 512:k * T + (t + 1) * 512]

    def rms_proj(w_col, fan_out, nkk, rhs_fn, dst, dst_nk, gain, wtok, fan_in_true):
        mods = []
        for t in range(2):
            stat_tok = None
            pending = None
            for m in range(dst_nk):
                slot = pool.take()
                ps, war = slot
                p.wait_tok("tensor", wtok)
                if war is not None:
                    p.wait_tok("tensor", war)
                mm = None
                for k in range(nkk):
                    mm = p.op("tensor", "matmul", ps[:, :], wslice(wsb, w_col, fan_out, k, m),
                              rhs_fn(k, t), start=(k == 0), stop=(k == nkk - 1),
                              sem=(ctx.pe if k == nkk - 1 else None))
                if pending is not None:
                    stat_tok = pending()
                tev = p.op("vector", "tensor_copy", dst[:, m * T + t * 512:m * T + (t + 1) * 512],
                           ps[:, :], sem=ctx.dv, after=(mm,))
                sqslot = sqpool.take()
                sqa, sqwar = sqslot
                tsq = p.op("scalar", "activation", sqa, ps[:, :], AF.Square, sem=ctx.sc,
                           after=(mm, tev) + ((sqwar,) if sqwar is not None else ()))
                slot[1] = (tev, tsq)

                def mk_stat(m=m, tsq=tsq, sqa=sqa, sqslot=sqslot):
                    def emit():
                        tk = p.op("tensor", "matmul", ps_st[0:1, :], ctx.ones[:, 0:1], sqa,
                                  start=(m == 0), stop=(m == dst_nk - 1), sem=ctx.pe,
                                  after=(tsq,) + ((ctx.psg,) if (m == 0 and ctx.psg) else ()))
                        sqslot[1] = tk
                        return tk
                    return emit
                pending = mk_stat()
            stat_tok = pending()
            tr = rmsnorm_post2(ctx, ps_st[0:1, :], t, strow, r128, bcslot.take(),
                               fan_in_true, 1e-5, stat_tok)
            for m in range(dst_nk):
                tg1 = p.op("vector", "tensor_scalar_mul",
                           dst[:, m * T + t * 512:m * T + (t + 1) * 512],
                           dst[:, m * T + t * 512:m * T + (t + 1) * 512],
                           gain[:, m:m + 1], sem=ctx.dv, after=(tr,))
                tg2 = p.op("vector", "tensor_mul",
                           dst[:, m * T + t * 512:m * T + (t + 1) * 512],
                           dst[:, m * T + t * 512:m * T + (t + 1) * 512],
                           r128[:, t * 512:(t + 1) * 512], sem=ctx.dv, after=(tg1,))
                mods.append(tg2)
        return mods

    nkv_mods = rms_proj(R_CKV, 512, 16, lnx_rhs, nkv, 4, kvg, tw_ckv, 512)

    def nkv_rhs(k, t):
        for tk in nkv_mods:
            ctx.p.wait_tok("tensor", tk)
        return nkv[:, k * T + t * 512:k * T + (t + 1) * 512]

    base_tok = None
    for t in range(2):
        slot = pool.take()
        ps, war = slot
        p.wait_tok("tensor", tw_kr)
        if war is not None:
            p.wait_tok("tensor", war)
        mm = None
        for k in range(16):
            mm = p.op("tensor", "matmul", ps[0:64, :], wslice(wsb, R_KR, 64, k, 0, 64),
                      lnx_rhs(k, t), start=(k == 0), stop=(k == 15),
                      sem=(ctx.pe if k == 15 else None))
        base_tok = p.op("scalar", "activation", base[:, t * 512:(t + 1) * 512], ps[0:64, :],
                        AF.Identity, sem=ctx.sc, after=(mm,))
        slot[1] = base_tok


    def k_rope_rhs(h, t):
        return base[:, t * 512:(t + 1) * 512]

    for t in range(2):
        p.wait_tok("tensor", tw_dkn)
        emit_qk_heads(ctx, t, wsb, R_DKN, 4, nkv_rhs, k_rope_rhs,
                      lambda h: rot[0:64, h * 64:(h + 1) * 64],
                      o_kn, o_kr, o_kn2,
                      pool, epool, erpool, rpool, sqpool, sq64p, ps_st, str2,
                      rot_tok, base_tok)
    k_done = p.tok(ctx.pe)
    tw_cq = load_weight(ctx, wsb, WA, w_cq_d, D, 1536, after=(k_done,))

    big_war = None
    vtm_war2 = [None, None]
    for t in range(2):
        vfm_tok = None
        for m in range(16):
            slot = pool.take()
            ps, war = slot
            p.wait_tok("tensor", tw_dv)
            if war is not None:
                p.wait_tok("tensor", war)
            mm = None
            for k in range(4):
                mm = p.op("tensor", "matmul", ps[:, :], wslice(wsb, R_DV, 2048, k, m),
                          nkv_rhs(k, t), start=(k == 0), stop=(k == 3),
                          sem=(ctx.pe if k == 3 else None))
            vfm_tok = p.op("scalar", "activation", big[:, m * 512:m * 512 + 512],
                           ps[:, :], AF.Identity, sem=ctx.sc, after=(mm, big_war))
            slot[1] = vfm_tok
        p.wait_tok("tensor", vfm_tok)
        for a in range(4):
            sl = a % 2
            if vtm_war2[sl] is not None:
                p.wait_tok("vector", vtm_war2[sl])
            for g4 in range(4):
                srcs = [big[:, (g4 * 4 + j) * 512 + a * 128:(g4 * 4 + j) * 512 + a * 128 + 128]
                        for j in range(4)]
                dst = vtm[:, sl * 2048 + g4 * 512:sl * 2048 + (g4 + 1) * 512]
                tg = transpose_group(ctx, srcs, dst, trpool.take())
            vtm_war2[sl] = p.op("sync", "dma_start", sem=ctx.st, n=16,
                                out=o_v[t * 512 + a * 128:t * 512 + (a + 1) * 128, :],
                                in_=vtm[:, sl * 2048:(sl + 1) * 2048], after=(tg,))
        big_war = p.tok(ctx.pe)
    v_done = p.tok(ctx.pe)
    if mega:
        p.wait("gpsimd", ctx.st)
        p.op("gpsimd", "collective_compute", "AllReduce", ALU.add,
             replica_groups=[list(range(NCORES))],
             ins=[mega["io"]["bar1"].opt()], outs=[mega["io"]["bar1o"].opt()],
             sem=ctx.cc, n=1)
        p.wait("gpsimd", ctx.cc)
        for src, dst in (("kn20", "kn2AG"), ("Kn0", "KnAG"), ("Kr0", "KrAG"), ("V0", "VAG")):
            p.op("gpsimd", "collective_compute", "AllGather", ALU.bypass,
                 replica_groups=[list(range(NCORES))],
                 ins=[mega["io"][src].opt()], outs=[mega["io"][dst].opt()],
                 sem=ctx.cc, n=1)
        mega["cc1"] = p.tok(ctx.cc)
        mega["cc_kn2"] = (ctx.cc, mega["cc1"][1] - 3)

    nq_mods = rms_proj(WA, 1536, 16, lnx_rhs, nq, 12, qg, tw_cq, 1536)
    nq_pe_done = p.tok(ctx.pe)

    def nq_rhs(k, t):
        for tk in nq_mods:
            ctx.p.wait_tok("tensor", tk)
        return nq[:, k * T + t * 512:k * T + (t + 1) * 512]

    tw_qr = load_weight(ctx, wsb, WB, w_dqr_d, 1536, 1024, after=(v_done,))
    for t in range(2):
        for m in range(8):
            slot = pool.take()
            ps, war = slot
            p.wait_tok("tensor", tw_qr)
            if war is not None:
                p.wait_tok("tensor", war)
            mm = None
            for k in range(12):
                mm = p.op("tensor", "matmul", ps[:, :], wslice(wsb, WB, 1024, k, m),
                          nq_rhs(k, t), start=(k == 0), stop=(k == 11),
                          sem=(ctx.pe if k == 11 else None))
            slot[1] = p.op("scalar", "activation",
                           big[:, m * T + t * 512:m * T + (t + 1) * 512],
                           ps[:, :], AF.Identity, sem=ctx.sc, after=(mm, v_done))
    prerot_done = p.tok(ctx.sc)
    prerot_pe = p.tok(ctx.pe)

    tw_qn = load_weight(ctx, wsb, WA, w_dqn_d, 1536, 2048, after=(nq_pe_done,))

    def q_rope_rhs(h, t):
        c, r0 = h // 2, (h % 2) * 64
        return big[r0:r0 + 64, c * T + t * 512:c * T + (t + 1) * 512]

    for t in range(2):
        p.wait_tok("tensor", tw_qn)
        emit_qk_heads(ctx, t, wsb, WA, 12, nq_rhs, q_rope_rhs,
                      lambda h: rot[(h % 2) * 64:(h % 2) * 64 + 64, h * 64:(h + 1) * 64],
                      o_qn, o_qr, o_qn2,
                      pool, epool, erpool, rpool, sqpool, sq64p, ps_st, str2,
                      rot_tok, prerot_done)
    q_done = p.tok(ctx.pe)

    if mega:
        ctx.barrier()
        ctx.al.end_stage()
    else:
        ctx.al.end_stage()
    return ctx



def run_l0pre(g):
    ctx = build_l0pre()
    nc = ctx.finish()
    shared = {k: g[k] for k in ["w_cq", "w_dq_nope", "w_dq_rope_p", "w_ckv", "w_dk_nope",
                                "w_dv", "w_k_rope_p", "qg", "kvg", "crow", "srow",
                                "ident", "i64", "mswap"]}
    in_maps = []
    for i in range(NCORES):
        m = dict(shared)
        m["x_sl"] = g["x"][i * T:(i + 1) * T]
        in_maps.append(m)
    res = run_bass_kernel_spmd(nc, in_maps, core_ids=list(range(NCORES)))
    return res.results


ATT_SCALE0 = 1.0 / float(np.sqrt(192.0))
VSCALE0 = 1.0 / float(np.sqrt(2048.0))


def build_l0attn(mega=None):
    ctx = mega["ctx"] if mega else Ctx("l0attn")
    ctx.al.begin_stage("b")
    p, al = ctx.p, ctx.al
    x = ctx.din("x_sl", [T, D])
    if mega:
        io = mega["io"]
        qn_d, qr_d, qn2_d = io["Qn0"], io["Qr0"], io["qn20"]
        kn0, kr0, v0 = io["Kn0"], io["Kr0"], io["V0"]
        knag, krag, vag, kn2ag = io["KnAG"], io["KrAG"], io["VAG"], io["kn2AG"]
        o_r1 = io["r1"]
        mscr = io["mscrB"]
    else:
        qn_d = ctx.din("Qn", [2048, T], BF16)
        qr_d = ctx.din("Qr", [1024, T], BF16)
        knm_d = ctx.din("KnM", [2048, 7680], BF16)
        krm_d = ctx.din("KrM", [1024, 7680], BF16)
        vm_d = ctx.din("VM", [7680, 2048], BF16)
        knd_d = ctx.din("KnD", [2, 2048, 512], BF16)
        krd_d = ctx.din("KrD", [2, 1024, 512], BF16)
        vd_d = ctx.din("VD", [2, 512, 2048], BF16)
        qn2_d = ctx.din("qn2", [16, T])
        kn2_d = ctx.din("kn2f", [16, SEQ])
        o_r1 = ctx.dout("resid1", [T, D])
        mscr = ctx.nc.dram_tensor("mscr", [16, 2], F32, kind="Internal").ap()
    km_d = ctx.din("keymask", [128, 2 * 15])
    maskc_d = ctx.din("maskC", [128, 2048])
    wproj_d = ctx.din("w_proj", [D, D])
    ctx.din("ident", [128, 128], BF16)

    if not hasattr(ctx, "ident"):
        ctx.common()
    wsb = al.sb("wsb", [128, 16 * 2048], BF16)
    qn_sb = al.sb("qn_sb", [128, 16 * T], BF16)
    qr_sb = al.sb("qr_sb", [64, 16 * T], BF16)
    pT = al.sb("pT", [128, 2 * 2048], BF16)
    attn = al.sb("attn", [128, 16 * 512], BF16)
    maskc = al.sb("sb_maskc", [128, 2048], F32)
    keymask = al.sb("sb_keymask", [128, 2 * 15], F32)
    biasC = al.sb("biasC", [128, 32 * 15], F32)
    smask = al.sb("smask", [128, 2048], F32)
    kn_t = al.sb("kn_t", [128, 3 * 512], BF16)
    kr_t = al.sb("kr_t", [64, 3 * 512], BF16)
    v_t = al.sb("v_t", [128, 3 * 512], BF16)
    lsum = al.sb("lsum", [128, 2 * 512], BF16)
    lsum2 = al.sb("lsum2", [128, 512], BF16)
    linv = al.sb("linv", [128, 512], F32)
    rl = al.sb("rl", [1, 512], F32)
    xrt = al.sb("xrt", [128, 2 * 512], F32)
    rout = al.sb("rout", [128, 2 * 512], F32)
    qn2 = al.sb("sb_qn2", [16, T], F32)
    k2r = al.sb("k2r", [16, T], F32)
    kmax2 = al.sb("kmax2", [16, 1], F32)
    kred = al.sb("kred", [16, 1], F32)
    qmax2 = al.sb("qmax2", [16, 2], F32)
    negM = al.sb("negM", [16, 2], F32)
    mrow = al.sb("mrow", [1, 32], F32)
    Mb = al.sb("Mb", [128, 32], F32)
    ps_sa = al.ps("ps_sa", [128, 1024])
    ps_sb = al.ps("ps_sb", [128, 1024])
    ps_pv = al.ps("ps_pv", [128, 512])
    ps_l = al.ps("ps_l", [1, 512])
    ps_bc = al.ps("ps_bc", [128, 512])
    ps_o = al.ps("ps_o", [128, 512])

    tw = p.op("gpsimd", "dma_start", sem=ctx.ld, n=16,
              out=wsb[:, :].rearrange("p (k m) -> p k m", k=16),
              in_=wproj_d.rearrange("(k p) m -> p k m", p=128))
    p.op("sync", "dma_start", sem=ctx.ls, n=16,
         out=qn_sb[:, :].rearrange("p (h t) -> p h t", h=16),
         in_=qn_d.rearrange("(h p) t -> p h t", p=128))
    p.op("sync", "dma_start", sem=ctx.ls, n=16,
         out=qr_sb[:, :].rearrange("p (h t) -> p h t", h=16),
         in_=qr_d.rearrange("(h p) t -> p h t", p=64))
    p.op("sync", "dma_start", sem=ctx.ls, n=16, out=maskc[:, :], in_=maskc_d[:, :])
    tkm = p.op("sync", "dma_start", sem=ctx.ls, n=16, out=keymask[:, :], in_=km_d[:, :])
    tq3 = p.op("sync", "dma_start", sem=ctx.ls, n=16, out=qn2[:, :], in_=qn2_d[:, :])
    q_ready = p.tok(ctx.ls)

    km_tok = None
    for c in range(8):
        src = kn2ag[c * 16:(c + 1) * 16, :] if mega else kn2_d[:, c * 1024:(c + 1) * 1024]
        tl = p.op("sync", "dma_start", sem=ctx.ls, n=16, out=k2r[:, :],
                  in_=src, after=((km_tok,) if km_tok else ()) +
                  ((mega["cc_kn2"],) if (mega and "cc_kn2" in mega) else ()))
        trd = p.op("vector", "reduce_max", kred[:, :], k2r[:, :], axis=AX.X,
                   sem=ctx.dv, after=(tl,))
        if c == 0:
            km_tok = p.op("vector", "tensor_copy", kmax2[:, :], kred[:, :], sem=ctx.dv,
                          after=(trd,))
        else:
            km_tok = p.op("vector", "tensor_max", kmax2[:, :], kmax2[:, :], kred[:, :],
                          sem=ctx.dv, after=(trd,))
    tq4 = None
    for q in range(2):
        tq4 = p.op("vector", "reduce_max", qmax2[:, q:q + 1], qn2[:, q * 512:(q + 1) * 512],
                   axis=AX.X, sem=ctx.dv, after=(tq3, km_tok))
    t5 = p.op("vector", "tensor_scalar_mul", negM[:, :], qmax2[:, :], kmax2[:, 0:1],
              sem=ctx.dv, after=(tq4,))
    t6 = p.op("scalar", "activation", negM[:, :], negM[:, :], AF.Sqrt, sem=ctx.sc, after=(t5,))
    t7 = p.op("vector", "tensor_scalar_mul", negM[:, :], negM[:, :], -ATT_SCALE0,
              sem=ctx.dv, after=(t6,))
    t8 = p.op("sync", "dma_start", sem=ctx.st, n=16, out=mscr[:, :], in_=negM[:, :], after=(t7,))
    t9 = p.op("sync", "dma_start", sem=ctx.ls, n=16, out=mrow[:, :],
              in_=mscr.rearrange("a b -> (a b)")[None, :], after=(t8,))
    t10 = p.op("tensor", "matmul", ps_bc[:, 0:32], ctx.ones32[0:1, 0:128], mrow[0:1, :],
               start=True, stop=True, sem=ctx.pe, after=(t9,))
    t11 = p.op("vector", "tensor_copy", Mb[:, :], ps_bc[:, 0:32], sem=ctx.dv, after=(t10,))
    bias_tok = None
    for h in range(16):
        for q in range(2):
            idx = h * 2 + q
            bias_tok = p.op("vector", "tensor_scalar_add",
                            biasC[:, idx * 15:(idx + 1) * 15],
                            keymask[:, q * 15:(q + 1) * 15], Mb[:, idx:idx + 1],
                            sem=ctx.dv, after=(t11, tkm))

    NSLOT = 16
    st_kbuf = {"war": [None, None, None], "exp_war": [None, None],
               "pT_war": [None, None], "smask_war": None, "psg": t11,
               "bc_war": t11, "attn_war": None}

    def emit_slot_loads(h, q, j):
        b = j % 3
        extra = ()
        if mega:
            if j < 15:
                extra = ((mega["cc1"],) if "cc1" in mega else ())
                r, c0 = j // 2, (j % 2) * 512
                kn_src = knag[r * 2048 + h * 128:r * 2048 + (h + 1) * 128, c0:c0 + 512]
                kr_src = krag[r * 1024 + h * 64:r * 1024 + (h + 1) * 64, c0:c0 + 512]
                v_src = vag[j * 512:(j + 1) * 512, h * 128:(h + 1) * 128]
            else:
                kn_src = kn0[h * 128:(h + 1) * 128, q * 512:(q + 1) * 512]
                kr_src = kr0[h * 64:(h + 1) * 64, q * 512:(q + 1) * 512]
                v_src = v0[q * 512:(q + 1) * 512, h * 128:(h + 1) * 128]
        elif j < 15:
            kn_src = knm_d[h * 128:(h + 1) * 128, j * 512:(j + 1) * 512]
            kr_src = krm_d[h * 64:(h + 1) * 64, j * 512:(j + 1) * 512]
            v_src = vm_d[j * 512:(j + 1) * 512, h * 128:(h + 1) * 128]
        else:
            kn_src = knd_d[q, h * 128:(h + 1) * 128, :]
            kr_src = krd_d[q, h * 64:(h + 1) * 64, :]
            v_src = vd_d[q, :, h * 128:(h + 1) * 128]
        war = ((st_kbuf["war"][b],) if st_kbuf["war"][b] else ()) + extra
        t1 = p.op("sync", "dma_start", sem=ctx.ls, n=16,
                  out=kn_t[:, b * 512:(b + 1) * 512], in_=kn_src, after=war)
        t2 = p.op("sync", "dma_start", sem=ctx.ls, n=16,
                  out=kr_t[:, b * 512:(b + 1) * 512], in_=kr_src, after=war)
        t3 = p.op("gpsimd", "dma_start", sem=ctx.ld, n=16,
                  out=v_t[:, b * 512:(b + 1) * 512].rearrange("p (c d) -> p c d", c=4),
                  in_=v_src.rearrange("(c p) d -> p c d", p=128), after=war)
        return (t1, t2, t3)

    def flash_head(h, q):
        hq = h * 2 + q
        qn_rhs = qn_sb[:, h * T + q * 512:h * T + (q + 1) * 512]
        qr_rhs = qr_sb[:, h * T + q * 512:h * T + (q + 1) * 512]
        prev = None
        ld_q = {0: emit_slot_loads(h, q, 0), 1: emit_slot_loads(h, q, 1),
                2: emit_slot_loads(h, q, 2)}
        for j in range(NSLOT + 1):
            if j < NSLOT:
                b = j % 3
                ld_toks = ld_q.pop(j)
                mm_a = mm_b = None
                for c in range(4):
                    sps = ps_sa if c < 2 else ps_sb
                    if c in (0, 2) and st_kbuf["exp_war"][0 if c < 2 else 1] is not None:
                        p.wait_tok("tensor", st_kbuf["exp_war"][0 if c < 2 else 1])
                    if c == 0:
                        for tt in ld_toks:
                            p.wait_tok("tensor", tt)
                        p.wait_tok("tensor", q_ready)
                    p.op("tensor", "matmul", sps[:, (c % 2) * 512:(c % 2 + 1) * 512],
                         kn_t[:, b * 512 + c * 128:b * 512 + (c + 1) * 128],
                         qn_rhs, start=True, stop=False)
                    mm = p.op("tensor", "matmul", sps[:, (c % 2) * 512:(c % 2 + 1) * 512],
                              kr_t[:, b * 512 + c * 128:b * 512 + (c + 1) * 128],
                              qr_rhs, start=False, stop=True,
                              sem=(ctx.pe if c in (1, 3) else None))
                    if c == 1:
                        mm_a = mm
                mm_b = mm
                pb = j % 2
                pw = (st_kbuf["pT_war"][pb],) if st_kbuf["pT_war"][pb] else ()
                bias_ap = (Mb[:, hq:hq + 1] if j == NSLOT - 1
                           else biasC[:, hq * 15 + j:hq * 15 + j + 1])
                if j == NSLOT - 1:
                    smw = (st_kbuf["smask_war"],) if st_kbuf["smask_war"] else ()
                    tm1 = p.op("vector", "tensor_add", smask[:, 0:1024],
                               ps_sa[:, :], maskc[:, 0:1024], sem=ctx.dv,
                               after=(mm_a, bias_tok) + smw)
                    tm2 = p.op("vector", "tensor_add", smask[:, 1024:2048],
                               ps_sb[:, :], maskc[:, 1024:2048], sem=ctx.dv,
                               after=(mm_b,))
                    e_a = p.op("scalar", "activation", pT[:, pb * 2048:pb * 2048 + 1024],
                               smask[:, 0:1024], AF.Exp, bias=bias_ap,
                               scale=ATT_SCALE0, sem=ctx.sc, after=(tm1,) + pw)
                    e_b = p.op("scalar", "activation",
                               pT[:, pb * 2048 + 1024:pb * 2048 + 2048],
                               smask[:, 1024:2048], AF.Exp, bias=bias_ap,
                               scale=ATT_SCALE0, sem=ctx.sc, after=(tm2,))
                    st_kbuf["smask_war"] = e_b
                else:
                    e_a = p.op("scalar", "activation", pT[:, pb * 2048:pb * 2048 + 1024],
                               ps_sa[:, :], AF.Exp, bias=bias_ap, scale=ATT_SCALE0,
                               sem=ctx.sc, after=(mm_a, bias_tok) + pw)
                    e_b = p.op("scalar", "activation",
                               pT[:, pb * 2048 + 1024:pb * 2048 + 2048],
                               ps_sb[:, :], AF.Exp, bias=bias_ap, scale=ATT_SCALE0,
                               sem=ctx.sc, after=(mm_b,))
                st_kbuf["exp_war"][0] = e_a
                st_kbuf["exp_war"][1] = e_b
            if prev is not None:
                jp, pbp, ea_p, eb_p = prev
                c0 = pbp * 2048
                lw = st_kbuf.get("lsum_war_%d" % pbp)
                tl1 = p.op("vector", "tensor_add", lsum[:, pbp * 512:(pbp + 1) * 512],
                           pT[:, c0:c0 + 512], pT[:, c0 + 512:c0 + 1024],
                           sem=ctx.dv, after=(ea_p,) + ((lw,) if lw else ()))
                tl2 = p.op("vector", "tensor_add", lsum2[:, :],
                           pT[:, c0 + 1024:c0 + 1536], pT[:, c0 + 1536:c0 + 2048],
                           sem=ctx.dv, after=(eb_p,))
                tl3 = p.op("vector", "tensor_add", lsum[:, pbp * 512:(pbp + 1) * 512],
                           lsum[:, pbp * 512:(pbp + 1) * 512], lsum2[:, :],
                           sem=ctx.dv, after=(tl2, tl1))
                lmm = p.op("tensor", "matmul", ps_l[0:1, :], ctx.ones[:, 0:1],
                           lsum[:, pbp * 512:(pbp + 1) * 512], start=(jp == 0),
                           stop=(jp == NSLOT - 1), sem=ctx.pe,
                           after=(tl3,) + ((st_kbuf["psg"],) if jp == 0 else ()))
                st_kbuf["lsum_war_%d" % pbp] = lmm
                pv_tok = None
                vb = jp % 3
                for c in range(4):
                    pv_tok = p.op("tensor", "matmul", ps_pv[:, :],
                                  v_t[:, vb * 512 + c * 128:vb * 512 + (c + 1) * 128],
                                  pT[:, c0 + c * 512:c0 + (c + 1) * 512],
                                  start=(jp == 0 and c == 0),
                                  stop=(jp == NSLOT - 1 and c == 3),
                                  after=((ea_p,) if c < 2 else (eb_p,)) +
                                        ((st_kbuf["psg"],) if (jp == 0 and c == 0) else ()),
                                  sem=ctx.pe)
                st_kbuf["war"][vb] = pv_tok
                st_kbuf["pT_war"][pbp] = (pv_tok, tl3)
                if jp + 3 < NSLOT:
                    ld_q[jp + 3] = emit_slot_loads(h, q, jp + 3)
            if j < NSLOT:
                prev = (j, j % 2, e_a, e_b)
        lpv_done = p.tok(ctx.pe)
        trc = p.op("vector", "reciprocal", rl[:, :], ps_l[0:1, :], sem=ctx.dv,
                   after=(lpv_done,))
        trc2 = p.op("vector", "tensor_scalar_mul", rl[:, :], rl[:, :], VSCALE0,
                    sem=ctx.dv, after=(trc,))
        tb = p.op("tensor", "matmul", ps_bc[:, :], ctx.ones32[0:1, 0:128], rl[0:1, :],
                  start=True, stop=True, sem=ctx.pe,
                  after=(trc2,) + ((st_kbuf["bc_war"],) if st_kbuf["bc_war"] else ()))
        tc = p.op("vector", "tensor_copy", linv[:, :], ps_bc[:, :], sem=ctx.dv, after=(tb,))
        st_kbuf["bc_war"] = tc
        aw = (st_kbuf["attn_war"],) if st_kbuf["attn_war"] else ()
        ta = p.op("vector", "tensor_mul", attn[:, h * 512:(h + 1) * 512],
                  ps_pv[:, :], linv[:, :], sem=ctx.dv, after=(tc,) + aw)
        st_kbuf["psg"] = ta
        return ta

    for q in range(2):
        attn_tok = None
        for h in range(16):
            attn_tok = flash_head(h, q)
        po_war = None
        for mq in range(4):
            for n in range(4):
                mm = None
                p.wait_tok("tensor", tw)
                p.wait_tok("tensor", attn_tok)
                if po_war is not None:
                    p.wait_tok("tensor", po_war)
                for k in range(16):
                    mm = p.op("tensor", "matmul", ps_o[:, :],
                              attn[:, k * 512 + mq * 128:k * 512 + (mq + 1) * 128],
                              wsb[:, k * 2048 + n * 512:k * 2048 + (n + 1) * 512],
                              start=(k == 0), stop=(k == 15),
                              sem=(ctx.pe if k == 15 else None))
                sl = (mq * 4 + n) % 2
                xa = xrt[:, sl * 512:(sl + 1) * 512]
                ra = rout[:, sl * 512:(sl + 1) * 512]
                war = st_kbuf.get("oslot_war_%d" % sl)
                tx = p.op("sync", "dma_start", sem=ctx.ls, n=16, out=xa,
                          in_=x[q * 512 + mq * 128:q * 512 + (mq + 1) * 128,
                                n * 512:(n + 1) * 512],
                          after=((war,) if war else ()))
                tadd = p.op("vector", "tensor_add", ra, ps_o[:, :], xa, sem=ctx.dv,
                            after=(mm, tx) + ((war,) if war else ()))
                po_war = tadd
                st_kbuf["oslot_war_%d" % sl] = p.op(
                    "sync", "dma_start", sem=ctx.st, n=16,
                    out=o_r1[q * 512 + mq * 128:q * 512 + (mq + 1) * 128,
                             n * 512:(n + 1) * 512],
                    in_=ra, after=(tadd,))
        st_kbuf["attn_war"] = p.tok(ctx.pe)
    if mega:
        ctx.barrier()
    ctx.al.end_stage()
    return ctx


def run_l0attn(g, a_res):
    ctx = build_l0attn()
    nc = ctx.finish()
    Kn_all = np.concatenate([a_res[i]["Kn"] for i in range(NCORES)], axis=1)
    Kr_all = np.concatenate([a_res[i]["Kr"] for i in range(NCORES)], axis=1)
    V_all = np.concatenate([a_res[i]["V"] for i in range(NCORES)], axis=0)
    kn2f = np.concatenate([a_res[i]["kn2"] for i in range(NCORES)], axis=1)
    KnM = np.ascontiguousarray(Kn_all[:, :7680])
    KrM = np.ascontiguousarray(Kr_all[:, :7680])
    VM = np.ascontiguousarray(V_all[:7680, :])
    in_maps = []
    for i in range(NCORES):
        m = {"KnM": KnM, "KrM": KrM, "VM": VM, "kn2f": kn2f,
             "maskC": g["maskC"], "w_proj": g["w_proj"], "ident": g["ident"]}
        m["x_sl"] = g["x"][i * T:(i + 1) * T]
        m["Qn"] = a_res[i]["Qn"]
        m["Qr"] = a_res[i]["Qr"]
        m["qn2"] = a_res[i]["qn2"]
        knd = np.stack([Kn_all[:, (2 * i + q) * 512:(2 * i + q + 1) * 512] for q in range(2)])
        krd = np.stack([Kr_all[:, (2 * i + q) * 512:(2 * i + q + 1) * 512] for q in range(2)])
        vd = np.stack([V_all[(2 * i + q) * 512:(2 * i + q + 1) * 512, :] for q in range(2)])
        m["KnD"], m["KrD"], m["VD"] = (np.ascontiguousarray(knd),
                                       np.ascontiguousarray(krd),
                                       np.ascontiguousarray(vd))
        km = np.zeros((128, 2 * 15), np.float32)
        for q in range(2):
            dq = 2 * i + q
            km[:, q * 15 + dq:(q + 1) * 15] = NEG
        m["keymask"] = km
        in_maps.append(m)
    res = run_bass_kernel_spmd(nc, in_maps, core_ids=list(range(NCORES)))
    return res.results


def build_l0ffn(mega=None):
    ctx = mega["ctx"] if mega else Ctx("l0ffn")
    ctx.al.begin_stage("c")
    ctx.psg = None
    ctx.strowg = {0: None, 1: None}
    p, al = ctx.p, ctx.al
    r1 = mega["io"]["r1"] if mega else ctx.din("resid1", [T, D])
    w0_d = ctx.din("W0T", [D, 8192])
    b0_d = ctx.din("b0col", [128, 64])
    w1_d = ctx.din("W1T", [8192, D])
    b1_d = ctx.din("b1row", [1, D])
    ctx.din("ident", [128, 128], BF16)
    o_r2 = mega["io"]["r2"] if mega else ctx.dout("resid2", [T, D])

    if not hasattr(ctx, "ident"):
        ctx.common()
    W0C, W1C = 0, 16384
    wsb = al.sb("wsb", [128, 32768], BF16)
    lnx = al.sb("lnx", [128, 16 * T], BF16)
    lnx_tm = al.sb("lnx_tm", [128, D], BF16)
    xt = al.sb("xt", [128, D], F32)
    hid = al.sb("hid", [128, 8 * 512], BF16)
    eacc = al.sb("eacc", [128, 8 * 2048], F32)
    b0 = al.sb("sb_b0", [128, 64], F32)
    b1 = al.sb("sb_b1", [1, D], BF16)
    scr_sq = al.sb("scr_sq", [128, D], BF16)
    stats = {k: al.sb("st_" + k, [128, 1], F32) for k in ["ssq", "smean", "m", "var", "rstd"]}
    rt = al.sb("rt", [128, 2 * 512], F32)
    ro = al.sb("ro", [128, 2 * 512], F32)
    ps_a = al.ps("ps_a", [128, 512])
    ps_b = al.ps("ps_b", [128, 512])
    ps_tr = al.ps("ps_tr", [128, 2048], BF16)
    ps_e = al.ps("ps_e", [128, 512])
    ps_e2 = al.ps("ps_e2", [128, 512])
    pool = SlotPool([ps_a, ps_b])
    epool = SlotPool([ps_e, ps_e2])
    trpool = SlotPool([ps_tr[:, 0:512], ps_tr[:, 1024:1536]])

    p.op("sync", "dma_start", sem=ctx.ls, n=16, out=b0[:, :], in_=b0_d[:, :])
    tb1 = p.op("gpsimd", "dma_start", sem=ctx.ld, n=16, out=b1[:, :], in_=b1_d[:, :])

    xt_war = [None]
    tm_war = [None]
    lnx_ready = None
    for a in range(8):
        tx = p.op("sync", "dma_start", sem=ctx.ls, n=16, out=xt[:, :],
                  in_=r1[a * 128:(a + 1) * 128, :],
                  after=((xt_war[0],) if xt_war[0] is not None else ()))
        tn = emit_ln(ctx, xt[:, :], lnx_tm[:, :], scr_sq, stats,
                     (tx,) + ((tm_war[0],) if tm_war[0] is not None else ()))
        xt_war[0] = tn
        p.wait_tok("tensor", tn)
        for g4 in range(4):
            srcs = [lnx_tm[:, (g4 * 4 + j) * 128:(g4 * 4 + j) * 128 + 128]
                    for j in range(4)]
            dst = lnx[:, :].rearrange("p (k t) -> p k t", k=16)[:, g4 * 4:(g4 + 1) * 4,
                                                               a * 128:(a + 1) * 128]
            lnx_ready = transpose_group(ctx, srcs, dst, trpool.take())
        tm_war[0] = p.tok(ctx.pe)

    eacc_tok = {}
    hid_war = None
    pool4 = SlotPool([ps_a, ps_b, ps_e, ps_e2])
    wslots = [(0, 8192), (16384, 24576)]
    wtoks = [None, None]
    wwars = [None, None]
    NHG = 16

    def load_hg(hg):
        b = hg % 2
        W0R, W1R = wslots[b]
        war = (wwars[b],) if wwars[b] else ()
        t0 = p.op("gpsimd", "dma_start", sem=ctx.ld, n=16,
                  out=wsb[:, W0R:W0R + 8192].rearrange("p (k m) -> p k m", k=16),
                  in_=w0_d[:, hg * 512:(hg + 1) * 512].rearrange("(k p) m -> p k m", p=128),
                  after=war)
        t1 = p.op("gpsimd", "dma_start", sem=ctx.ld, n=16,
                  out=wsb[:, W1R:W1R + 8192].rearrange("p (k m) -> p k m", k=4),
                  in_=w1_d[hg * 512:(hg + 1) * 512, :].rearrange("(k p) m -> p k m", p=128),
                  after=war)
        wtoks[b] = (t0, t1)

    load_hg(0)
    for hg in range(NHG):
        b = hg % 2
        W0R, W1R = wslots[b]
        tw0, tw1 = wtoks[b]
        hid_toks = []
        for t in range(2):
            for c in range(4):
                gm = hg * 4 + c
                slot = pool4.take()
                ps, war = slot
                p.wait_tok("tensor", tw0)
                p.wait_tok("tensor", lnx_ready)
                if war is not None:
                    p.wait_tok("tensor", war)
                mm = None
                for k in range(16):
                    mm = p.op("tensor", "matmul", ps[:, :], wslice(wsb, W0R, 512, k, c),
                              lnx[:, k * T + t * 512:k * T + (t + 1) * 512],
                              start=(k == 0), stop=(k == 15),
                              sem=(ctx.pe if k == 15 else None))
                tg = p.op("scalar", "activation",
                          hid[:, (t * 4 + c) * 512:(t * 4 + c + 1) * 512], ps[:, :],
                          AF.Gelu, bias=b0[:, gm:gm + 1], sem=ctx.sc,
                          after=(mm,) + ((hid_war,) if hid_war else ()))
                hid_toks.append(tg)
                slot[1] = tg
        if hg + 1 < NHG:
            load_hg(hg + 1)
        for tk in hid_toks:
            p.wait_tok("tensor", tk)
        p.wait_tok("tensor", tw1)
        last_emm = None
        for t in range(2):
            for mq in range(4):
                for n in range(4):
                    ekey = (t, mq, n)
                    if ekey in eacc_tok:
                        p.wait_tok("tensor", eacc_tok[ekey])
                    eslot = pool4.take()
                    pse, ewar = eslot
                    if ewar is not None:
                        p.wait_tok("tensor", ewar)
                    mm = None
                    for k in range(4):
                        mm = p.op("tensor", "matmul", pse[:, :],
                                  hid[:, (t * 4 + k) * 512 + mq * 128:(t * 4 + k) * 512 + (mq + 1) * 128],
                                  wsb[:, W1R + k * 2048 + n * 512:W1R + k * 2048 + (n + 1) * 512],
                                  start=(k == 0), stop=(k == 3 and hg > 0),
                                  sem=(ctx.pe if (k == 3 and hg > 0) else None))
                    if hg == 0:
                        mm = p.op("tensor", "matmul", pse[:, :], ctx.ones[0:1, 0:128],
                                  b1[0:1, n * 512:(n + 1) * 512], start=False, stop=True,
                                  sem=ctx.pe, after=(tb1,))
                    last_emm = mm
                    dst = eacc[:, (t * 4 + mq) * 2048 + n * 512:(t * 4 + mq) * 2048 + (n + 1) * 512]
                    if hg == 0:
                        tadd = p.op("vector", "tensor_copy", dst, pse[:, :], sem=ctx.dv,
                                    after=(mm,))
                    else:
                        tadd = p.op("vector", "tensor_add", dst, dst, pse[:, :],
                                    sem=ctx.dv, after=(mm,))
                    eslot[1] = tadd
                    eacc_tok[ekey] = tadd
        hid_war = last_emm
        wwars[b] = p.tok(ctx.pe)

    oslot_war = [None, None]
    for t in range(2):
        for mq in range(4):
            for n in range(4):
                sl = (mq * 4 + n) % 2
                war = (oslot_war[sl],) if oslot_war[sl] else ()
                tx = p.op("sync", "dma_start", sem=ctx.ls, n=16,
                          out=rt[:, sl * 512:(sl + 1) * 512],
                          in_=r1[t * 512 + mq * 128:t * 512 + (mq + 1) * 128,
                                 n * 512:(n + 1) * 512], after=war)
                src = eacc[:, (t * 4 + mq) * 2048 + n * 512:(t * 4 + mq) * 2048 + (n + 1) * 512]
                tadd = p.op("vector", "tensor_add", ro[:, sl * 512:(sl + 1) * 512],
                            src, rt[:, sl * 512:(sl + 1) * 512], sem=ctx.dv,
                            after=(tx, eacc_tok[(t, mq, n)]) + war)
                oslot_war[sl] = p.op("sync", "dma_start", sem=ctx.st, n=16,
                                     out=o_r2[t * 512 + mq * 128:t * 512 + (mq + 1) * 128,
                                              n * 512:(n + 1) * 512],
                                     in_=ro[:, sl * 512:(sl + 1) * 512], after=(tadd,))
    if mega:
        ctx.barrier()
    ctx.al.end_stage()
    return ctx


def run_l0ffn(g, resid1_per_core):
    ctx = build_l0ffn()
    nc = ctx.finish()
    in_maps = []
    for i in range(NCORES):
        m = {"W0T": g["W0T"], "b0col": g["b0col"], "W1T": g["W1T"], "b1row": g["b1row"],
             "ident": g["ident"], "resid1": resid1_per_core[i]}
        in_maps.append(m)
    res = run_bass_kernel_spmd(nc, in_maps, core_ids=list(range(NCORES)))
    return res.results


def build_l1pre(stop_after=None, mega=None):
    ctx = mega["ctx"] if mega else Ctx("l1pre")
    ctx.al.begin_stage("d")
    ctx.psg = None
    ctx.str2g = None
    ctx.strowg = {0: None, 1: None}
    p, al = ctx.p, ctx.al
    r2 = mega["io"]["r2"] if mega else ctx.din("resid2", [T, D])
    wqkv_d = ctx.din("Wqkv", [D, 6144])
    qg_in = ctx.din("qg1", [128, 16])
    kg_in = ctx.din("kg1", [128, 16])
    cos_d = ctx.din("cosT_sl", [128, T], BF16)
    sin_d = ctx.din("sinT_sl", [128, T], BF16)
    psw_d = ctx.din("pswap", [128, 128], BF16)
    ctx.din("ident", [128, 128], BF16)
    if mega:
        io = mega["io"]
        o_q, o_k, o_v = io["Q1loc"], io["K1loc"], io["V1loc"]
        o_qn2, o_kn2 = io["qn21loc"], io["kn21loc"]
    else:
        o_q = ctx.dout("Q1", [2048, T], BF16)
        o_k = ctx.dout("K1", [2048, T], BF16)
        o_v = ctx.dout("V1", [T, 2048], BF16)
        o_qn2 = ctx.dout("qn21", [16, T])
        o_kn2 = ctx.dout("kn21", [16, T])

    if not hasattr(ctx, "ident"):
        ctx.common()
    wsb = al.sb("wsb", [128, 32768], BF16)
    lnx = al.sb("lnx", [128, 16 * T], BF16)
    lnx_tm = al.sb("lnx_tm", [128, D], BF16)
    xt = al.sb("xt", [128, D], F32)
    qa = al.sb("qa", [128, 16 * T], BF16)
    ev4 = al.sb("ev4", [128, 4 * 512], BF16)
    sqs = al.sb("sqs", [128, 2 * 512], BF16)
    psw = al.sb("sb_psw", [128, 128], BF16)
    uscr = al.sb("uscr", [128, 512], BF16)
    wscr = al.sb("wscr", [128, 512], BF16)
    cosT = al.sb("sb_cos", [128, T], BF16)
    sinT = al.sb("sb_sin", [128, T], BF16)
    qg = al.sb("sb_qg1", [128, 16], F32)
    kg = al.sb("sb_kg1", [128, 16], F32)
    strow = al.sb("strow", [1, 2 * 512], F32)
    str2 = al.sb("str2", [1, 512], F32)
    scr_sq = al.sb("scr_sq", [128, D], BF16)
    stats = {k: al.sb("st_" + k, [128, 1], F32) for k in ["ssq", "smean", "m", "var", "rstd"]}
    r128 = al.sb("r128", [128, T], F32)
    vtm = al.sb("vtm", [128, 2 * 2048], BF16)
    ps_a = al.ps("ps_a", [128, 512])
    ps_b = al.ps("ps_b", [128, 512])
    ps_tr = al.ps("ps_tr", [128, 2048], BF16)
    ps_st = al.ps("ps_st", [1, 512])
    ps_bc = al.ps("ps_bc", [128, 512])
    ps_s1 = al.ps("ps_s1", [128, 512])
    ps_s2 = al.ps("ps_s2", [128, 512])
    pool = SlotPool([ps_a, ps_b])
    trpool = SlotPool([ps_tr[:, 0:512], ps_tr[:, 1024:1536]])
    epool = SlotPool([ev4[:, i * 512:(i + 1) * 512] for i in range(4)])
    sqpool = SlotPool([sqs[:, 0:512], sqs[:, 512:1024]])
    swpool = SlotPool([ps_s1, ps_s2])
    bcslot = SlotPool([ps_bc])

    for dst, src in ((cosT, cos_d), (sinT, sin_d), (qg, qg_in), (kg, kg_in), (psw, psw_d)):
        p.op("sync", "dma_start", sem=ctx.ls, n=16, out=dst[:, :], in_=src[:, :])
    c_ready = p.tok(ctx.ls)

    xt_war = [None]
    tm_war = [None]
    lnx_ready = None
    for a in range(8):
        tx = p.op("sync", "dma_start", sem=ctx.ls, n=16, out=xt[:, :],
                  in_=r2[a * 128:(a + 1) * 128, :],
                  after=((xt_war[0],) if xt_war[0] is not None else ()))
        tn = emit_ln(ctx, xt[:, :], lnx_tm[:, :], scr_sq, stats,
                     (tx,) + ((tm_war[0],) if tm_war[0] is not None else ()))
        xt_war[0] = tn
        p.wait_tok("tensor", tn)
        for g4 in range(4):
            srcs = [lnx_tm[:, (g4 * 4 + j) * 128:(g4 * 4 + j) * 128 + 128]
                    for j in range(4)]
            dst = lnx[:, :].rearrange("p (k t) -> p k t", k=16)[:, g4 * 4:(g4 + 1) * 4,
                                                               a * 128:(a + 1) * 128]
            lnx_ready = transpose_group(ctx, srcs, dst, trpool.take())
        tm_war[0] = p.tok(ctx.pe)

    def lnx_rhs(k, t):
        ctx.p.wait_tok("tensor", lnx_ready)
        return lnx[:, k * T + t * 512:k * T + (t + 1) * 512]

    def qk_tensor(part, gain, o_fm, o_n2, w_war):
        last_to1 = [None]
        tw = p.op("gpsimd", "dma_start", sem=ctx.ld, n=16,
                  out=wsb[:, :].rearrange("p (k m) -> p k m", k=16),
                  in_=wqkv_d[:, part * 2048:(part + 1) * 2048].rearrange(
                      "(k p) m -> p k m", p=128),
                  after=(w_war,) if w_war else ())
        for t in range(2):
            stat_tok = None
            pending = None
            for m in range(16):
                slot = pool.take()
                ps, war = slot
                p.wait_tok("tensor", tw)
                if war is not None:
                    p.wait_tok("tensor", war)
                mm = None
                for k in range(16):
                    mm = p.op("tensor", "matmul", ps[:, :], wslice(wsb, 0, 2048, k, m),
                              lnx_rhs(k, t), start=(k == 0), stop=(k == 15),
                              sem=(ctx.pe if k == 15 else None))
                if pending is not None:
                    stat_tok = pending()
                tev = p.op("vector", "tensor_copy", qa[:, m * T + t * 512:m * T + (t + 1) * 512],
                           ps[:, :], sem=ctx.dv, after=(mm,))
                sqslot = sqpool.take()
                sqa_, sqwar = sqslot
                tsq = p.op("scalar", "activation", sqa_, ps[:, :], AF.Square, sem=ctx.sc,
                           after=(mm, tev) + ((sqwar,) if sqwar is not None else ()))
                slot[1] = (tev, tsq)

                def mk_stat(m=m, tsq=tsq, sqa_=sqa_, sqslot=sqslot):
                    def emit():
                        tk = p.op("tensor", "matmul", ps_st[0:1, :], ctx.ones[:, 0:1], sqa_,
                                  start=(m == 0), stop=(m == 15), sem=ctx.pe,
                                  after=(tsq,) + ((ctx.psg,) if (m == 0 and ctx.psg) else ()))
                        sqslot[1] = tk
                        return tk
                    return emit
                pending = mk_stat()
            stat_tok = pending()
            tr = rmsnorm_post2(ctx, ps_st[0:1, :], t, strow, r128, bcslot.take(),
                               2048, 1e-6, stat_tok)
            for m in range(16):
                ch = qa[:, m * T + t * 512:m * T + (t + 1) * 512]
                tg1 = p.op("vector", "tensor_scalar_mul", ch, ch, gain[:, m:m + 1],
                           sem=ctx.dv, after=(tr, c_ready))
                tg2 = p.op("vector", "tensor_mul", ch, ch, r128[:, t * 512:(t + 1) * 512],
                           sem=ctx.dv, after=(tg1,))
                sqslot = sqpool.take()
                sqa_, sqwar = sqslot
                tsq = p.op("scalar", "activation", sqa_, ch, AF.Square, sem=ctx.sc,
                           after=(tg2,) + ((sqwar,) if sqwar is not None else ()))
                tmm = p.op("tensor", "matmul", ps_st[0:1, :], ctx.ones[:, 0:1], sqa_,
                           start=True, stop=True, sem=ctx.pe,
                           after=(tsq,) + ((ctx.psg,) if ctx.psg else ()))
                sqslot[1] = tmm
                tse = p.op("vector", "tensor_copy", str2[0:1, :], ps_st[0:1, :], sem=ctx.dv,
                           after=(tmm,) + ((ctx.str2g,) if ctx.str2g else ()))
                ctx.psg = tse
                ctx.str2g = p.op("sync", "dma_start", sem=ctx.st, n=16,
                                 out=o_n2[m:m + 1, t * 512:(t + 1) * 512],
                                 in_=str2[0:1, :], after=(tse,))
                ch128 = qa[:, m * T + t * 512:m * T + (t + 1) * 512]
                Ct = cosT[:, t * 512:(t + 1) * 512]
                St = sinT[:, t * 512:(t + 1) * 512]
                es = epool.take()
                ev, ewar = es
                eww = (ewar,) if ewar is not None else ()
                swsl = swpool.take()
                ps_sw, swwar = swsl
                if swwar is not None:
                    p.wait_tok("tensor", swwar)
                tsw = p.op("tensor", "matmul", ps_sw[:, :], psw[:, :], ch128,
                           start=True, stop=True, sem=ctx.pe, after=(tg2, c_ready))
                tu = p.op("vector", "tensor_mul", uscr[:, :], ch128, Ct, sem=ctx.dv,
                          after=(tg2, c_ready))
                tw_ = p.op("vector", "tensor_mul", wscr[:, :], ps_sw[:, :], St, sem=ctx.dv,
                           after=(tsw,))
                swsl[1] = tw_
                to1 = p.op("vector", "tensor_add", ev[:, :], uscr[:, :], wscr[:, :],
                           sem=ctx.dv, after=(tu, tw_) + eww)
                last_to1[0] = to1
                es[1] = p.op("sync", "dma_start", sem=ctx.st, n=16,
                             out=o_fm[m * 128:(m + 1) * 128, t * 512:(t + 1) * 512],
                             in_=ev[:, :], after=(to1,))
        return p.tok(ctx.pe)

    if stop_after == "ln":
        ctx.al.end_stage()
        return ctx
    qtok = qk_tensor(0, qg, o_q, o_qn2, None)
    if stop_after == "q":
        ctx.al.end_stage()
        return ctx
    ktok = qk_tensor(1, kg, o_k, o_kn2, qtok)
    if stop_after == "k":
        ctx.al.end_stage()
        return ctx
    if mega:
        io = mega["io"]
        p.wait("gpsimd", ctx.st)
        p.op("gpsimd", "collective_compute", "AllReduce", ALU.add,
             replica_groups=[list(range(NCORES))],
             ins=[io["bar2"].opt()], outs=[io["bar2o"].opt()], sem=ctx.cc, n=1)
        p.wait("gpsimd", ctx.cc)
        for src, dst in (("K1loc", "K1AG"), ("kn21loc", "kn21AG")):
            p.op("gpsimd", "collective_compute", "AllGather", ALU.bypass,
                 replica_groups=[list(range(NCORES))],
                 ins=[io[src].opt()], outs=[io[dst].opt()], sem=ctx.cc, n=1)
        mega["ccK"] = p.tok(ctx.cc)

    tw = p.op("gpsimd", "dma_start", sem=ctx.ld, n=16,
              out=wsb[:, :].rearrange("p (k m) -> p k m", k=16),
              in_=wqkv_d[:, 4096:6144].rearrange("(k p) m -> p k m", p=128),
              after=(ktok,))
    qa_war = p.tok(ctx.gp)
    qa_war_v = p.tok(ctx.dv)
    vtm_war = [None, None]
    for t in range(2):
        vfm_tok = None
        for m in range(16):
            slot = pool.take()
            ps, war = slot
            p.wait_tok("tensor", tw)
            if war is not None:
                p.wait_tok("tensor", war)
            mm = None
            for k in range(16):
                mm = p.op("tensor", "matmul", ps[:, :], wslice(wsb, 0, 2048, k, m),
                          lnx_rhs(k, t), start=(k == 0), stop=(k == 15),
                          sem=(ctx.pe if k == 15 else None))
            vfm_tok = p.op("scalar", "activation",
                           qa[:, m * T + t * 512:m * T + (t + 1) * 512]
                           if False else qa[:, m * 512 + t * 8192:m * 512 + t * 8192 + 512],
                           ps[:, :], AF.Identity, sem=ctx.sc,
                           after=(mm, qa_war, qa_war_v))
            slot[1] = vfm_tok
        p.wait_tok("tensor", vfm_tok)
        for a in range(4):
            sl = a % 2
            if vtm_war[sl] is not None:
                p.wait_tok("vector", vtm_war[sl])
            for g4 in range(4):
                srcs = [qa[:, t * 8192 + (g4 * 4 + j) * 512 + a * 128:
                           t * 8192 + (g4 * 4 + j) * 512 + a * 128 + 128] for j in range(4)]
                dst = vtm[:, sl * 2048 + g4 * 512:sl * 2048 + (g4 + 1) * 512]
                tg = transpose_group(ctx, srcs, dst, trpool.take())
            vtm_war[sl] = p.op("sync", "dma_start", sem=ctx.st, n=16,
                               out=o_v[t * 512 + a * 128:t * 512 + (a + 1) * 128, :],
                               in_=vtm[:, sl * 2048:(sl + 1) * 2048], after=(tg,))
    if mega:
        io = mega["io"]
        p.wait("gpsimd", ctx.st)
        p.op("gpsimd", "collective_compute", "AllReduce", ALU.add,
             replica_groups=[list(range(NCORES))],
             ins=[io["bar3"].opt()], outs=[io["bar3o"].opt()], sem=ctx.cc, n=1)
        p.wait("gpsimd", ctx.cc)
        p.op("gpsimd", "collective_compute", "AllGather", ALU.bypass,
             replica_groups=[list(range(NCORES))],
             ins=[io["V1loc"].opt()], outs=[io["V1AG"].opt()], sem=ctx.cc, n=1)
        p.wait("gpsimd", ctx.cc)

        def nbcopy(g, io=io):
            pid = g.partition_id()
            o1 = g.snap((pid + 7) % 8 * 2048)
            g.dma_start(out=io["K1nb"][:, :],
                        in_=io["K1AG"][bass.ds(o1, 2048), :]).then_inc(ctx.ld, 16)
            o3 = g.snap((pid + 7) % 8 * 16)
            g.dma_start(out=io["kn21nb"][:, :],
                        in_=io["kn21AG"][bass.ds(o3, 16), :]).then_inc(ctx.ld, 16)
            o2 = g.snap((pid + 7) % 8 * 1024)
            g.dma_start(out=io["V1nb"][:, :],
                        in_=io["V1AG"][bass.ds(o2, 1024), :]).then_inc(ctx.ld, 16)
        p.emit("gpsimd", nbcopy)
        p.inc(ctx.ld, 48)
        mega["nb_tok"] = p.tok(ctx.ld)
        ctx.barrier()
    ctx.al.end_stage()
    return ctx


def run_l1pre(g, resid2_per_core):
    ctx = build_l1pre()
    nc = ctx.finish()
    cosb = _bf(g["cosT"])
    sinb = _bf(g["sinT"])
    in_maps = []
    for i in range(NCORES):
        m = {"Wqkv": g["Wqkv"], "qg1": g["qg1"], "kg1": g["kg1"], "ident": g["ident"],
             "pswap": g["pswap"], "resid2": resid2_per_core[i],
             "cosT_sl": np.ascontiguousarray(cosb[:, i * T:(i + 1) * T]),
             "sinT_sl": np.ascontiguousarray(sinb[:, i * T:(i + 1) * T])}
        in_maps.append(m)
    res = run_bass_kernel_spmd(nc, in_maps, core_ids=list(range(NCORES)))
    return res.results


ATT_SCALE1 = 1.0 / float(np.sqrt(128.0))


def build_l1attn(mega=None):
    ctx = mega["ctx"] if mega else Ctx("l1attn")
    ctx.al.begin_stage("e")
    p, al = ctx.p, ctx.al
    if mega:
        io = mega["io"]
        x = io["r2"]
        q_d, qn2_d = io["Q1loc"], io["qn21loc"]
        k1loc, v1loc, kn21loc = io["K1loc"], io["V1loc"], io["kn21loc"]
        k1nb, v1nb, kn21nb = io["K1nb"], io["V1nb"], io["kn21nb"]
        o_r3 = io["r3"]
        mscr = io["mscrE"]
    else:
        x = ctx.din("resid2", [T, D])
        q_d = ctx.din("Q1", [2048, T], BF16)
        k_d = ctx.din("K1in", [2048, 2 * 1536], BF16)
        v_d = ctx.din("V1in", [2 * 1536, 2048], BF16)
        qn2_d = ctx.din("qn21", [16, T])
        kn2_d = ctx.din("kn21in", [16, 2 * 1536])
        o_r3 = ctx.dout("resid3", [T, D])
        mscr = ctx.nc.dram_tensor("mscr", [16, 2], F32, kind="Internal").ap()
    km_d = ctx.din("keymask1", [128, 6])
    maskc_d = ctx.din("maskC", [128, 2048])
    maskw_d = ctx.din("maskW", [128, 2048])
    wo_d = ctx.din("Wo", [D, D])
    ctx.din("ident", [128, 128], BF16)

    if not hasattr(ctx, "ident"):
        ctx.common()
    wsb = al.sb("wsb", [128, 16 * 2048], BF16)
    q_sb = al.sb("q_sb", [128, 16 * T], BF16)
    pT = al.sb("pT", [128, 2 * 2048], BF16)
    attn = al.sb("attn", [128, 16 * 512], BF16)
    maskc = al.sb("sb_maskc", [128, 2048], F32)
    maskw = al.sb("sb_maskw", [128, 2048], F32)
    keymask = al.sb("sb_keymask", [128, 6], F32)
    biasC = al.sb("biasC", [128, 32 * 3], F32)
    smask = al.sb("smask", [128, 2048], F32)
    kn_t = al.sb("kn_t", [128, 3 * 512], BF16)
    v_t = al.sb("v_t", [128, 3 * 512], BF16)
    lsum = al.sb("lsum", [128, 2 * 512], BF16)
    lsum2 = al.sb("lsum2", [128, 512], BF16)
    linv = al.sb("linv", [128, 512], F32)
    rl = al.sb("rl", [1, 512], F32)
    xrt = al.sb("xrt", [128, 2 * 512], F32)
    rout = al.sb("rout", [128, 2 * 512], F32)
    qn2 = al.sb("sb_qn2", [16, T], F32)
    k2r = al.sb("k2r", [16, 2 * 1536], F32)
    kmax2 = al.sb("kmax2", [16, 2], F32)
    qmax2 = al.sb("qmax2", [16, 2], F32)
    negM = al.sb("negM", [16, 2], F32)
    mrow = al.sb("mrow", [1, 32], F32)
    Mb = al.sb("Mb", [128, 32], F32)
    ps_sa = al.ps("ps_sa", [128, 1024])
    ps_sb = al.ps("ps_sb", [128, 1024])
    ps_pv = al.ps("ps_pv", [128, 512])
    ps_l = al.ps("ps_l", [1, 512])
    ps_bc = al.ps("ps_bc", [128, 512])
    ps_o = al.ps("ps_o", [128, 512])

    tw = p.op("gpsimd", "dma_start", sem=ctx.ld, n=16,
              out=wsb[:, :].rearrange("p (k m) -> p k m", k=16),
              in_=wo_d.rearrange("(k p) m -> p k m", p=128))
    p.op("sync", "dma_start", sem=ctx.ls, n=16,
         out=q_sb[:, :].rearrange("p (h t) -> p h t", h=16),
         in_=q_d.rearrange("(h p) t -> p h t", p=128))
    p.op("sync", "dma_start", sem=ctx.ls, n=16, out=maskc[:, :], in_=maskc_d[:, :])
    p.op("sync", "dma_start", sem=ctx.ls, n=16, out=maskw[:, :], in_=maskw_d[:, :])
    tkm = p.op("sync", "dma_start", sem=ctx.ls, n=16, out=keymask[:, :], in_=km_d[:, :])
    tq3 = p.op("sync", "dma_start", sem=ctx.ls, n=16, out=qn2[:, :], in_=qn2_d[:, :])
    regs = {}
    if mega:

        tq5 = p.op("sync", "dma_start", sem=ctx.ls, n=16, out=k2r[:, 0:1024],
                   in_=kn21loc[:, :])
        tq5b = p.op("gpsimd", "dma_start", sem=ctx.ld, n=16, out=k2r[:, 1024:2048],
                    in_=kn21nb[:, :], after=(mega["nb_tok"],))
    else:
        tq5 = p.op("sync", "dma_start", sem=ctx.ls, n=16, out=k2r[:, :], in_=kn2_d[:, :])
    q_ready = p.tok(ctx.ls)

    tq4 = None
    if mega:
        ta_ = p.op("vector", "reduce_max", kmax2[:, 0:1], k2r[:, 0:1024], axis=AX.X,
                   sem=ctx.dv, after=(tq5,))
        tb_ = p.op("vector", "reduce_max", kmax2[:, 1:2], k2r[:, 1024:2048], axis=AX.X,
                   sem=ctx.dv, after=(tq5b, ta_))
        tkx = p.op("vector", "tensor_max", kmax2[:, 0:1], kmax2[:, 0:1], kmax2[:, 1:2],
                   sem=ctx.dv, after=(tb_,))
        for q in range(2):
            tq4 = p.op("vector", "reduce_max", qmax2[:, q:q + 1],
                       qn2[:, q * 512:(q + 1) * 512], axis=AX.X, sem=ctx.dv,
                       after=(tq3, tkx))
        t5 = p.op("vector", "tensor_scalar_mul", negM[:, :], qmax2[:, :], kmax2[:, 0:1],
                  sem=ctx.dv, after=(tq4,))
    else:
        for q in range(2):
            p.op("vector", "reduce_max", kmax2[:, q:q + 1], k2r[:, q * 1536:(q + 1) * 1536],
                 axis=AX.X, sem=ctx.dv, after=(tq5,))
            tq4 = p.op("vector", "reduce_max", qmax2[:, q:q + 1], qn2[:, q * 512:(q + 1) * 512],
                       axis=AX.X, sem=ctx.dv, after=(tq3,))
        t5 = p.op("vector", "tensor_mul", negM[:, :], qmax2[:, :], kmax2[:, :],
                  sem=ctx.dv, after=(tq4,))
    t6 = p.op("scalar", "activation", negM[:, :], negM[:, :], AF.Sqrt, sem=ctx.sc, after=(t5,))
    t7 = p.op("vector", "tensor_scalar_mul", negM[:, :], negM[:, :], -ATT_SCALE1,
              sem=ctx.dv, after=(t6,))
    t8 = p.op("sync", "dma_start", sem=ctx.st, n=16, out=mscr[:, :], in_=negM[:, :], after=(t7,))
    t9 = p.op("sync", "dma_start", sem=ctx.ls, n=16, out=mrow[:, :],
              in_=mscr.rearrange("a b -> (a b)")[None, :], after=(t8,))
    t10 = p.op("tensor", "matmul", ps_bc[:, 0:32], ctx.ones32[0:1, 0:128], mrow[0:1, :],
               start=True, stop=True, sem=ctx.pe, after=(t9,))
    t11 = p.op("vector", "tensor_copy", Mb[:, :], ps_bc[:, 0:32], sem=ctx.dv, after=(t10,))
    bias_tok = None
    for h in range(16):
        for q in range(2):
            idx = h * 2 + q
            bias_tok = p.op("vector", "tensor_scalar_add",
                            biasC[:, idx * 3:(idx + 1) * 3],
                            keymask[:, q * 3:(q + 1) * 3], Mb[:, idx:idx + 1],
                            sem=ctx.dv, after=(t11, tkm))

    NSLOT = 3
    st_k = {"war": [None, None, None], "exp_war": [None, None],
            "pT_war": [None, None], "smask_war": None, "psg": t11,
            "bc_war": t11, "attn_war": None}

    def emit_slot_loads(h, q, j):
        b = j
        war = (st_k["war"][b],) if st_k["war"][b] else ()
        if mega:
            rel = 512 * (q + j - 2)
            if rel < 0:
                loc = rel + 1024
                war = war + (mega["nb_tok"],)
                kn_src = k1nb[h * 128:(h + 1) * 128, loc:loc + 512]
                v_src = v1nb[loc:loc + 512, h * 128:(h + 1) * 128]
            else:
                kn_src = k1loc[h * 128:(h + 1) * 128, rel:rel + 512]
                v_src = v1loc[rel:rel + 512, h * 128:(h + 1) * 128]
        else:
            kn_src = k_d[h * 128:(h + 1) * 128, q * 1536 + j * 512:q * 1536 + (j + 1) * 512]
            v_src = v_d[q * 1536 + j * 512:q * 1536 + (j + 1) * 512, h * 128:(h + 1) * 128]
        t1 = p.op("sync", "dma_start", sem=ctx.ls, n=16,
                  out=kn_t[:, b * 512:(b + 1) * 512], in_=kn_src, after=war)
        t3 = p.op("gpsimd", "dma_start", sem=ctx.ld, n=16,
                  out=v_t[:, b * 512:(b + 1) * 512].rearrange("p (c d) -> p c d", c=4),
                  in_=v_src.rearrange("(c p) d -> p c d", p=128), after=war)
        return (t1, t3)

    SLOTS = [(q_, h_, j_) for q_ in range(2) for h_ in range(16) for j_ in range(NSLOT)]
    pump_state = {"i": 0}
    ld_q = {}

    def pump():
        i = pump_state["i"]
        if i < len(SLOTS):
            q_, h_, j_ = SLOTS[i]
            ld_q[(q_, h_, j_)] = emit_slot_loads(h_, q_, j_)
            pump_state["i"] = i + 1

    def flash_head(h, q):
        hq = h * 2 + q
        q_rhs = q_sb[:, h * T + q * 512:h * T + (q + 1) * 512]
        prev = None
        for j in range(NSLOT + 1):
            if j < NSLOT:
                ld_toks = ld_q.pop((q, h, j))
                b = j
                mm_a = mm_b = None
                for c in range(4):
                    sps = ps_sa if c < 2 else ps_sb
                    if c in (0, 2) and st_k["exp_war"][0 if c < 2 else 1] is not None:
                        p.wait_tok("tensor", st_k["exp_war"][0 if c < 2 else 1])
                    if c == 0:
                        for tt in ld_toks:
                            p.wait_tok("tensor", tt)
                        p.wait_tok("tensor", q_ready)
                    mm = p.op("tensor", "matmul", sps[:, (c % 2) * 512:(c % 2 + 1) * 512],
                              kn_t[:, b * 512 + c * 128:b * 512 + (c + 1) * 128],
                              q_rhs, start=True, stop=True,
                              sem=(ctx.pe if c in (1, 3) else None))
                    if c == 1:
                        mm_a = mm
                mm_b = mm
                pb = j % 2
                pw = (st_k["pT_war"][pb],) if st_k["pT_war"][pb] else ()
                bias_ap = biasC[:, hq * 3 + j:hq * 3 + j + 1]
                mk = maskw if j == 0 else (maskc if j == 2 else None)
                if mk is not None:
                    smw = (st_k["smask_war"],) if st_k["smask_war"] else ()
                    tm1 = p.op("vector", "tensor_add", smask[:, 0:1024],
                               ps_sa[:, :], mk[:, 0:1024], sem=ctx.dv,
                               after=(mm_a, bias_tok) + smw)
                    tm2 = p.op("vector", "tensor_add", smask[:, 1024:2048],
                               ps_sb[:, :], mk[:, 1024:2048], sem=ctx.dv,
                               after=(mm_b,))
                    e_a = p.op("scalar", "activation", pT[:, pb * 2048:pb * 2048 + 1024],
                               smask[:, 0:1024], AF.Exp, bias=bias_ap,
                               scale=ATT_SCALE1, sem=ctx.sc, after=(tm1,) + pw)
                    e_b = p.op("scalar", "activation",
                               pT[:, pb * 2048 + 1024:pb * 2048 + 2048],
                               smask[:, 1024:2048], AF.Exp, bias=bias_ap,
                               scale=ATT_SCALE1, sem=ctx.sc, after=(tm2,))
                    st_k["smask_war"] = e_b
                else:
                    e_a = p.op("scalar", "activation", pT[:, pb * 2048:pb * 2048 + 1024],
                               ps_sa[:, :], AF.Exp, bias=bias_ap, scale=ATT_SCALE1,
                               sem=ctx.sc, after=(mm_a, bias_tok) + pw)
                    e_b = p.op("scalar", "activation",
                               pT[:, pb * 2048 + 1024:pb * 2048 + 2048],
                               ps_sb[:, :], AF.Exp, bias=bias_ap, scale=ATT_SCALE1,
                               sem=ctx.sc, after=(mm_b,))
                st_k["exp_war"][0] = e_a
                st_k["exp_war"][1] = e_b
            if prev is not None:
                jp, pbp, ea_p, eb_p = prev
                c0 = pbp * 2048
                lw = st_k.get("lsum_war_%d" % pbp)
                tl1 = p.op("vector", "tensor_add", lsum[:, pbp * 512:(pbp + 1) * 512],
                           pT[:, c0:c0 + 512], pT[:, c0 + 512:c0 + 1024],
                           sem=ctx.dv, after=(ea_p,) + ((lw,) if lw else ()))
                tl2 = p.op("vector", "tensor_add", lsum2[:, :],
                           pT[:, c0 + 1024:c0 + 1536], pT[:, c0 + 1536:c0 + 2048],
                           sem=ctx.dv, after=(eb_p,))
                tl3 = p.op("vector", "tensor_add", lsum[:, pbp * 512:(pbp + 1) * 512],
                           lsum[:, pbp * 512:(pbp + 1) * 512], lsum2[:, :],
                           sem=ctx.dv, after=(tl2, tl1))
                lmm = p.op("tensor", "matmul", ps_l[0:1, :], ctx.ones[:, 0:1],
                     lsum[:, pbp * 512:(pbp + 1) * 512], start=(jp == 0),
                     stop=(jp == NSLOT - 1), sem=ctx.pe,
                     after=(tl3,) + ((st_k["psg"],) if jp == 0 else ()))
                st_k["lsum_war_%d" % pbp] = lmm
                pv_tok = None
                for c in range(4):
                    pv_tok = p.op("tensor", "matmul", ps_pv[:, :],
                                  v_t[:, jp * 512 + c * 128:jp * 512 + (c + 1) * 128],
                                  pT[:, c0 + c * 512:c0 + (c + 1) * 512],
                                  start=(jp == 0 and c == 0),
                                  stop=(jp == NSLOT - 1 and c == 3),
                                  after=((ea_p,) if c < 2 else (eb_p,)) +
                                        ((st_k["psg"],) if (jp == 0 and c == 0) else ()),
                                  sem=ctx.pe)
                st_k["war"][jp] = pv_tok
                st_k["pT_war"][pbp] = (pv_tok, tl3)
                pump()
            if j < NSLOT:
                prev = (j, j % 2, e_a, e_b)
        lpv_done = p.tok(ctx.pe)
        trc = p.op("vector", "reciprocal", rl[:, :], ps_l[0:1, :], sem=ctx.dv,
                   after=(lpv_done,))
        tb = p.op("tensor", "matmul", ps_bc[:, :], ctx.ones32[0:1, 0:128], rl[0:1, :],
                  start=True, stop=True, sem=ctx.pe,
                  after=(trc,) + ((st_k["bc_war"],) if st_k["bc_war"] else ()))
        tc = p.op("vector", "tensor_copy", linv[:, :], ps_bc[:, :], sem=ctx.dv, after=(tb,))
        st_k["bc_war"] = tc
        aw = (st_k["attn_war"],) if st_k["attn_war"] else ()
        ta = p.op("vector", "tensor_mul", attn[:, h * 512:(h + 1) * 512],
                  ps_pv[:, :], linv[:, :], sem=ctx.dv, after=(tc,) + aw)
        st_k["psg"] = ta
        return ta

    pump(); pump(); pump()
    for q in range(2):
        attn_tok = None
        for h in range(16):
            attn_tok = flash_head(h, q)
        po_war = None
        for mq in range(4):
            for n in range(4):
                mm = None
                p.wait_tok("tensor", tw)
                p.wait_tok("tensor", attn_tok)
                if po_war is not None:
                    p.wait_tok("tensor", po_war)
                for k in range(16):
                    mm = p.op("tensor", "matmul", ps_o[:, :],
                              attn[:, k * 512 + mq * 128:k * 512 + (mq + 1) * 128],
                              wsb[:, k * 2048 + n * 512:k * 2048 + (n + 1) * 512],
                              start=(k == 0), stop=(k == 15),
                              sem=(ctx.pe if k == 15 else None))
                sl = (mq * 4 + n) % 2
                xa = xrt[:, sl * 512:(sl + 1) * 512]
                ra = rout[:, sl * 512:(sl + 1) * 512]
                war = st_k.get("oslot_war_%d" % sl)
                tx = p.op("sync", "dma_start", sem=ctx.ls, n=16, out=xa,
                          in_=x[q * 512 + mq * 128:q * 512 + (mq + 1) * 128,
                                n * 512:(n + 1) * 512],
                          after=((war,) if war else ()))
                tadd = p.op("vector", "tensor_add", ra, ps_o[:, :], xa, sem=ctx.dv,
                            after=(mm, tx) + ((war,) if war else ()))
                po_war = tadd
                st_k["oslot_war_%d" % sl] = p.op(
                    "sync", "dma_start", sem=ctx.st, n=16,
                    out=o_r3[q * 512 + mq * 128:q * 512 + (mq + 1) * 128,
                             n * 512:(n + 1) * 512],
                    in_=ra, after=(tadd,))
        st_k["attn_war"] = p.tok(ctx.pe)
    if mega:
        ctx.barrier()
    ctx.al.end_stage()
    return ctx


def run_l1attn(g, r2_cores, d_res):
    ctx = build_l1attn()
    nc = ctx.finish()
    K1_all = np.concatenate([d_res[i]["K1"] for i in range(NCORES)], axis=1)
    V1_all = np.concatenate([d_res[i]["V1"] for i in range(NCORES)], axis=0)
    kn21_all = np.concatenate([d_res[i]["kn21"] for i in range(NCORES)], axis=1)
    in_maps = []
    for i in range(NCORES):
        kin = np.zeros((2048, 2 * 1536), K1_all.dtype)
        vin = np.zeros((2 * 1536, 2048), V1_all.dtype)
        k2in = np.zeros((16, 2 * 1536), np.float32)
        km = np.zeros((128, 6), np.float32)
        for q in range(2):
            q0 = 1024 * i + 512 * q
            lo = q0 - 1024
            for s in range(3):
                a, b_ = lo + s * 512, lo + (s + 1) * 512
                if b_ <= 0:
                    km[:, q * 3 + s] = NEG
                    continue
                aa = max(a, 0)
                pad = aa - a
                kin[:, q * 1536 + s * 512 + pad:q * 1536 + (s + 1) * 512] = K1_all[:, aa:b_]
                vin[q * 1536 + s * 512 + pad:q * 1536 + (s + 1) * 512, :] = V1_all[aa:b_, :]
                k2in[:, q * 1536 + s * 512 + pad:q * 1536 + (s + 1) * 512] = kn21_all[:, aa:b_]
                if pad > 0 and s != 2:
                    pass
        m = {"resid2": r2_cores[i], "Q1": d_res[i]["Q1"], "qn21": d_res[i]["qn21"],
             "K1in": kin, "V1in": vin, "kn21in": k2in, "keymask1": km,
             "maskC": g["maskC"], "maskW": g["maskW"], "Wo": g["Wo"], "ident": g["ident"]}
        in_maps.append(m)
    res = run_bass_kernel_spmd(nc, in_maps, core_ids=list(range(NCORES)))
    return res.results


def build_l1moe(mega=None):
    ctx = mega["ctx"] if mega else Ctx("l1moe")
    ctx.al.begin_stage("f")
    ctx.psg = None
    ctx.strowg = {0: None, 1: None}
    p, al = ctx.p, ctx.al
    r3 = mega["io"]["r3"] if mega else ctx.din("resid3", [T, D])
    rw_d = ctx.din("rwT", [D, 8])
    rb_d = ctx.din("rbrow", [1, 8])
    gw_d = ctx.din("gw", [8, D, D])
    uw_d = ctx.din("uw", [8, D, D])
    gb_d = ctx.din("gbcol", [8, 128, 16])
    ub_d = ctx.din("ubcol", [8, 128, 16])
    dw_d = ctx.din("dw", [8, D, D])
    db_d = ctx.din("dbrow", [8, D])
    ctx.din("ident", [128, 128], BF16)
    o_out = ctx.dout("out_sl", [T, D])

    if not hasattr(ctx, "ident"):
        ctx.common()
    resid = al.sb("resid", [128, 8 * D], F32)
    lnx = al.sb("lnx", [128, 16 * T], BF16)
    lnx_tm = al.sb("lnx_tm", [128, D], BF16)
    xt = al.sb("xt", [128, D], F32)
    WQ = 8192
    wsb = al.sb("wsb", [128, 2 * WQ], BF16)
    gsb = al.sb("gsb", [128, 16 * 512], BF16)
    act = al.sb("act", [128, 16 * 1024], BF16)
    u2s = al.sb("u2s", [128, 512], BF16)
    sgs = al.sb("sgs", [128, 512], BF16)
    rw_sb = al.sb("sb_rw", [128, 16 * 8], BF16)
    rb_sb = al.sb("sb_rb", [1, 8], BF16)
    gb = al.sb("sb_gb", [128, 16], F32)
    ub = al.sb("sb_ub", [128, 16], F32)
    db = al.sb("sb_db", [1, D], BF16)
    logits = al.sb("logits", [128, 8 * 8], F32)
    scores = al.sb("scores", [128, 8 * 8], F32)
    m1 = al.sb("sb_m1", [128, 1], F32)
    m2 = al.sb("sb_m2", [128, 1], F32)
    w1 = al.sb("sb_w1", [128, 1], F32)
    w2 = al.sb("sb_w2", [128, 1], F32)
    msk = al.sb("sb_msk", [128, 8], F32)
    msk2 = al.sb("sb_msk2", [128, 8], F32)
    scr_sq = al.sb("scr_sq", [128, D], BF16)
    stats = {k: al.sb("st_" + k, [128, 1], F32) for k in ["ssq", "smean", "m", "var", "rstd"]}
    ps_a = al.ps("ps_a", [128, 512])
    ps_b = al.ps("ps_b", [128, 512])
    ps_tr = al.ps("ps_tr", [128, 2048], BF16)
    ps_rt = al.ps("ps_rt", [128, 64])
    ps_d = al.ps("ps_d", [128, 512])
    ps_d2 = al.ps("ps_d2", [128, 512])
    pool = SlotPool([ps_a, ps_b])
    dpool = SlotPool([ps_d, ps_d2])
    trpool = SlotPool([ps_tr[:, 0:512], ps_tr[:, 1024:1536]])

    trw = p.op("gpsimd", "dma_start", sem=ctx.ld, n=16,
               out=rw_sb[:, :].rearrange("p (k e) -> p k e", k=16),
               in_=rw_d.rearrange("(k p) e -> p k e", p=128))
    trb = p.op("gpsimd", "dma_start", sem=ctx.ld, n=16, out=rb_sb[:, :], in_=rb_d[:, :])

    res_toks = []
    tm_war = [None]
    lnx_ready = None
    for a in range(8):
        tx = p.op("sync", "dma_start", sem=ctx.ls, n=16, out=resid[:, a * D:(a + 1) * D],
                  in_=r3[a * 128:(a + 1) * 128, :])
        res_toks.append(tx)
        tn = emit_ln(ctx, resid[:, a * D:(a + 1) * D], lnx_tm[:, :], scr_sq, stats,
                     (tx,) + ((tm_war[0],) if tm_war[0] is not None else ()))
        p.wait_tok("tensor", tn)
        for g4 in range(4):
            srcs = [lnx_tm[:, (g4 * 4 + j) * 128:(g4 * 4 + j) * 128 + 128]
                    for j in range(4)]
            dst = lnx[:, :].rearrange("p (k t) -> p k t", k=16)[:, g4 * 4:(g4 + 1) * 4,
                                                               a * 128:(a + 1) * 128]
            lnx_ready = transpose_group(ctx, srcs, dst, trpool.take())
        tm_war[0] = p.tok(ctx.pe)

    sc_toks = []
    rt_guard = None
    for a in range(8):
        mm = None
        p.wait_tok("tensor", trw)
        p.wait_tok("tensor", lnx_ready)
        if rt_guard is not None:
            p.wait_tok("tensor", rt_guard)
        for k in range(16):
            mm = p.op("tensor", "matmul", ps_rt[:, a * 8:(a + 1) * 8],
                      lnx[:, k * T + a * 128:k * T + (a + 1) * 128],
                      rw_sb[:, k * 8:(k + 1) * 8],
                      start=(k == 0), stop=False)
        mm = p.op("tensor", "matmul", ps_rt[:, a * 8:(a + 1) * 8], ctx.ones[0:1, 0:128],
                  rb_sb[0:1, :], start=False, stop=True, sem=ctx.pe, after=(trb,))
        la = logits[:, a * 8:(a + 1) * 8]
        sa = scores[:, a * 8:(a + 1) * 8]
        t1 = p.op("vector", "tensor_copy", la, ps_rt[:, a * 8:(a + 1) * 8], sem=ctx.dv,
                  after=(mm,))
        rt_guard = t1
        t2 = p.op("vector", "reduce_max", m1[:, :], la, axis=AX.X, sem=ctx.dv, after=(t1,))
        t3 = p.op("vector", "tensor_single_scalar", msk[:, :], la, m1[:, :],
                  ALU.is_equal, sem=ctx.dv, after=(t2,))
        t4 = p.op("vector", "scalar_tensor_tensor", msk2[:, :], msk[:, :], -1.0e30, la,
                  ALU.mult, ALU.add, sem=ctx.dv, after=(t3,))
        t5 = p.op("vector", "reduce_max", m2[:, :], msk2[:, :], axis=AX.X, sem=ctx.dv,
                  after=(t4,))
        t6 = p.op("vector", "tensor_sub", w1[:, :], m1[:, :], m2[:, :], sem=ctx.dv,
                  after=(t5,))
        t7 = p.op("scalar", "activation", w1[:, :], w1[:, :], AF.Sigmoid, sem=ctx.sc,
                  after=(t6,))
        t8 = p.op("vector", "tensor_scalar", w2[:, :], w1[:, :], -1.0, 1.0,
                  ALU.mult, ALU.add, sem=ctx.dv, after=(t7,))
        t9 = p.op("vector", "tensor_single_scalar", msk2[:, :], la, m2[:, :],
                  ALU.is_equal, sem=ctx.dv, after=(t8,))
        t10 = p.op("vector", "tensor_scalar_mul", sa, msk[:, :], w1[:, :], sem=ctx.dv,
                   after=(t9,))
        t11 = p.op("vector", "scalar_tensor_tensor", sa, msk2[:, :], w2[:, :], sa,
                   ALU.mult, ALU.add, sem=ctx.dv, after=(t10,))
        sc_toks.append(t11)

    wpool = SlotPool([wsb[:, i * 4096:(i + 1) * 4096] for i in range(4)])
    gch = al.sb("gch", [128, 512], BF16)
    sgch = al.sb("sgch", [128, 512], BF16)
    u2ch = al.sb("u2ch", [128, 512], BF16)
    glch = al.sb("glch", [128, 512], BF16)
    act_war = None
    bias_war = None
    db_war = None
    dpool2 = SlotPool([wsb[:, 0:8192], wsb[:, 8192:16384]])
    sc_tok = sc_toks[-1]
    res_out_tok = {}
    for e in range(8):
        tgb = p.op("sync", "dma_start", sem=ctx.ls, n=16, out=gb[:, :], in_=gb_d[e, :, :],
                   after=(bias_war,) if bias_war else ())
        tub = p.op("sync", "dma_start", sem=ctx.ls, n=16, out=ub[:, :], in_=ub_d[e, :, :],
                   after=(bias_war,) if bias_war else ())
        tdb = p.op("gpsimd", "dma_start", sem=ctx.ld, n=16, out=db[:, :],
                   in_=db_d[e:e + 1, :], after=(db_war,) if db_war else ())
        last_act = None
        last_ts = None
        for pd in range(8):
            wg_s = wpool.take()
            wg, wgw = wg_s
            twg = p.op("gpsimd", "dma_start", sem=ctx.ld, n=16,
                       out=wg[:, :].rearrange("p (k m) -> p k m", k=16),
                       in_=gw_d[e, :, pd * 256:(pd + 1) * 256].rearrange(
                           "(k p) m -> p k m", p=128),
                       after=(wgw,) if wgw else ())
            wu_s = wpool.take()
            wu, wuw = wu_s
            twu = p.op("gpsimd", "dma_start", sem=ctx.ld, n=16,
                       out=wu[:, :].rearrange("p (k m) -> p k m", k=16),
                       in_=uw_d[e, :, pd * 256:(pd + 1) * 256].rearrange(
                           "(k p) m -> p k m", p=128),
                       after=(wuw,) if wuw else ())
            for mi in range(2):
                m = pd * 2 + mi
                for t in range(2):
                    gslot = pool.take()
                    gps, gwar = gslot
                    p.wait_tok("tensor", twg)
                    p.wait_tok("tensor", lnx_ready)
                    if act_war is not None:
                        p.wait_tok("tensor", act_war)
                    if gwar is not None:
                        p.wait_tok("tensor", gwar)
                    gmm = None
                    for k in range(16):
                        gmm = p.op("tensor", "matmul", gps[:, :],
                                   wg[:, k * 256 + mi * 128:k * 256 + (mi + 1) * 128],
                                   lnx[:, k * T + t * 512:k * T + (t + 1) * 512],
                                   start=(k == 0), stop=(k == 15),
                                   sem=(ctx.pe if k == 15 else None))
                    uslot = pool.take()
                    ups, uwar = uslot
                    p.wait_tok("tensor", twu)
                    if uwar is not None:
                        p.wait_tok("tensor", uwar)
                    umm = None
                    for k in range(16):
                        umm = p.op("tensor", "matmul", ups[:, :],
                                   wu[:, k * 256 + mi * 128:k * 256 + (mi + 1) * 128],
                                   lnx[:, k * T + t * 512:k * T + (t + 1) * 512],
                                   start=(k == 0), stop=(k == 15),
                                   sem=(ctx.pe if k == 15 else None))
                    tg1 = p.op("vector", "tensor_scalar", gch[:, :], gps[:, :],
                               gb[:, m:m + 1], 7.0, ALU.add, ALU.min, sem=ctx.dv,
                               after=(gmm, tgb) + ((last_ts,) if last_ts else ()))
                    gslot[1] = tg1
                    tsg = p.op("scalar", "activation", sgch[:, :], gch[:, :], AF.Sigmoid,
                               scale=1.702, sem=ctx.sc, after=(tg1,))
                    tu1 = p.op("vector", "tensor_scalar", u2ch[:, :], ups[:, :],
                               ub[:, m:m + 1], -7.0, ALU.add, ALU.max, sem=ctx.dv,
                               after=(umm, tub))
                    uslot[1] = tu1
                    tu2 = p.op("vector", "tensor_scalar", u2ch[:, :], u2ch[:, :],
                               7.0, 1.0, ALU.min, ALU.add, sem=ctx.dv, after=(tu1,))
                    tgl = p.op("vector", "tensor_mul", glch[:, :], gch[:, :], sgch[:, :],
                               sem=ctx.dv, after=(tsg,))
                    last_act = p.op("vector", "tensor_mul",
                                    act[:, m * 1024 + t * 512:m * 1024 + (t + 1) * 512],
                                    u2ch[:, :], glch[:, :], sem=ctx.dv, after=(tgl, tu2))
                    last_ts = last_act
            piece_done = p.tok(ctx.pe)
            wg_s[1] = piece_done
            wu_s[1] = piece_done
        bias_war = last_act
        dn_last = None
        for nd in range(4):
            wd_s = dpool2.take()
            wd, wdw = wd_s
            p.wait_tok("gpsimd", last_act)
            twd = p.op("gpsimd", "dma_start", sem=ctx.ld, n=16,
                       out=wd[:, :].rearrange("p (k m) -> p k m", k=16),
                       in_=dw_d[e, :, nd * 512:(nd + 1) * 512].rearrange(
                           "(k p) m -> p k m", p=128),
                       after=((wdw,) if wdw else ()) + ((act_war,) if act_war else ()))
            for t in range(2):
                for mq in range(4):
                    dslot = dpool.take()
                    dps, dwar = dslot
                    p.wait_tok("tensor", twd)
                    p.wait_tok("tensor", last_act)
                    if dwar is not None:
                        p.wait_tok("tensor", dwar)
                    mm = None
                    for k in range(16):
                        mm = p.op("tensor", "matmul", dps[:, :],
                                  act[:, k * 1024 + t * 512 + mq * 128:
                                      k * 1024 + t * 512 + (mq + 1) * 128],
                                  wd[:, k * 512:(k + 1) * 512],
                                  start=(k == 0), stop=False)
                    mm = p.op("tensor", "matmul", dps[:, :], ctx.ones[0:1, 0:128],
                              db[0:1, nd * 512:(nd + 1) * 512], start=False, stop=True,
                              sem=ctx.pe, after=(tdb,))
                    dn_last = mm
                    a = t * 4 + mq
                    rkey = (a, nd)
                    tacc = p.op("vector", "scalar_tensor_tensor",
                                resid[:, a * D + nd * 512:a * D + (nd + 1) * 512],
                                dps[:, :], scores[:, a * 8 + e:a * 8 + e + 1],
                                resid[:, a * D + nd * 512:a * D + (nd + 1) * 512],
                                ALU.mult, ALU.add, sem=ctx.dv,
                                after=(mm, sc_tok, res_toks[a]) +
                                      ((res_out_tok.get(rkey),) if res_out_tok.get(rkey) else ()))
                    dslot[1] = tacc
                    res_out_tok[rkey] = tacc
            wd_s[1] = dn_last
        act_war = dn_last
        db_war = dn_last
        for tile in wpool.tiles:
            tile[1] = dn_last

    for a in range(8):
        toks = tuple(res_out_tok[(a, nd)] for nd in range(4))
        p.op("sync", "dma_start", sem=ctx.st, n=16, out=o_out[a * 128:(a + 1) * 128, :],
             in_=resid[:, a * D:(a + 1) * D], after=toks)
    ctx.al.end_stage()
    return ctx


def run_l1moe(g, r3_cores):
    ctx = build_l1moe()
    nc = ctx.finish()
    in_maps = []
    for i in range(NCORES):
        m = {"rwT": g["rwT"], "rbrow": g["rbrow"], "gw": g["gw"], "uw": g["uw"],
             "gbcol": g["gbcol"], "ubcol": g["ubcol"], "dw": g["dw"], "dbrow": g["dbrow"],
             "ident": g["ident"], "resid3": r3_cores[i]}
        in_maps.append(m)
    res = run_bass_kernel_spmd(nc, in_maps, core_ids=list(range(NCORES)))
    return res.results


def kernel(**inputs):
    return kernel_mega(**inputs)


def kernel_unfused(**inputs):
    g = host_prep(inputs)
    a_res = run_l0pre(g)
    b_res = run_l0attn(g, a_res)
    r1 = [b_res[i]["resid1"] for i in range(NCORES)]
    c_res = run_l0ffn(g, r1)
    r2 = [c_res[i]["resid2"] for i in range(NCORES)]
    d_res = run_l1pre(g, r2)
    e_res = run_l1attn(g, r2, d_res)
    r3 = [e_res[i]["resid3"] for i in range(NCORES)]
    f_res = run_l1moe(g, r3)
    out = np.concatenate([f_res[i]["out_sl"] for i in range(NCORES)], axis=0)
    return out.reshape(1, SEQ, D).astype(np.float32)


def build_mega(debug=False):
    ctx = Ctx("mega")
    io = {}
    dt = ctx.dint
    io["Qn0"] = dt("i_Qn0", [2048, T], BF16)
    io["Qr0"] = dt("i_Qr0", [1024, T], BF16)
    io["Kn0"] = dt("i_Kn0", [2048, T], BF16)
    io["Kr0"] = dt("i_Kr0", [1024, T], BF16)
    io["V0"] = dt("i_V0", [T, 2048], BF16)
    io["qn20"] = dt("i_qn20", [16, T], F32)
    io["kn20"] = dt("i_kn20", [16, T], F32)
    io["KnAG"] = dt("i_KnAG", [8 * 2048, T], BF16)
    io["KrAG"] = dt("i_KrAG", [8 * 1024, T], BF16)
    io["VAG"] = dt("i_VAG", [8 * T, 2048], BF16)
    io["kn2AG"] = dt("i_kn2AG", [8 * 16, T], F32)
    io["r1"] = dt("i_r1", [T, D], F32)
    io["r2"] = dt("i_r2", [T, D], F32)
    io["r3"] = dt("i_r3", [T, D], F32)
    io["Q1loc"] = dt("i_Q1loc", [2048, T], BF16)
    io["K1loc"] = dt("i_K1loc", [2048, T], BF16)
    io["V1loc"] = dt("i_V1loc", [T, 2048], BF16)
    io["qn21loc"] = dt("i_qn21loc", [16, T], F32)
    io["kn21loc"] = dt("i_kn21loc", [16, T], F32)
    io["K1AG"] = dt("i_K1AG", [8 * 2048, T], BF16)
    io["V1AG"] = dt("i_V1AG", [8 * T, 2048], BF16)
    io["kn21AG"] = dt("i_kn21AG", [8 * 16, T], F32)
    io["K1nb"] = dt("i_K1nb", [2048, T], BF16)
    io["V1nb"] = dt("i_V1nb", [T, 2048], BF16)
    io["kn21nb"] = dt("i_kn21nb", [16, T], F32)
    io["mscrB"] = dt("i_mscrB", [16, 2], F32)
    io["mscrE"] = dt("i_mscrE", [16, 2], F32)
    for bn in ("bar1", "bar1o", "bar2", "bar2o", "bar3", "bar3o"):
        io[bn] = dt("i_" + bn, [128, 16], F32)
    mega = {"ctx": ctx, "io": io}
    p = ctx.p

    build_l0pre(mega=mega)
    groups = [list(range(NCORES))]
    build_l0attn(mega=mega)
    build_l0ffn(mega=mega)
    build_l1pre(mega=mega)
    build_l1attn(mega=mega)
    build_l1moe(mega=mega)
    if debug:
        ctx.barrier()
        p = ctx.p
        for nm, shape, dt_ in (("r1", [T, D], F32), ("r2", [T, D], F32), ("r3", [T, D], F32),
                               ("Qn0", [2048, T], BF16), ("Kn0", [2048, T], BF16),
                               ("qn20", [16, T], F32), ("kn20", [16, T], F32),
                               ("kn2AG", [128, T], F32), ("K1loc", [2048, T], BF16),
                               ("K1nb", [2048, T], BF16), ("V1nb", [T, 2048], BF16),
                               ("qn21loc", [16, T], F32)):
            o = ctx.dout("dbg_" + nm, shape, dt_)
            sl = tuple(slice(None) for _ in shape)
            p.op("sync", "dma_start", sem=ctx.st, n=16, out=o[sl], in_=io[nm][sl])
        kslice = ctx.dout("dbg_KnAG", [2048, T], BF16)
        p.op("sync", "dma_start", sem=ctx.st, n=16, out=kslice[:, :], in_=io["KnAG"][2 * 2048:3 * 2048, :])
        for rr in range(8):
            vsl = ctx.dout("dbg_V1AG%d" % rr, [T, 2048], BF16)
            p.op("sync", "dma_start", sem=ctx.st, n=16, out=vsl[:, :],
                 in_=io["V1AG"][rr * T:(rr + 1) * T, :])
    return ctx


def mega_in_map(g, i):
    km = np.zeros((128, 2 * 15), np.float32)
    for q in range(2):
        dq = 2 * i + q
        km[:, q * 15 + dq:(q + 1) * 15] = NEG
    km1 = np.zeros((128, 6), np.float32)
    if i == 0:
        km1[:, 0] = NEG
        km1[:, 1] = NEG
        km1[:, 3] = NEG
    m = {
        "x_sl": g["x"][i * T:(i + 1) * T],
        "w_cq": g["w_cq"], "w_dq_nope": g["w_dq_nope"], "w_dq_rope_p": g["w_dq_rope_p"],
        "w_ckv": g["w_ckv"], "w_dk_nope": g["w_dk_nope"], "w_dv": g["w_dv"],
        "w_k_rope_p": g["w_k_rope_p"], "qg": g["qg"], "kvg": g["kvg"],
        "crow": g["crow"], "srow": g["srow"], "ident": g["ident"], "i64": g["i64"],
        "mswap": g["mswap"], "keymask": km, "maskC": g["maskC"], "maskW": g["maskW"],
        "w_proj": g["w_proj"], "W0T": g["W0T"], "b0col": g["b0col"], "W1T": g["W1T"],
        "b1row": g["b1row"], "Wqkv": g["Wqkv"], "qg1": g["qg1"], "kg1": g["kg1"],
        "pswap": g["pswap"],
        "cosT_sl": np.ascontiguousarray(_bf(g["cosT"])[:, i * T:(i + 1) * T]),
        "sinT_sl": np.ascontiguousarray(_bf(g["sinT"])[:, i * T:(i + 1) * T]),
        "keymask1": km1, "Wo": g["Wo"], "rwT": g["rwT"], "rbrow": g["rbrow"],
        "gw": g["gw"], "uw": g["uw"], "gbcol": g["gbcol"], "ubcol": g["ubcol"],
        "dw": g["dw"], "dbrow": g["dbrow"],
    }
    return m


def kernel_mega(**inputs):
    g = host_prep(inputs)
    ctx = build_mega()
    nc = ctx.finish()
    in_maps = [mega_in_map(g, i) for i in range(NCORES)]
    res = run_bass_kernel_spmd(nc, in_maps, core_ids=list(range(NCORES)))
    out = np.concatenate([res.results[i]["out_sl"] for i in range(NCORES)], axis=0)
    return out.reshape(1, SEQ, D).astype(np.float32)


# version 10
# speedup vs baseline: 1.0762x; 1.0085x over previous
import numpy as np
import ml_dtypes
from contextlib import ExitStack

import concourse.bass as bass
import concourse.mybir as mybir
from concourse.bass_utils import run_bass_kernel_spmd

F32 = mybir.dt.float32
BF16 = mybir.dt.bfloat16
AF = mybir.ActivationFunctionType
ALU = mybir.AluOpType
AX = mybir.AxisListType
BF = ml_dtypes.bfloat16

NCORES = 8
T = 1024
SEQ = 8192
D = 2048
NEG = -1.0e9

ENGINES = ("sync", "tensor", "vector", "scalar", "gpsimd")



class Prog:

    def __init__(self, nc):
        self.nc = nc
        self.q = {e: [] for e in ENGINES}
        self.semvals = {}
        self.waited = {e: {} for e in ENGINES}

    def _key(self, sem):
        return sem.name if hasattr(sem, "name") else sem

    def inc(self, sem, n=1):
        k = self._key(sem)
        self.semvals[k] = self.semvals.get(k, 0) + n
        return self.semvals[k]

    def val(self, sem):
        return self.semvals.get(self._key(sem), 0)

    def tok(self, sem):
        return (sem, self.val(sem))

    def emit(self, eng, fn):
        self.q[eng].append(fn)

    def wait(self, eng, sem, v=None):
        if v is None:
            v = self.val(sem)
        if v <= 0:
            return
        k = self._key(sem)
        if self.waited[eng].get(k, 0) >= v:
            return
        self.waited[eng][k] = v
        self.emit(eng, lambda e, s=sem, vv=v: e.wait_ge(s, vv))

    def wait_tok(self, eng, tok):
        if tok is None:
            return
        if isinstance(tok[0], tuple):
            for t in tok:
                if t is not None:
                    self.wait(eng, t[0], t[1])
            return
        self.wait(eng, tok[0], tok[1])

    def op(self, eng, method, *args, sem=None, n=1, after=(), **kwargs):
        for t in after:
            self.wait_tok(eng, t)
        v = self.inc(sem, n) if sem is not None else None

        def f(e, m=method, a=args, kw=kwargs, s=sem, nn=n):
            inst = getattr(e, m)(*a, **kw)
            if s is not None:
                inst.then_inc(s, nn)
        self.emit(eng, f)
        return (sem, v) if sem is not None else None

    def finalize(self, block):
        for name in ENGINES:
            fns = self.q[name]

            def runner(eng, fns=fns):
                for fn in fns:
                    fn(eng)
            getattr(block, name)(runner)


class Alloc:
    def __init__(self, nc):
        self.nc = nc
        self.stack = ExitStack()
        self.stage_stack = None
        self.prefix = ""

    def _st(self):
        return self.stage_stack if self.stage_stack is not None else self.stack

    def sb(self, name, shape, dtype):
        return self._st().enter_context(self.nc.sbuf_tensor(self.prefix + name, shape, dtype))

    def ps(self, name, shape, dtype=F32):
        return self._st().enter_context(self.nc.psum_tensor(self.prefix + name, shape, dtype))

    def sem(self, name):
        return self.stack.enter_context(self.nc.semaphore(name))

    def block(self):
        return self.stack.enter_context(self.nc.Block())

    def begin_stage(self, prefix):
        assert self.stage_stack is None
        self.stage_stack = ExitStack()
        self.prefix = prefix + "_"

    def end_stage(self):
        self.stage_stack.close()
        self.stage_stack = None
        self.prefix = ""

    def close(self):
        if self.stage_stack is not None:
            self.end_stage()
        self.stack.close()


class Ctx:
    def __init__(self, name):
        self.nc = bass.Bass("TRN2", target_bir_lowering=False)
        self.al = Alloc(self.nc)
        self.p = Prog(self.nc)
        self.name = name
        self.ins = {}
        self.outs = {}
        self.ld = self.al.sem("s_ld")
        self.ls = self.al.sem("s_ls")
        self.pe = self.al.sem("s_pe")
        self.dv = self.al.sem("s_dv")
        self.sc = self.al.sem("s_sc")
        self.gp = self.al.sem("s_gp")
        self.st = self.al.sem("s_st")
        self.cc = self.al.sem("s_cc")

    def din(self, name, shape, dtype=F32):
        if name in self.ins:
            return self.ins[name]
        ap = self.nc.dram_tensor(name, list(shape), dtype, kind="ExternalInput").ap()
        self.ins[name] = ap
        return ap

    def dint(self, name, shape, dtype=F32, addr_space="Local"):
        ap = self.nc.dram_tensor(name, list(shape), dtype, addr_space=addr_space).ap()
        return ap

    def barrier(self):
        p = self.p
        sems = [self.ld, self.ls, self.pe, self.dv, self.sc, self.gp, self.st]
        for eng in ENGINES:
            for sm in sems:
                p.wait(eng, sm)

    def dout(self, name, shape, dtype=F32):
        ap = self.nc.dram_tensor(name, list(shape), dtype, kind="ExternalOutput").ap()
        self.outs[name] = ap
        return ap

    def common(self):
        _st, _pf = self.al.stage_stack, self.al.prefix
        self.al.stage_stack, self.al.prefix = None, ""
        self.ident = self.al.sb("c_ident", [128, 128], BF16)
        self.ones = self.al.sb("c_ones", [128, 128], BF16)
        self.ones32 = self.al.sb("c_ones32", [1, 128], F32)
        self.al.stage_stack, self.al.prefix = _st, _pf
        self.p.op("vector", "memset", self.ones[:, :], 1.0, sem=self.dv)
        self.p.op("vector", "memset", self.ones32[:, :], 1.0, sem=self.dv)
        self.p.op("sync", "dma_start", sem=self.ls, n=16, out=self.ident[:, :],
                  in_=self.ins["ident"][:, :])

    def finish(self):
        self.p.wait("sync", self.st)
        blk = self.al.block()
        self.p.finalize(blk)
        self.al.close()
        return self.nc


class SlotPool:

    def __init__(self, tiles):
        self.tiles = [[t, None] for t in tiles]
        self.i = 0

    def take(self):
        t = self.tiles[self.i % len(self.tiles)]
        self.i += 1
        return t



def load_weight(ctx, wsb, col0, dram, fan_in, fan_out, after=()):
    nk = fan_in // 128
    return ctx.p.op("gpsimd", "dma_start", sem=ctx.ld, n=16, after=after,
                    out=wsb[:, col0:col0 + nk * fan_out].rearrange("p (k m) -> p k m", k=nk),
                    in_=dram.rearrange("(k p) m -> p k m", p=128))


def wslice(wsb, col0, fan_out, k, m, mw=128):
    return wsb[:, col0 + k * fan_out + m * 128:col0 + k * fan_out + m * 128 + mw]


def emit_ln(ctx, x_tile_ap, lnx_tm_ap, scr_sq, stats, toks, eps=1e-5):
    p = ctx.p
    prev = getattr(ctx, "ln_prev", ())
    t_sq = p.op("scalar", "activation", scr_sq[:, :], x_tile_ap, AF.Square,
                accum_out=stats["ssq"][:, :], sem=ctx.sc, after=tuple(toks) + tuple(prev))
    t_sm = p.op("vector", "reduce_sum", stats["smean"][:, :], x_tile_ap, axis=AX.X,
                sem=ctx.dv, after=tuple(toks) + tuple(prev))
    t_m = p.op("vector", "tensor_scalar_mul", stats["m"][:, :], stats["smean"][:, :],
               1.0 / D, sem=ctx.dv, after=(t_sm,))
    t_msq = p.op("vector", "tensor_mul", stats["var"][:, :], stats["m"][:, :],
                 stats["m"][:, :], sem=ctx.dv, after=(t_m,))
    t_v = p.op("vector", "scalar_tensor_tensor", stats["var"][:, :], stats["ssq"][:, :],
               1.0 / D, stats["var"][:, :], ALU.mult, ALU.subtract, sem=ctx.dv,
               after=(t_msq, t_sq))
    t_e = p.op("vector", "tensor_scalar_add", stats["var"][:, :], stats["var"][:, :],
               eps, sem=ctx.dv, after=(t_v,))
    t_s = p.op("scalar", "activation", stats["rstd"][:, :], stats["var"][:, :], AF.Sqrt,
               sem=ctx.sc, after=(t_e,))
    t_r = p.op("vector", "reciprocal", stats["rstd"][:, :], stats["rstd"][:, :],
               sem=ctx.dv, after=(t_s,))
    tn = p.op("vector", "tensor_scalar", lnx_tm_ap, x_tile_ap, stats["m"][:, :],
              stats["rstd"][:, :], ALU.subtract, ALU.mult, sem=ctx.dv, after=(t_r,))
    ctx.ln_prev = (tn, t_sq, t_s)
    return tn


def transpose_group(ctx, srcs, dst_ap, ps_slot):
    p = ctx.p
    ps, war = ps_slot
    n = len(srcs)
    if war is not None:
        p.wait_tok("tensor", war)
    tok = None
    for j, s in enumerate(srcs):
        tok = p.op("tensor", "transpose", ps[:, j * 128:(j + 1) * 128], s,
                   ctx.ident[:, :], sem=(ctx.pe if j == n - 1 else None))
    ev = p.op("vector", "tensor_copy", dst_ap, ps[:, 0:n * 128], sem=ctx.dv, after=(tok,))
    ps_slot[1] = ev
    return ev



def _bf(x):
    return np.ascontiguousarray(x).astype(BF)


def _perm_ri(n_pairs):
    return np.concatenate([np.arange(0, 2 * n_pairs, 2), np.arange(1, 2 * n_pairs, 2)])


def host_prep(inp):
    g = {}
    x = np.ascontiguousarray(inp["L0_attn_x"].reshape(SEQ, D))
    g["x"] = x
    g["w_cq"] = np.ascontiguousarray(inp["L0_attn_w_cq"])
    g["w_dq_nope"] = np.ascontiguousarray(inp["L0_attn_w_dq_nope"])
    p64 = _perm_ri(32)
    wqr = inp["L0_attn_w_dq_rope"]
    g["w_dq_rope_p"] = np.ascontiguousarray(
        wqr.reshape(1536, 16, 64)[:, :, p64].reshape(1536, 1024))
    g["w_ckv"] = np.ascontiguousarray(inp["L0_attn_w_ckv"])
    g["w_dk_nope"] = np.ascontiguousarray(inp["L0_attn_w_dk_nope"])
    g["w_dv"] = np.ascontiguousarray(inp["L0_attn_w_dv"])
    g["w_k_rope_p"] = np.ascontiguousarray(inp["L0_attn_w_k_rope"][:, p64])
    g["w_proj"] = np.ascontiguousarray(inp["L0_attn_w_proj"])
    g["qg"] = np.ascontiguousarray(inp["L0_attn_q_norm_w"].reshape(12, 128).T)
    g["kvg"] = np.ascontiguousarray(inp["L0_attn_kv_norm_w"].reshape(4, 128).T)
    g["crow"] = np.ascontiguousarray(np.tile(inp["L0_attn_freqs_cos"], (1, 2)).T)
    g["srow"] = np.ascontiguousarray(np.tile(inp["L0_attn_freqs_sin"], (1, 2)).T)
    g["ident"] = _bf(np.eye(128, dtype=np.float32))
    psw = np.zeros((128, 128), np.float32)
    for j in range(64):
        psw[j, j + 64] = 1.0
        psw[j + 64, j] = 1.0
    g["pswap"] = _bf(psw)
    g["i64"] = _bf(np.eye(64, dtype=np.float32))
    msw = np.zeros((64, 64), np.float32)
    for j in range(32):
        msw[j, j + 32] = 1.0
        msw[j + 32, j] = -1.0
    g["mswap"] = _bf(msw)
    g["W0T"] = np.ascontiguousarray(inp["L0_ffn_W0"].T)
    g["b0col"] = np.ascontiguousarray(inp["L0_ffn_bias0"].reshape(64, 128).T)
    g["W1T"] = np.ascontiguousarray(inp["L0_ffn_W1"].T)
    g["b1row"] = np.ascontiguousarray(inp["L0_ffn_bias1"].reshape(1, D))
    g["Wqkv"] = np.ascontiguousarray(inp["L1_attn_Wqkv"])
    g["qg1"] = np.ascontiguousarray(inp["L1_attn_q_norm_w"].reshape(16, 128).T)
    g["kg1"] = np.ascontiguousarray(inp["L1_attn_k_norm_w"].reshape(16, 128).T)
    g["Wo"] = np.ascontiguousarray(inp["L1_attn_Wo"])
    hd = 128
    inv = 1.0 / (10000.0 ** (np.arange(0, hd, 2, dtype=np.float32) / hd))
    pos = np.arange(SEQ, dtype=np.float32)
    fr = pos[:, None] * inv[None, :]
    ct = np.cos(fr).astype(np.float32).T
    st = np.sin(fr).astype(np.float32).T
    g["cosT"] = np.ascontiguousarray(np.concatenate([ct, ct], 0))
    g["sinT"] = np.ascontiguousarray(np.concatenate([-st, st], 0))
    g["rwT"] = np.ascontiguousarray(inp["L1_ffn_router_weight"].T)
    g["rbrow"] = np.ascontiguousarray(inp["L1_ffn_router_bias"].reshape(1, 8))
    gu = inp["L1_ffn_gate_up_proj"]
    g["gw"] = np.ascontiguousarray(gu[:, :, 0::2])
    g["uw"] = np.ascontiguousarray(gu[:, :, 1::2])
    gub = inp["L1_ffn_gate_up_proj_bias"]
    g["gbcol"] = np.ascontiguousarray(gub[:, 0::2].reshape(8, 16, 128).transpose(0, 2, 1))
    g["ubcol"] = np.ascontiguousarray(gub[:, 1::2].reshape(8, 16, 128).transpose(0, 2, 1))
    g["dw"] = np.ascontiguousarray(inp["L1_ffn_down_proj"])
    g["dbrow"] = np.ascontiguousarray(inp["L1_ffn_down_proj_bias"])
    kk = np.arange(512)[:, None]
    qq = np.arange(512)[None, :]
    mC = np.where(qq >= kk, 0.0, NEG).astype(np.float32)
    mW = np.where(qq < kk, 0.0, NEG).astype(np.float32)
    g["maskC"] = np.ascontiguousarray(mC.reshape(4, 128, 512).transpose(1, 0, 2).reshape(128, 2048))
    g["maskW"] = np.ascontiguousarray(mW.reshape(4, 128, 512).transpose(1, 0, 2).reshape(128, 2048))
    return g




def rmsnorm_post2(ctx, ps_st_region, t, strow, r128, bslot, fan_in, eps, stat_tok):
    p = ctx.p
    c0 = t * 512
    ps_bc, bwar = bslot
    t1 = p.op("vector", "tensor_scalar", strow[:, c0:c0 + 512], ps_st_region,
              1.0 / fan_in, eps, ALU.mult, ALU.add, sem=ctx.dv,
              after=(stat_tok,) + ((ctx.strowg.get(t),) if ctx.strowg.get(t) else ()))
    ctx.psg = t1
    t2 = p.op("scalar", "activation", strow[:, c0:c0 + 512], strow[:, c0:c0 + 512],
              AF.Sqrt, sem=ctx.sc, after=(t1,))
    t3 = p.op("vector", "reciprocal", strow[:, c0:c0 + 512], strow[:, c0:c0 + 512],
              sem=ctx.dv, after=(t2,))
    if bwar is not None:
        p.wait_tok("tensor", bwar)
    t4 = p.op("tensor", "matmul", ps_bc[:, :], ctx.ones32[0:1, 0:128],
              strow[0:1, c0:c0 + 512], start=True, stop=True, sem=ctx.pe, after=(t3,))
    ctx.strowg[t] = t4
    t5 = p.op("vector", "tensor_copy", r128[:, c0:c0 + 512], ps_bc[:, :],
              sem=ctx.dv, after=(t4,))
    bslot[1] = t5
    return t5


def emit_qk_heads(ctx, t, wsb, wcol, nk, rhs_fn, rope_rhs_fn, rope_lhs_fn, o_n, o_r, o_n2,
                  pool, epool, erpool, rpool, sqpool, sq64p, ps_st, str2, rot_tok,
                  rope_src_tok):
    p = ctx.p
    pending = [None]
    for h in range(16):
        slot = pool.take()
        ps, war = slot
        if war is not None:
            p.wait_tok("tensor", war)
        mm = None
        for k in range(nk):
            mm = p.op("tensor", "matmul", ps[:, :], wslice(wsb, wcol, 2048, k, h),
                      rhs_fn(k, t), start=(k == 0), stop=(k == nk - 1),
                      sem=(ctx.pe if k == nk - 1 else None))
        if pending[0] is not None:
            pending[0]()
            pending[0] = None
        es = epool.take()
        ev, ewar = es
        tev = p.op("scalar", "activation", ev[:, :], ps[:, :], AF.Identity, sem=ctx.sc,
                   after=(mm,) + ((ewar,) if ewar is not None else ()))
        es[1] = p.op("sync", "dma_start", sem=ctx.st, n=16,
                     out=o_n[h * 128:(h + 1) * 128, t * 512:(t + 1) * 512],
                     in_=ev[:, :], after=(tev,))
        sqslot = sqpool.take()
        sqa, sqwar = sqslot
        tsq1 = p.op("scalar", "activation", sqa, ps[:, :], AF.Square, sem=ctx.sc,
                    after=(mm,) + ((sqwar,) if sqwar is not None else ()))
        slot[1] = tsq1
        rslot = rpool.take()
        rps, rwar = rslot
        if rwar is not None:
            p.wait_tok("tensor", rwar)
        p.wait_tok("tensor", rot_tok)
        p.wait_tok("tensor", rope_src_tok)
        rmm = p.op("tensor", "matmul", rps, rope_lhs_fn(h),
                   rope_rhs_fn(h, t), start=True, stop=True, sem=ctx.pe)
        ers = erpool.take()
        er, erwar = ers
        trev = p.op("scalar", "activation", er, rps, AF.Identity, sem=ctx.sc,
                    after=(rmm,) + ((erwar,) if erwar is not None else ()))
        ers[1] = p.op("sync", "dma_start", sem=ctx.st, n=16,
                      out=o_r[h * 64:(h + 1) * 64, t * 512:(t + 1) * 512],
                      in_=er, after=(trev,))
        s64 = sq64p.take()
        sqb, sqbwar = s64
        tsq2 = p.op("scalar", "activation", sqb, er, AF.Square, sem=ctx.sc,
                    after=(trev,) + ((sqbwar,) if sqbwar is not None else ()))
        rslot[1] = trev

        def mk_stats(h=h, tsq1=tsq1, tsq2=tsq2, sqa=sqa, sqb=sqb,
                     sqslot=sqslot, s64=s64):
            def emit():
                ts1 = p.op("tensor", "matmul", ps_st[0:1, :], ctx.ones[:, 0:1], sqa,
                           start=True, stop=False, sem=ctx.pe,
                           after=(tsq1,) + ((ctx.psg,) if ctx.psg else ()))
                sqslot[1] = ts1
                ts2 = p.op("tensor", "matmul", ps_st[0:1, :], ctx.ones[0:64, 0:1], sqb,
                           start=False, stop=True, sem=ctx.pe, after=(tsq2,))
                s64[1] = ts2
                tse = p.op("vector", "tensor_copy", str2[0:1, :], ps_st[0:1, :], sem=ctx.dv,
                           after=(ts2,) + ((ctx.str2g,) if ctx.str2g else ()))
                ctx.psg = tse
                ctx.str2g = p.op("sync", "dma_start", sem=ctx.st, n=16,
                                 out=o_n2[h:h + 1, t * 512:(t + 1) * 512],
                                 in_=str2[0:1, :], after=(tse,))
            return emit
        pending[0] = mk_stats()
    pending[0]()


def build_l0pre(mega=None):
    ctx = mega["ctx"] if mega else Ctx("l0pre")
    ctx.al.begin_stage("a")
    ctx.psg = None
    ctx.str2g = None
    ctx.strowg = {0: None, 1: None}
    p, al = ctx.p, ctx.al
    x = ctx.din("x_sl", [T, D])
    w_cq_d = ctx.din("w_cq", [D, 1536])
    w_dqn_d = ctx.din("w_dq_nope", [1536, 2048])
    w_dqr_d = ctx.din("w_dq_rope_p", [1536, 1024])
    w_ckv_d = ctx.din("w_ckv", [D, 512])
    w_dkn_d = ctx.din("w_dk_nope", [512, 2048])
    w_dv_d = ctx.din("w_dv", [512, 2048])
    w_krope_d = ctx.din("w_k_rope_p", [D, 64])
    qg_in = ctx.din("qg", [128, 12])
    kvg_in = ctx.din("kvg", [128, 4])
    crow_in = ctx.din("crow", [64, 16])
    srow_in = ctx.din("srow", [64, 16])
    ctx.din("ident", [128, 128], BF16)
    i64_in = ctx.din("i64", [64, 64], BF16)
    mswap_in = ctx.din("mswap", [64, 64], BF16)

    if mega:
        io = mega["io"]
        o_qn, o_qr, o_kn, o_kr = io["Qn0"], io["Qr0"], io["Kn0"], io["Kr0"]
        o_v, o_qn2, o_kn2 = io["V0"], io["qn20"], io["kn20"]
    else:
        o_qn = ctx.dout("Qn", [2048, T], BF16)
        o_qr = ctx.dout("Qr", [1024, T], BF16)
        o_kn = ctx.dout("Kn", [2048, T], BF16)
        o_kr = ctx.dout("Kr", [1024, T], BF16)
        o_v = ctx.dout("V", [T, 2048], BF16)
        o_qn2 = ctx.dout("qn2", [16, T])
        o_kn2 = ctx.dout("kn2", [16, T])

    if not hasattr(ctx, "ident"):
        ctx.common()
    WA, WB = 0, 24576
    R_CKV, R_KR, R_DKN, R_DV = 0, 8192, 9216, 24576
    wsb = al.sb("wsb", [128, WB + 12288], BF16)
    lnx = al.sb("lnx", [128, 16 * T], BF16)
    lnx_tm = al.sb("lnx_tm", [128, D], BF16)
    xt = al.sb("xt", [128, D], F32)
    nq = al.sb("nq", [128, 12 * T], BF16)
    nkv = al.sb("nkv", [128, 4 * T], BF16)
    base = al.sb("base", [64, T], BF16)
    rot = al.sb("rot", [128, 16 * 64], BF16)
    big = al.sb("big", [128, 8 * T], BF16)
    ev4 = al.sb("ev4", [128, 4 * 512], BF16)
    er2 = al.sb("er2", [64, 2 * 512], BF16)
    sqs = al.sb("sqs", [128, 2 * 512], BF16)
    sq64s = al.sb("sq64s", [64, 2 * 512], BF16)
    vtm = al.sb("vtm", [128, 2 * 2048], BF16)
    strow = al.sb("strow", [1, 2 * 512], F32)
    str2 = al.sb("str2", [1, 512], F32)
    i64 = al.sb("sb_i64", [64, 64], BF16)
    mswap = al.sb("sb_mswap", [64, 64], BF16)
    crow = al.sb("sb_crow", [64, 16], F32)
    srow = al.sb("sb_srow", [64, 16], F32)
    qg = al.sb("sb_qg", [128, 12], F32)
    kvg = al.sb("sb_kvg", [128, 4], F32)
    scr_sq = al.sb("scr_sq", [128, D], BF16)
    stats = {k: al.sb("st_" + k, [128, 1], F32) for k in ["ssq", "smean", "m", "var", "rstd"]}
    r128 = al.sb("r128", [128, T], F32)
    ps_a = al.ps("ps_a", [128, 512])
    ps_b = al.ps("ps_b", [128, 512])
    ps_tr = al.ps("ps_tr", [128, 2048], BF16)
    ps_st = al.ps("ps_st", [1, 512])
    ps_bc = al.ps("ps_bc", [128, 512])
    ps_r = al.ps("ps_r", [64, 1024])
    pool = SlotPool([ps_a, ps_b])
    epool = SlotPool([ev4[:, i * 512:(i + 1) * 512] for i in range(4)])
    erpool = SlotPool([er2[:, 0:512], er2[:, 512:1024]])
    trpool = SlotPool([ps_tr[:, 0:512], ps_tr[:, 1024:1536]])
    rpool = SlotPool([ps_r[:, 0:512], ps_r[:, 512:1024]])
    sqpool = SlotPool([sqs[:, 0:512], sqs[:, 512:1024]])
    sq64p = SlotPool([sq64s[:, 0:512], sq64s[:, 512:1024]])
    bcslot = SlotPool([ps_bc])

    for dst, src in ((i64, i64_in), (mswap, mswap_in), (crow, crow_in), (srow, srow_in),
                     (qg, qg_in), (kvg, kvg_in)):
        p.op("sync", "dma_start", sem=ctx.ls, n=16, out=dst[:, :], in_=src[:, :])
    p.wait("vector", ctx.ls)
    tsrow = p.tok(ctx.dv)
    rot_tok = None
    for h in range(16):
        t1 = p.op("vector", "tensor_scalar_mul", rot[0:64, h * 64:(h + 1) * 64], i64[:, :],
                  crow[:, h:h + 1], sem=ctx.dv, after=(tsrow,))
        rot_tok = p.op("vector", "scalar_tensor_tensor", rot[0:64, h * 64:(h + 1) * 64],
                       mswap[:, :], srow[:, h:h + 1], rot[0:64, h * 64:(h + 1) * 64],
                       ALU.mult, ALU.add, sem=ctx.dv, after=(t1,))
    rot_tok = p.op("sync", "dma_start", sem=ctx.ls, n=16, out=rot[64:128, :],
                   in_=rot[0:64, :], after=(rot_tok,))

    tw_ckv = load_weight(ctx, wsb, R_CKV, w_ckv_d, D, 512)
    tw_kr = load_weight(ctx, wsb, R_KR, w_krope_d, D, 64)
    tw_dkn = load_weight(ctx, wsb, R_DKN, w_dkn_d, 512, 2048)
    tw_dv = load_weight(ctx, wsb, R_DV, w_dv_d, 512, 2048)
    xt_war = [None, None]
    tm_war = [None, None]
    lnx_ready = None
    for a in range(8):
        sl = 0
        tx = p.op("sync", "dma_start", sem=ctx.ls, n=16, out=xt[:, :],
                  in_=x[a * 128:(a + 1) * 128, :],
                  after=((xt_war[sl],) if xt_war[sl] is not None else ()))
        tn = emit_ln(ctx, xt[:, :], lnx_tm[:, :],
                     scr_sq, stats, (tx,) + ((tm_war[sl],) if tm_war[sl] is not None else ()))
        xt_war[sl] = tn
        p.wait_tok("tensor", tn)
        for g4 in range(4):
            srcs = [lnx_tm[:, (g4 * 4 + j) * 128:(g4 * 4 + j) * 128 + 128]
                    for j in range(4)]
            dst = lnx[:, :].rearrange("p (k t) -> p k t", k=16)[:, g4 * 4:(g4 + 1) * 4,
                                                               a * 128:(a + 1) * 128]
            lnx_ready = transpose_group(ctx, srcs, dst, trpool.take())
        tm_war[sl] = p.tok(ctx.pe)

    def lnx_rhs(k, t):
        ctx.p.wait_tok("tensor", lnx_ready)
        return lnx[:, k * T + t * 512:k * T + (t + 1) * 512]

    def rms_proj(w_col, fan_out, nkk, rhs_fn, dst, dst_nk, gain, wtok, fan_in_true):
        mods = []
        for t in range(2):
            stat_tok = None
            pending = None
            for m in range(dst_nk):
                slot = pool.take()
                ps, war = slot
                p.wait_tok("tensor", wtok)
                if war is not None:
                    p.wait_tok("tensor", war)
                mm = None
                for k in range(nkk):
                    mm = p.op("tensor", "matmul", ps[:, :], wslice(wsb, w_col, fan_out, k, m),
                              rhs_fn(k, t), start=(k == 0), stop=(k == nkk - 1),
                              sem=(ctx.pe if k == nkk - 1 else None))
                if pending is not None:
                    stat_tok = pending()
                tev = p.op("vector", "tensor_copy", dst[:, m * T + t * 512:m * T + (t + 1) * 512],
                           ps[:, :], sem=ctx.dv, after=(mm,))
                sqslot = sqpool.take()
                sqa, sqwar = sqslot
                tsq = p.op("scalar", "activation", sqa, ps[:, :], AF.Square, sem=ctx.sc,
                           after=(mm, tev) + ((sqwar,) if sqwar is not None else ()))
                slot[1] = (tev, tsq)

                def mk_stat(m=m, tsq=tsq, sqa=sqa, sqslot=sqslot):
                    def emit():
                        tk = p.op("tensor", "matmul", ps_st[0:1, :], ctx.ones[:, 0:1], sqa,
                                  start=(m == 0), stop=(m == dst_nk - 1), sem=ctx.pe,
                                  after=(tsq,) + ((ctx.psg,) if (m == 0 and ctx.psg) else ()))
                        sqslot[1] = tk
                        return tk
                    return emit
                pending = mk_stat()
            stat_tok = pending()
            tr = rmsnorm_post2(ctx, ps_st[0:1, :], t, strow, r128, bcslot.take(),
                               fan_in_true, 1e-5, stat_tok)
            for m in range(dst_nk):
                tg1 = p.op("vector", "tensor_scalar_mul",
                           dst[:, m * T + t * 512:m * T + (t + 1) * 512],
                           dst[:, m * T + t * 512:m * T + (t + 1) * 512],
                           gain[:, m:m + 1], sem=ctx.dv, after=(tr,))
                tg2 = p.op("vector", "tensor_mul",
                           dst[:, m * T + t * 512:m * T + (t + 1) * 512],
                           dst[:, m * T + t * 512:m * T + (t + 1) * 512],
                           r128[:, t * 512:(t + 1) * 512], sem=ctx.dv, after=(tg1,))
                mods.append(tg2)
        return mods

    nkv_mods = rms_proj(R_CKV, 512, 16, lnx_rhs, nkv, 4, kvg, tw_ckv, 512)

    def nkv_rhs(k, t):
        for tk in nkv_mods:
            ctx.p.wait_tok("tensor", tk)
        return nkv[:, k * T + t * 512:k * T + (t + 1) * 512]

    base_tok = None
    for t in range(2):
        slot = pool.take()
        ps, war = slot
        p.wait_tok("tensor", tw_kr)
        if war is not None:
            p.wait_tok("tensor", war)
        mm = None
        for k in range(16):
            mm = p.op("tensor", "matmul", ps[0:64, :], wslice(wsb, R_KR, 64, k, 0, 64),
                      lnx_rhs(k, t), start=(k == 0), stop=(k == 15),
                      sem=(ctx.pe if k == 15 else None))
        base_tok = p.op("scalar", "activation", base[:, t * 512:(t + 1) * 512], ps[0:64, :],
                        AF.Identity, sem=ctx.sc, after=(mm,))
        slot[1] = base_tok


    def k_rope_rhs(h, t):
        return base[:, t * 512:(t + 1) * 512]

    for t in range(2):
        p.wait_tok("tensor", tw_dkn)
        emit_qk_heads(ctx, t, wsb, R_DKN, 4, nkv_rhs, k_rope_rhs,
                      lambda h: rot[0:64, h * 64:(h + 1) * 64],
                      o_kn, o_kr, o_kn2,
                      pool, epool, erpool, rpool, sqpool, sq64p, ps_st, str2,
                      rot_tok, base_tok)
    k_done = p.tok(ctx.pe)
    tw_cq = load_weight(ctx, wsb, WA, w_cq_d, D, 1536, after=(k_done,))

    big_war = None
    vtm_war2 = [None, None]
    for t in range(2):
        vfm_tok = None
        for m in range(16):
            slot = pool.take()
            ps, war = slot
            p.wait_tok("tensor", tw_dv)
            if war is not None:
                p.wait_tok("tensor", war)
            mm = None
            for k in range(4):
                mm = p.op("tensor", "matmul", ps[:, :], wslice(wsb, R_DV, 2048, k, m),
                          nkv_rhs(k, t), start=(k == 0), stop=(k == 3),
                          sem=(ctx.pe if k == 3 else None))
            vfm_tok = p.op("scalar", "activation", big[:, m * 512:m * 512 + 512],
                           ps[:, :], AF.Identity, sem=ctx.sc, after=(mm, big_war))
            slot[1] = vfm_tok
        p.wait_tok("tensor", vfm_tok)
        for a in range(4):
            sl = a % 2
            if vtm_war2[sl] is not None:
                p.wait_tok("vector", vtm_war2[sl])
            for g4 in range(4):
                srcs = [big[:, (g4 * 4 + j) * 512 + a * 128:(g4 * 4 + j) * 512 + a * 128 + 128]
                        for j in range(4)]
                dst = vtm[:, sl * 2048 + g4 * 512:sl * 2048 + (g4 + 1) * 512]
                tg = transpose_group(ctx, srcs, dst, trpool.take())
            vtm_war2[sl] = p.op("sync", "dma_start", sem=ctx.st, n=16,
                                out=o_v[t * 512 + a * 128:t * 512 + (a + 1) * 128, :],
                                in_=vtm[:, sl * 2048:(sl + 1) * 2048], after=(tg,))
        big_war = p.tok(ctx.pe)
    v_done = p.tok(ctx.pe)
    if mega:
        p.wait("gpsimd", ctx.st)
        p.op("gpsimd", "collective_compute", "AllReduce", ALU.add,
             replica_groups=[list(range(NCORES))],
             ins=[mega["io"]["bar1"].opt()], outs=[mega["io"]["bar1o"].opt()],
             sem=ctx.cc, n=1)
        p.wait("gpsimd", ctx.cc)
        for src, dst in (("kn20", "kn2AG"), ("Kn0", "KnAG"), ("Kr0", "KrAG"), ("V0", "VAG")):
            p.op("gpsimd", "collective_compute", "AllGather", ALU.bypass,
                 replica_groups=[list(range(NCORES))],
                 ins=[mega["io"][src].opt()], outs=[mega["io"][dst].opt()],
                 sem=ctx.cc, n=1)
        mega["cc1"] = p.tok(ctx.cc)
        mega["cc_kn2"] = (ctx.cc, mega["cc1"][1] - 3)

    nq_mods = rms_proj(WA, 1536, 16, lnx_rhs, nq, 12, qg, tw_cq, 1536)
    nq_pe_done = p.tok(ctx.pe)

    def nq_rhs(k, t):
        for tk in nq_mods:
            ctx.p.wait_tok("tensor", tk)
        return nq[:, k * T + t * 512:k * T + (t + 1) * 512]

    tw_qr = load_weight(ctx, wsb, WB, w_dqr_d, 1536, 1024, after=(v_done,))
    for t in range(2):
        for m in range(8):
            slot = pool.take()
            ps, war = slot
            p.wait_tok("tensor", tw_qr)
            if war is not None:
                p.wait_tok("tensor", war)
            mm = None
            for k in range(12):
                mm = p.op("tensor", "matmul", ps[:, :], wslice(wsb, WB, 1024, k, m),
                          nq_rhs(k, t), start=(k == 0), stop=(k == 11),
                          sem=(ctx.pe if k == 11 else None))
            slot[1] = p.op("scalar", "activation",
                           big[:, m * T + t * 512:m * T + (t + 1) * 512],
                           ps[:, :], AF.Identity, sem=ctx.sc, after=(mm, v_done))
    prerot_done = p.tok(ctx.sc)
    prerot_pe = p.tok(ctx.pe)

    tw_qn = load_weight(ctx, wsb, WA, w_dqn_d, 1536, 2048, after=(nq_pe_done,))

    def q_rope_rhs(h, t):
        c, r0 = h // 2, (h % 2) * 64
        return big[r0:r0 + 64, c * T + t * 512:c * T + (t + 1) * 512]

    for t in range(2):
        p.wait_tok("tensor", tw_qn)
        emit_qk_heads(ctx, t, wsb, WA, 12, nq_rhs, q_rope_rhs,
                      lambda h: rot[(h % 2) * 64:(h % 2) * 64 + 64, h * 64:(h + 1) * 64],
                      o_qn, o_qr, o_qn2,
                      pool, epool, erpool, rpool, sqpool, sq64p, ps_st, str2,
                      rot_tok, prerot_done)
    q_done = p.tok(ctx.pe)

    if mega:
        ctx.barrier()
        ctx.al.end_stage()
    else:
        ctx.al.end_stage()
    return ctx



def run_l0pre(g):
    ctx = build_l0pre()
    nc = ctx.finish()
    shared = {k: g[k] for k in ["w_cq", "w_dq_nope", "w_dq_rope_p", "w_ckv", "w_dk_nope",
                                "w_dv", "w_k_rope_p", "qg", "kvg", "crow", "srow",
                                "ident", "i64", "mswap"]}
    in_maps = []
    for i in range(NCORES):
        m = dict(shared)
        m["x_sl"] = g["x"][i * T:(i + 1) * T]
        in_maps.append(m)
    res = run_bass_kernel_spmd(nc, in_maps, core_ids=list(range(NCORES)))
    return res.results


ATT_SCALE0 = 1.0 / float(np.sqrt(192.0))
VSCALE0 = 1.0 / float(np.sqrt(2048.0))


def build_l0attn(mega=None):
    ctx = mega["ctx"] if mega else Ctx("l0attn")
    ctx.al.begin_stage("b")
    p, al = ctx.p, ctx.al
    x = ctx.din("x_sl", [T, D])
    if mega:
        io = mega["io"]
        qn_d, qr_d, qn2_d = io["Qn0"], io["Qr0"], io["qn20"]
        kn0, kr0, v0 = io["Kn0"], io["Kr0"], io["V0"]
        knag, krag, vag, kn2ag = io["KnAG"], io["KrAG"], io["VAG"], io["kn2AG"]
        o_r1 = io["r1"]
        mscr = io["mscrB"]
    else:
        qn_d = ctx.din("Qn", [2048, T], BF16)
        qr_d = ctx.din("Qr", [1024, T], BF16)
        knm_d = ctx.din("KnM", [2048, 7680], BF16)
        krm_d = ctx.din("KrM", [1024, 7680], BF16)
        vm_d = ctx.din("VM", [7680, 2048], BF16)
        knd_d = ctx.din("KnD", [2, 2048, 512], BF16)
        krd_d = ctx.din("KrD", [2, 1024, 512], BF16)
        vd_d = ctx.din("VD", [2, 512, 2048], BF16)
        qn2_d = ctx.din("qn2", [16, T])
        kn2_d = ctx.din("kn2f", [16, SEQ])
        o_r1 = ctx.dout("resid1", [T, D])
        mscr = ctx.nc.dram_tensor("mscr", [16, 2], F32, kind="Internal").ap()
    km_d = ctx.din("keymask", [128, 2 * 15])
    maskc_d = ctx.din("maskC", [128, 2048])
    wproj_d = ctx.din("w_proj", [D, D])
    ctx.din("ident", [128, 128], BF16)

    if not hasattr(ctx, "ident"):
        ctx.common()
    wsb = al.sb("wsb", [128, 16 * 2048], BF16)
    qn_sb = al.sb("qn_sb", [128, 16 * T], BF16)
    qr_sb = al.sb("qr_sb", [64, 16 * T], BF16)
    pT = al.sb("pT", [128, 2 * 2048], BF16)
    attn = al.sb("attn", [128, 16 * 512], BF16)
    maskc = al.sb("sb_maskc", [128, 2048], F32)
    keymask = al.sb("sb_keymask", [128, 2 * 15], F32)
    biasC = al.sb("biasC", [128, 32 * 15], F32)
    smask = al.sb("smask", [128, 2048], F32)
    kn_t = al.sb("kn_t", [128, 3 * 512], BF16)
    kr_t = al.sb("kr_t", [64, 3 * 512], BF16)
    v_t = al.sb("v_t", [128, 3 * 512], BF16)
    lsum = al.sb("lsum", [128, 2 * 512], BF16)
    lsum2 = al.sb("lsum2", [128, 512], BF16)
    linv = al.sb("linv", [128, 512], F32)
    rl = al.sb("rl", [1, 512], F32)
    xrt = al.sb("xrt", [128, 2 * 512], F32)
    rout = al.sb("rout", [128, 2 * 512], F32)
    qn2 = al.sb("sb_qn2", [16, T], F32)
    k2r = al.sb("k2r", [16, T], F32)
    kmax2 = al.sb("kmax2", [16, 1], F32)
    kred = al.sb("kred", [16, 1], F32)
    qmax2 = al.sb("qmax2", [16, 2], F32)
    negM = al.sb("negM", [16, 2], F32)
    mrow = al.sb("mrow", [1, 32], F32)
    Mb = al.sb("Mb", [128, 32], F32)
    ps_sa = al.ps("ps_sa", [128, 1024])
    ps_sb = al.ps("ps_sb", [128, 1024])
    ps_pv = al.ps("ps_pv", [128, 512])
    ps_l = al.ps("ps_l", [1, 512])
    ps_bc = al.ps("ps_bc", [128, 512])
    ps_o = al.ps("ps_o", [128, 512])

    tw = p.op("gpsimd", "dma_start", sem=ctx.ld, n=16,
              out=wsb[:, :].rearrange("p (k m) -> p k m", k=16),
              in_=wproj_d.rearrange("(k p) m -> p k m", p=128))
    p.op("sync", "dma_start", sem=ctx.ls, n=16,
         out=qn_sb[:, :].rearrange("p (h t) -> p h t", h=16),
         in_=qn_d.rearrange("(h p) t -> p h t", p=128))
    p.op("sync", "dma_start", sem=ctx.ls, n=16,
         out=qr_sb[:, :].rearrange("p (h t) -> p h t", h=16),
         in_=qr_d.rearrange("(h p) t -> p h t", p=64))
    p.op("sync", "dma_start", sem=ctx.ls, n=16, out=maskc[:, :], in_=maskc_d[:, :])
    tkm = p.op("sync", "dma_start", sem=ctx.ls, n=16, out=keymask[:, :], in_=km_d[:, :])
    tq3 = p.op("sync", "dma_start", sem=ctx.ls, n=16, out=qn2[:, :], in_=qn2_d[:, :])
    q_ready = p.tok(ctx.ls)

    km_tok = None
    for c in range(8):
        src = kn2ag[c * 16:(c + 1) * 16, :] if mega else kn2_d[:, c * 1024:(c + 1) * 1024]
        tl = p.op("sync", "dma_start", sem=ctx.ls, n=16, out=k2r[:, :],
                  in_=src, after=((km_tok,) if km_tok else ()) +
                  ((mega["cc_kn2"],) if (mega and "cc_kn2" in mega) else ()))
        trd = p.op("vector", "reduce_max", kred[:, :], k2r[:, :], axis=AX.X,
                   sem=ctx.dv, after=(tl,))
        if c == 0:
            km_tok = p.op("vector", "tensor_copy", kmax2[:, :], kred[:, :], sem=ctx.dv,
                          after=(trd,))
        else:
            km_tok = p.op("vector", "tensor_max", kmax2[:, :], kmax2[:, :], kred[:, :],
                          sem=ctx.dv, after=(trd,))
    tq4 = None
    for q in range(2):
        tq4 = p.op("vector", "reduce_max", qmax2[:, q:q + 1], qn2[:, q * 512:(q + 1) * 512],
                   axis=AX.X, sem=ctx.dv, after=(tq3, km_tok))
    t5 = p.op("vector", "tensor_scalar_mul", negM[:, :], qmax2[:, :], kmax2[:, 0:1],
              sem=ctx.dv, after=(tq4,))
    t6 = p.op("scalar", "activation", negM[:, :], negM[:, :], AF.Sqrt, sem=ctx.sc, after=(t5,))
    t7 = p.op("vector", "tensor_scalar_mul", negM[:, :], negM[:, :], -ATT_SCALE0,
              sem=ctx.dv, after=(t6,))
    t8 = p.op("sync", "dma_start", sem=ctx.st, n=16, out=mscr[:, :], in_=negM[:, :], after=(t7,))
    t9 = p.op("sync", "dma_start", sem=ctx.ls, n=16, out=mrow[:, :],
              in_=mscr.rearrange("a b -> (a b)")[None, :], after=(t8,))
    t10 = p.op("tensor", "matmul", ps_bc[:, 0:32], ctx.ones32[0:1, 0:128], mrow[0:1, :],
               start=True, stop=True, sem=ctx.pe, after=(t9,))
    t11 = p.op("vector", "tensor_copy", Mb[:, :], ps_bc[:, 0:32], sem=ctx.dv, after=(t10,))
    bias_tok = None
    for h in range(16):
        for q in range(2):
            idx = h * 2 + q
            bias_tok = p.op("vector", "tensor_scalar_add",
                            biasC[:, idx * 15:(idx + 1) * 15],
                            keymask[:, q * 15:(q + 1) * 15], Mb[:, idx:idx + 1],
                            sem=ctx.dv, after=(t11, tkm))

    NSLOT = 16
    st_kbuf = {"war": [None, None, None], "exp_war": [None, None],
               "pT_war": [None, None], "smask_war": None, "psg": t11,
               "bc_war": t11, "attn_war": None}

    def emit_slot_loads(h, q, j):
        b = j % 3
        extra = ()
        if mega:
            if j < 15:
                extra = ((mega["cc1"],) if "cc1" in mega else ())
                r, c0 = j // 2, (j % 2) * 512
                kn_src = knag[r * 2048 + h * 128:r * 2048 + (h + 1) * 128, c0:c0 + 512]
                kr_src = krag[r * 1024 + h * 64:r * 1024 + (h + 1) * 64, c0:c0 + 512]
                v_src = vag[j * 512:(j + 1) * 512, h * 128:(h + 1) * 128]
            else:
                kn_src = kn0[h * 128:(h + 1) * 128, q * 512:(q + 1) * 512]
                kr_src = kr0[h * 64:(h + 1) * 64, q * 512:(q + 1) * 512]
                v_src = v0[q * 512:(q + 1) * 512, h * 128:(h + 1) * 128]
        elif j < 15:
            kn_src = knm_d[h * 128:(h + 1) * 128, j * 512:(j + 1) * 512]
            kr_src = krm_d[h * 64:(h + 1) * 64, j * 512:(j + 1) * 512]
            v_src = vm_d[j * 512:(j + 1) * 512, h * 128:(h + 1) * 128]
        else:
            kn_src = knd_d[q, h * 128:(h + 1) * 128, :]
            kr_src = krd_d[q, h * 64:(h + 1) * 64, :]
            v_src = vd_d[q, :, h * 128:(h + 1) * 128]
        war = ((st_kbuf["war"][b],) if st_kbuf["war"][b] else ()) + extra
        t1 = p.op("sync", "dma_start", sem=ctx.ls, n=16,
                  out=kn_t[:, b * 512:(b + 1) * 512], in_=kn_src, after=war)
        t2 = p.op("sync", "dma_start", sem=ctx.ls, n=16,
                  out=kr_t[:, b * 512:(b + 1) * 512], in_=kr_src, after=war)
        t3 = p.op("gpsimd", "dma_start", sem=ctx.ld, n=16,
                  out=v_t[:, b * 512:(b + 1) * 512].rearrange("p (c d) -> p c d", c=4),
                  in_=v_src.rearrange("(c p) d -> p c d", p=128), after=war)
        return (t1, t2, t3)

    def flash_head(h, q):
        hq = h * 2 + q
        par = (q * 16 + h) % 2
        pv_cur = ps_pv if par == 0 else ps_o
        qn_rhs = qn_sb[:, h * T + q * 512:h * T + (q + 1) * 512]
        qr_rhs = qr_sb[:, h * T + q * 512:h * T + (q + 1) * 512]
        prev = None
        ld_q = {0: emit_slot_loads(h, q, 0), 1: emit_slot_loads(h, q, 1),
                2: emit_slot_loads(h, q, 2)}
        for j in range(NSLOT + 1):
            if j < NSLOT:
                b = j % 3
                ld_toks = ld_q.pop(j)
                mm_a = mm_b = None
                for c in range(4):
                    sps = ps_sa if c < 2 else ps_sb
                    if c in (0, 2) and st_kbuf["exp_war"][0 if c < 2 else 1] is not None:
                        p.wait_tok("tensor", st_kbuf["exp_war"][0 if c < 2 else 1])
                    if c == 0:
                        for tt in ld_toks:
                            p.wait_tok("tensor", tt)
                        p.wait_tok("tensor", q_ready)
                    p.op("tensor", "matmul", sps[:, (c % 2) * 512:(c % 2 + 1) * 512],
                         kn_t[:, b * 512 + c * 128:b * 512 + (c + 1) * 128],
                         qn_rhs, start=True, stop=False)
                    mm = p.op("tensor", "matmul", sps[:, (c % 2) * 512:(c % 2 + 1) * 512],
                              kr_t[:, b * 512 + c * 128:b * 512 + (c + 1) * 128],
                              qr_rhs, start=False, stop=True,
                              sem=(ctx.pe if c in (1, 3) else None))
                    if c == 1:
                        mm_a = mm
                mm_b = mm
                pb = j % 2
                pw = (st_kbuf["pT_war"][pb],) if st_kbuf["pT_war"][pb] else ()
                bias_ap = (Mb[:, hq:hq + 1] if j == NSLOT - 1
                           else biasC[:, hq * 15 + j:hq * 15 + j + 1])
                if j == NSLOT - 1:
                    smw = (st_kbuf["smask_war"],) if st_kbuf["smask_war"] else ()
                    tm1 = p.op("vector", "tensor_add", smask[:, 0:1024],
                               ps_sa[:, :], maskc[:, 0:1024], sem=ctx.dv,
                               after=(mm_a, bias_tok) + smw)
                    tm2 = p.op("vector", "tensor_add", smask[:, 1024:2048],
                               ps_sb[:, :], maskc[:, 1024:2048], sem=ctx.dv,
                               after=(mm_b,))
                    e_a = p.op("scalar", "activation", pT[:, pb * 2048:pb * 2048 + 1024],
                               smask[:, 0:1024], AF.Exp, bias=bias_ap,
                               scale=ATT_SCALE0, sem=ctx.sc, after=(tm1,) + pw)
                    e_b = p.op("scalar", "activation",
                               pT[:, pb * 2048 + 1024:pb * 2048 + 2048],
                               smask[:, 1024:2048], AF.Exp, bias=bias_ap,
                               scale=ATT_SCALE0, sem=ctx.sc, after=(tm2,))
                    st_kbuf["smask_war"] = e_b
                else:
                    e_a = p.op("scalar", "activation", pT[:, pb * 2048:pb * 2048 + 1024],
                               ps_sa[:, :], AF.Exp, bias=bias_ap, scale=ATT_SCALE0,
                               sem=ctx.sc, after=(mm_a, bias_tok) + pw)
                    e_b = p.op("scalar", "activation",
                               pT[:, pb * 2048 + 1024:pb * 2048 + 2048],
                               ps_sb[:, :], AF.Exp, bias=bias_ap, scale=ATT_SCALE0,
                               sem=ctx.sc, after=(mm_b,))
                st_kbuf["exp_war"][0] = e_a
                st_kbuf["exp_war"][1] = e_b
            if prev is not None:
                jp, pbp, ea_p, eb_p = prev
                c0 = pbp * 2048
                lw = st_kbuf.get("lsum_war_%d" % pbp)
                tl1 = p.op("vector", "tensor_add", lsum[:, pbp * 512:(pbp + 1) * 512],
                           pT[:, c0:c0 + 512], pT[:, c0 + 512:c0 + 1024],
                           sem=ctx.dv, after=(ea_p,) + ((lw,) if lw else ()))
                tl2 = p.op("vector", "tensor_add", lsum2[:, :],
                           pT[:, c0 + 1024:c0 + 1536], pT[:, c0 + 1536:c0 + 2048],
                           sem=ctx.dv, after=(eb_p,))
                tl3 = p.op("vector", "tensor_add", lsum[:, pbp * 512:(pbp + 1) * 512],
                           lsum[:, pbp * 512:(pbp + 1) * 512], lsum2[:, :],
                           sem=ctx.dv, after=(tl2, tl1))
                lmm = p.op("tensor", "matmul", ps_l[0:1, :], ctx.ones[:, 0:1],
                           lsum[:, pbp * 512:(pbp + 1) * 512], start=(jp == 0),
                           stop=(jp == NSLOT - 1), sem=ctx.pe,
                           after=(tl3,) + ((st_kbuf.get("psg_l"),) if (jp == 0 and st_kbuf.get("psg_l")) else ()))
                st_kbuf["lsum_war_%d" % pbp] = lmm
                pv_tok = None
                vb = jp % 3
                for c in range(4):
                    pv_tok = p.op("tensor", "matmul", pv_cur[:, :],
                                  v_t[:, vb * 512 + c * 128:vb * 512 + (c + 1) * 128],
                                  pT[:, c0 + c * 512:c0 + (c + 1) * 512],
                                  start=(jp == 0 and c == 0),
                                  stop=(jp == NSLOT - 1 and c == 3),
                                  after=((ea_p,) if c < 2 else (eb_p,)) +
                                        ((st_kbuf.get("psg_pv_%d" % par),) if (jp == 0 and c == 0 and st_kbuf.get("psg_pv_%d" % par)) else ()),
                                  sem=ctx.pe)
                st_kbuf["war"][vb] = pv_tok
                st_kbuf["pT_war"][pbp] = (pv_tok, tl3)
                if jp + 3 < NSLOT:
                    ld_q[jp + 3] = emit_slot_loads(h, q, jp + 3)
            if j < NSLOT:
                prev = (j, j % 2, e_a, e_b)
        lpv_done = p.tok(ctx.pe)
        trc = p.op("vector", "reciprocal", rl[:, :], ps_l[0:1, :], sem=ctx.dv,
                   after=(lpv_done,))
        trc2 = p.op("vector", "tensor_scalar_mul", rl[:, :], rl[:, :], VSCALE0,
                    sem=ctx.dv, after=(trc,))
        tb = p.op("tensor", "matmul", ps_bc[:, :], ctx.ones32[0:1, 0:128], rl[0:1, :],
                  start=True, stop=True, sem=ctx.pe,
                  after=(trc2,) + ((st_kbuf["bc_war"],) if st_kbuf["bc_war"] else ()))
        tc = p.op("vector", "tensor_copy", linv[:, :], ps_bc[:, :], sem=ctx.dv, after=(tb,))
        st_kbuf["bc_war"] = tc
        aw = (st_kbuf["attn_war"],) if st_kbuf["attn_war"] else ()
        ta = p.op("vector", "tensor_mul", attn[:, h * 512:(h + 1) * 512],
                  pv_cur[:, :], linv[:, :], sem=ctx.dv, after=(tc,) + aw)
        st_kbuf["psg_pv_%d" % par] = ta
        st_kbuf["psg_l"] = trc
        return ta

    for q in range(2):
        attn_tok = None
        for h in range(16):
            attn_tok = flash_head(h, q)
        po_war = None
        for mq in range(4):
            for n in range(4):
                mm = None
                p.wait_tok("tensor", tw)
                p.wait_tok("tensor", attn_tok)
                if po_war is not None:
                    p.wait_tok("tensor", po_war)
                for k in range(16):
                    mm = p.op("tensor", "matmul", ps_o[:, :],
                              attn[:, k * 512 + mq * 128:k * 512 + (mq + 1) * 128],
                              wsb[:, k * 2048 + n * 512:k * 2048 + (n + 1) * 512],
                              start=(k == 0), stop=(k == 15),
                              sem=(ctx.pe if k == 15 else None))
                sl = (mq * 4 + n) % 2
                xa = xrt[:, sl * 512:(sl + 1) * 512]
                ra = rout[:, sl * 512:(sl + 1) * 512]
                war = st_kbuf.get("oslot_war_%d" % sl)
                tx = p.op("sync", "dma_start", sem=ctx.ls, n=16, out=xa,
                          in_=x[q * 512 + mq * 128:q * 512 + (mq + 1) * 128,
                                n * 512:(n + 1) * 512],
                          after=((war,) if war else ()))
                tadd = p.op("vector", "tensor_add", ra, ps_o[:, :], xa, sem=ctx.dv,
                            after=(mm, tx) + ((war,) if war else ()))
                po_war = tadd
                st_kbuf["oslot_war_%d" % sl] = p.op(
                    "sync", "dma_start", sem=ctx.st, n=16,
                    out=o_r1[q * 512 + mq * 128:q * 512 + (mq + 1) * 128,
                             n * 512:(n + 1) * 512],
                    in_=ra, after=(tadd,))
        st_kbuf["attn_war"] = p.tok(ctx.pe)
        st_kbuf["psg_pv_1"] = (ctx.dv, ctx.p.val(ctx.dv))
        st_kbuf["psg_pv_0"] = (ctx.dv, ctx.p.val(ctx.dv))
    if mega:
        ctx.barrier()
    ctx.al.end_stage()
    return ctx


def run_l0attn(g, a_res):
    ctx = build_l0attn()
    nc = ctx.finish()
    Kn_all = np.concatenate([a_res[i]["Kn"] for i in range(NCORES)], axis=1)
    Kr_all = np.concatenate([a_res[i]["Kr"] for i in range(NCORES)], axis=1)
    V_all = np.concatenate([a_res[i]["V"] for i in range(NCORES)], axis=0)
    kn2f = np.concatenate([a_res[i]["kn2"] for i in range(NCORES)], axis=1)
    KnM = np.ascontiguousarray(Kn_all[:, :7680])
    KrM = np.ascontiguousarray(Kr_all[:, :7680])
    VM = np.ascontiguousarray(V_all[:7680, :])
    in_maps = []
    for i in range(NCORES):
        m = {"KnM": KnM, "KrM": KrM, "VM": VM, "kn2f": kn2f,
             "maskC": g["maskC"], "w_proj": g["w_proj"], "ident": g["ident"]}
        m["x_sl"] = g["x"][i * T:(i + 1) * T]
        m["Qn"] = a_res[i]["Qn"]
        m["Qr"] = a_res[i]["Qr"]
        m["qn2"] = a_res[i]["qn2"]
        knd = np.stack([Kn_all[:, (2 * i + q) * 512:(2 * i + q + 1) * 512] for q in range(2)])
        krd = np.stack([Kr_all[:, (2 * i + q) * 512:(2 * i + q + 1) * 512] for q in range(2)])
        vd = np.stack([V_all[(2 * i + q) * 512:(2 * i + q + 1) * 512, :] for q in range(2)])
        m["KnD"], m["KrD"], m["VD"] = (np.ascontiguousarray(knd),
                                       np.ascontiguousarray(krd),
                                       np.ascontiguousarray(vd))
        km = np.zeros((128, 2 * 15), np.float32)
        for q in range(2):
            dq = 2 * i + q
            km[:, q * 15 + dq:(q + 1) * 15] = NEG
        m["keymask"] = km
        in_maps.append(m)
    res = run_bass_kernel_spmd(nc, in_maps, core_ids=list(range(NCORES)))
    return res.results


def build_l0ffn(mega=None):
    ctx = mega["ctx"] if mega else Ctx("l0ffn")
    ctx.al.begin_stage("c")
    ctx.psg = None
    ctx.strowg = {0: None, 1: None}
    p, al = ctx.p, ctx.al
    r1 = mega["io"]["r1"] if mega else ctx.din("resid1", [T, D])
    w0_d = ctx.din("W0T", [D, 8192])
    b0_d = ctx.din("b0col", [128, 64])
    w1_d = ctx.din("W1T", [8192, D])
    b1_d = ctx.din("b1row", [1, D])
    ctx.din("ident", [128, 128], BF16)
    o_r2 = mega["io"]["r2"] if mega else ctx.dout("resid2", [T, D])

    if not hasattr(ctx, "ident"):
        ctx.common()
    W0C, W1C = 0, 16384
    wsb = al.sb("wsb", [128, 32768], BF16)
    lnx = al.sb("lnx", [128, 16 * T], BF16)
    lnx_tm = al.sb("lnx_tm", [128, D], BF16)
    xt = al.sb("xt", [128, D], F32)
    hid = al.sb("hid", [128, 8 * 512], BF16)
    eacc = al.sb("eacc", [128, 8 * 2048], F32)
    b0 = al.sb("sb_b0", [128, 64], F32)
    b1 = al.sb("sb_b1", [1, D], BF16)
    scr_sq = al.sb("scr_sq", [128, D], BF16)
    stats = {k: al.sb("st_" + k, [128, 1], F32) for k in ["ssq", "smean", "m", "var", "rstd"]}
    rt = al.sb("rt", [128, 2 * 512], F32)
    ro = al.sb("ro", [128, 2 * 512], F32)
    ps_a = al.ps("ps_a", [128, 512])
    ps_b = al.ps("ps_b", [128, 512])
    ps_tr = al.ps("ps_tr", [128, 2048], BF16)
    ps_e = al.ps("ps_e", [128, 512])
    ps_e2 = al.ps("ps_e2", [128, 512])
    pool = SlotPool([ps_a, ps_b])
    epool = SlotPool([ps_e, ps_e2])
    trpool = SlotPool([ps_tr[:, 0:512], ps_tr[:, 1024:1536]])

    p.op("sync", "dma_start", sem=ctx.ls, n=16, out=b0[:, :], in_=b0_d[:, :])
    tb1 = p.op("gpsimd", "dma_start", sem=ctx.ld, n=16, out=b1[:, :], in_=b1_d[:, :])

    xt_war = [None]
    tm_war = [None]
    lnx_ready = None
    for a in range(8):
        tx = p.op("sync", "dma_start", sem=ctx.ls, n=16, out=xt[:, :],
                  in_=r1[a * 128:(a + 1) * 128, :],
                  after=((xt_war[0],) if xt_war[0] is not None else ()))
        tn = emit_ln(ctx, xt[:, :], lnx_tm[:, :], scr_sq, stats,
                     (tx,) + ((tm_war[0],) if tm_war[0] is not None else ()))
        xt_war[0] = tn
        p.wait_tok("tensor", tn)
        for g4 in range(4):
            srcs = [lnx_tm[:, (g4 * 4 + j) * 128:(g4 * 4 + j) * 128 + 128]
                    for j in range(4)]
            dst = lnx[:, :].rearrange("p (k t) -> p k t", k=16)[:, g4 * 4:(g4 + 1) * 4,
                                                               a * 128:(a + 1) * 128]
            lnx_ready = transpose_group(ctx, srcs, dst, trpool.take())
        tm_war[0] = p.tok(ctx.pe)

    eacc_tok = {}
    hid_war = None
    pool4 = SlotPool([ps_a, ps_b, ps_e, ps_e2])
    wslots = [(0, 8192), (16384, 24576)]
    wtoks = [None, None]
    wwars = [None, None]
    NHG = 16

    def load_hg(hg):
        b = hg % 2
        W0R, W1R = wslots[b]
        war = (wwars[b],) if wwars[b] else ()
        t0 = p.op("gpsimd", "dma_start", sem=ctx.ld, n=16,
                  out=wsb[:, W0R:W0R + 8192].rearrange("p (k m) -> p k m", k=16),
                  in_=w0_d[:, hg * 512:(hg + 1) * 512].rearrange("(k p) m -> p k m", p=128),
                  after=war)
        t1 = p.op("gpsimd", "dma_start", sem=ctx.ld, n=16,
                  out=wsb[:, W1R:W1R + 8192].rearrange("p (k m) -> p k m", k=4),
                  in_=w1_d[hg * 512:(hg + 1) * 512, :].rearrange("(k p) m -> p k m", p=128),
                  after=war)
        wtoks[b] = (t0, t1)

    load_hg(0)
    for hg in range(NHG):
        b = hg % 2
        W0R, W1R = wslots[b]
        tw0, tw1 = wtoks[b]
        hid_toks = []
        for t in range(2):
            for c in range(4):
                gm = hg * 4 + c
                slot = pool4.take()
                ps, war = slot
                p.wait_tok("tensor", tw0)
                p.wait_tok("tensor", lnx_ready)
                if war is not None:
                    p.wait_tok("tensor", war)
                mm = None
                for k in range(16):
                    mm = p.op("tensor", "matmul", ps[:, :], wslice(wsb, W0R, 512, k, c),
                              lnx[:, k * T + t * 512:k * T + (t + 1) * 512],
                              start=(k == 0), stop=(k == 15),
                              sem=(ctx.pe if k == 15 else None))
                tg = p.op("scalar", "activation",
                          hid[:, (t * 4 + c) * 512:(t * 4 + c + 1) * 512], ps[:, :],
                          AF.Gelu, bias=b0[:, gm:gm + 1], sem=ctx.sc,
                          after=(mm,) + ((hid_war,) if hid_war else ()))
                hid_toks.append(tg)
                slot[1] = tg
        if hg + 1 < NHG:
            load_hg(hg + 1)
        for tk in hid_toks:
            p.wait_tok("tensor", tk)
        p.wait_tok("tensor", tw1)
        last_emm = None
        for t in range(2):
            for mq in range(4):
                for n in range(4):
                    ekey = (t, mq, n)
                    if ekey in eacc_tok:
                        p.wait_tok("tensor", eacc_tok[ekey])
                    eslot = pool4.take()
                    pse, ewar = eslot
                    if ewar is not None:
                        p.wait_tok("tensor", ewar)
                    mm = None
                    for k in range(4):
                        mm = p.op("tensor", "matmul", pse[:, :],
                                  hid[:, (t * 4 + k) * 512 + mq * 128:(t * 4 + k) * 512 + (mq + 1) * 128],
                                  wsb[:, W1R + k * 2048 + n * 512:W1R + k * 2048 + (n + 1) * 512],
                                  start=(k == 0), stop=(k == 3 and hg > 0),
                                  sem=(ctx.pe if (k == 3 and hg > 0) else None))
                    if hg == 0:
                        mm = p.op("tensor", "matmul", pse[:, :], ctx.ones[0:1, 0:128],
                                  b1[0:1, n * 512:(n + 1) * 512], start=False, stop=True,
                                  sem=ctx.pe, after=(tb1,))
                    last_emm = mm
                    dst = eacc[:, (t * 4 + mq) * 2048 + n * 512:(t * 4 + mq) * 2048 + (n + 1) * 512]
                    if hg == 0:
                        tadd = p.op("vector", "tensor_copy", dst, pse[:, :], sem=ctx.dv,
                                    after=(mm,))
                    else:
                        tadd = p.op("vector", "tensor_add", dst, dst, pse[:, :],
                                    sem=ctx.dv, after=(mm,))
                    eslot[1] = tadd
                    eacc_tok[ekey] = tadd
        hid_war = last_emm
        wwars[b] = p.tok(ctx.pe)

    oslot_war = [None, None]
    for t in range(2):
        for mq in range(4):
            for n in range(4):
                sl = (mq * 4 + n) % 2
                war = (oslot_war[sl],) if oslot_war[sl] else ()
                tx = p.op("sync", "dma_start", sem=ctx.ls, n=16,
                          out=rt[:, sl * 512:(sl + 1) * 512],
                          in_=r1[t * 512 + mq * 128:t * 512 + (mq + 1) * 128,
                                 n * 512:(n + 1) * 512], after=war)
                src = eacc[:, (t * 4 + mq) * 2048 + n * 512:(t * 4 + mq) * 2048 + (n + 1) * 512]
                tadd = p.op("vector", "tensor_add", ro[:, sl * 512:(sl + 1) * 512],
                            src, rt[:, sl * 512:(sl + 1) * 512], sem=ctx.dv,
                            after=(tx, eacc_tok[(t, mq, n)]) + war)
                oslot_war[sl] = p.op("sync", "dma_start", sem=ctx.st, n=16,
                                     out=o_r2[t * 512 + mq * 128:t * 512 + (mq + 1) * 128,
                                              n * 512:(n + 1) * 512],
                                     in_=ro[:, sl * 512:(sl + 1) * 512], after=(tadd,))
    if mega:
        ctx.barrier()
    ctx.al.end_stage()
    return ctx


def run_l0ffn(g, resid1_per_core):
    ctx = build_l0ffn()
    nc = ctx.finish()
    in_maps = []
    for i in range(NCORES):
        m = {"W0T": g["W0T"], "b0col": g["b0col"], "W1T": g["W1T"], "b1row": g["b1row"],
             "ident": g["ident"], "resid1": resid1_per_core[i]}
        in_maps.append(m)
    res = run_bass_kernel_spmd(nc, in_maps, core_ids=list(range(NCORES)))
    return res.results


def build_l1pre(stop_after=None, mega=None):
    ctx = mega["ctx"] if mega else Ctx("l1pre")
    ctx.al.begin_stage("d")
    ctx.psg = None
    ctx.str2g = None
    ctx.strowg = {0: None, 1: None}
    p, al = ctx.p, ctx.al
    r2 = mega["io"]["r2"] if mega else ctx.din("resid2", [T, D])
    wqkv_d = ctx.din("Wqkv", [D, 6144])
    qg_in = ctx.din("qg1", [128, 16])
    kg_in = ctx.din("kg1", [128, 16])
    cos_d = ctx.din("cosT_sl", [128, T], BF16)
    sin_d = ctx.din("sinT_sl", [128, T], BF16)
    psw_d = ctx.din("pswap", [128, 128], BF16)
    ctx.din("ident", [128, 128], BF16)
    if mega:
        io = mega["io"]
        o_q, o_k, o_v = io["Q1loc"], io["K1loc"], io["V1loc"]
        o_qn2, o_kn2 = io["qn21loc"], io["kn21loc"]
    else:
        o_q = ctx.dout("Q1", [2048, T], BF16)
        o_k = ctx.dout("K1", [2048, T], BF16)
        o_v = ctx.dout("V1", [T, 2048], BF16)
        o_qn2 = ctx.dout("qn21", [16, T])
        o_kn2 = ctx.dout("kn21", [16, T])

    if not hasattr(ctx, "ident"):
        ctx.common()
    wsb = al.sb("wsb", [128, 32768], BF16)
    lnx = al.sb("lnx", [128, 16 * T], BF16)
    lnx_tm = al.sb("lnx_tm", [128, D], BF16)
    xt = al.sb("xt", [128, D], F32)
    qa = al.sb("qa", [128, 16 * T], BF16)
    ev4 = al.sb("ev4", [128, 4 * 512], BF16)
    sqs = al.sb("sqs", [128, 2 * 512], BF16)
    psw = al.sb("sb_psw", [128, 128], BF16)
    uscr = al.sb("uscr", [128, 512], BF16)
    wscr = al.sb("wscr", [128, 512], BF16)
    cosT = al.sb("sb_cos", [128, T], BF16)
    sinT = al.sb("sb_sin", [128, T], BF16)
    qg = al.sb("sb_qg1", [128, 16], F32)
    kg = al.sb("sb_kg1", [128, 16], F32)
    strow = al.sb("strow", [1, 2 * 512], F32)
    str2 = al.sb("str2", [1, 512], F32)
    scr_sq = al.sb("scr_sq", [128, D], BF16)
    stats = {k: al.sb("st_" + k, [128, 1], F32) for k in ["ssq", "smean", "m", "var", "rstd"]}
    r128 = al.sb("r128", [128, T], F32)
    vtm = al.sb("vtm", [128, 2 * 2048], BF16)
    ps_a = al.ps("ps_a", [128, 512])
    ps_b = al.ps("ps_b", [128, 512])
    ps_tr = al.ps("ps_tr", [128, 2048], BF16)
    ps_st = al.ps("ps_st", [1, 512])
    ps_bc = al.ps("ps_bc", [128, 512])
    ps_s1 = al.ps("ps_s1", [128, 512])
    ps_s2 = al.ps("ps_s2", [128, 512])
    pool = SlotPool([ps_a, ps_b])
    trpool = SlotPool([ps_tr[:, 0:512], ps_tr[:, 1024:1536]])
    epool = SlotPool([ev4[:, i * 512:(i + 1) * 512] for i in range(4)])
    sqpool = SlotPool([sqs[:, 0:512], sqs[:, 512:1024]])
    swpool = SlotPool([ps_s1, ps_s2])
    bcslot = SlotPool([ps_bc])

    for dst, src in ((cosT, cos_d), (sinT, sin_d), (qg, qg_in), (kg, kg_in), (psw, psw_d)):
        p.op("sync", "dma_start", sem=ctx.ls, n=16, out=dst[:, :], in_=src[:, :])
    c_ready = p.tok(ctx.ls)

    xt_war = [None]
    tm_war = [None]
    lnx_ready = None
    for a in range(8):
        tx = p.op("sync", "dma_start", sem=ctx.ls, n=16, out=xt[:, :],
                  in_=r2[a * 128:(a + 1) * 128, :],
                  after=((xt_war[0],) if xt_war[0] is not None else ()))
        tn = emit_ln(ctx, xt[:, :], lnx_tm[:, :], scr_sq, stats,
                     (tx,) + ((tm_war[0],) if tm_war[0] is not None else ()))
        xt_war[0] = tn
        p.wait_tok("tensor", tn)
        for g4 in range(4):
            srcs = [lnx_tm[:, (g4 * 4 + j) * 128:(g4 * 4 + j) * 128 + 128]
                    for j in range(4)]
            dst = lnx[:, :].rearrange("p (k t) -> p k t", k=16)[:, g4 * 4:(g4 + 1) * 4,
                                                               a * 128:(a + 1) * 128]
            lnx_ready = transpose_group(ctx, srcs, dst, trpool.take())
        tm_war[0] = p.tok(ctx.pe)

    def lnx_rhs(k, t):
        ctx.p.wait_tok("tensor", lnx_ready)
        return lnx[:, k * T + t * 512:k * T + (t + 1) * 512]

    def qk_tensor(part, gain, o_fm, o_n2, w_war):
        last_to1 = [None]
        tw = p.op("gpsimd", "dma_start", sem=ctx.ld, n=16,
                  out=wsb[:, :].rearrange("p (k m) -> p k m", k=16),
                  in_=wqkv_d[:, part * 2048:(part + 1) * 2048].rearrange(
                      "(k p) m -> p k m", p=128),
                  after=(w_war,) if w_war else ())
        for t in range(2):
            stat_tok = None
            pending = None
            for m in range(16):
                slot = pool.take()
                ps, war = slot
                p.wait_tok("tensor", tw)
                if war is not None:
                    p.wait_tok("tensor", war)
                mm = None
                for k in range(16):
                    mm = p.op("tensor", "matmul", ps[:, :], wslice(wsb, 0, 2048, k, m),
                              lnx_rhs(k, t), start=(k == 0), stop=(k == 15),
                              sem=(ctx.pe if k == 15 else None))
                if pending is not None:
                    stat_tok = pending()
                tev = p.op("vector", "tensor_copy", qa[:, m * T + t * 512:m * T + (t + 1) * 512],
                           ps[:, :], sem=ctx.dv, after=(mm,))
                sqslot = sqpool.take()
                sqa_, sqwar = sqslot
                tsq = p.op("scalar", "activation", sqa_, ps[:, :], AF.Square, sem=ctx.sc,
                           after=(mm, tev) + ((sqwar,) if sqwar is not None else ()))
                slot[1] = (tev, tsq)

                def mk_stat(m=m, tsq=tsq, sqa_=sqa_, sqslot=sqslot):
                    def emit():
                        tk = p.op("tensor", "matmul", ps_st[0:1, :], ctx.ones[:, 0:1], sqa_,
                                  start=(m == 0), stop=(m == 15), sem=ctx.pe,
                                  after=(tsq,) + ((ctx.psg,) if (m == 0 and ctx.psg) else ()))
                        sqslot[1] = tk
                        return tk
                    return emit
                pending = mk_stat()
            stat_tok = pending()
            tr = rmsnorm_post2(ctx, ps_st[0:1, :], t, strow, r128, bcslot.take(),
                               2048, 1e-6, stat_tok)
            for m in range(16):
                ch = qa[:, m * T + t * 512:m * T + (t + 1) * 512]
                tg1 = p.op("vector", "tensor_scalar_mul", ch, ch, gain[:, m:m + 1],
                           sem=ctx.dv, after=(tr, c_ready))
                tg2 = p.op("vector", "tensor_mul", ch, ch, r128[:, t * 512:(t + 1) * 512],
                           sem=ctx.dv, after=(tg1,))
                sqslot = sqpool.take()
                sqa_, sqwar = sqslot
                tsq = p.op("scalar", "activation", sqa_, ch, AF.Square, sem=ctx.sc,
                           after=(tg2,) + ((sqwar,) if sqwar is not None else ()))
                tmm = p.op("tensor", "matmul", ps_st[0:1, :], ctx.ones[:, 0:1], sqa_,
                           start=True, stop=True, sem=ctx.pe,
                           after=(tsq,) + ((ctx.psg,) if ctx.psg else ()))
                sqslot[1] = tmm
                tse = p.op("vector", "tensor_copy", str2[0:1, :], ps_st[0:1, :], sem=ctx.dv,
                           after=(tmm,) + ((ctx.str2g,) if ctx.str2g else ()))
                ctx.psg = tse
                ctx.str2g = p.op("sync", "dma_start", sem=ctx.st, n=16,
                                 out=o_n2[m:m + 1, t * 512:(t + 1) * 512],
                                 in_=str2[0:1, :], after=(tse,))
                ch128 = qa[:, m * T + t * 512:m * T + (t + 1) * 512]
                Ct = cosT[:, t * 512:(t + 1) * 512]
                St = sinT[:, t * 512:(t + 1) * 512]
                es = epool.take()
                ev, ewar = es
                eww = (ewar,) if ewar is not None else ()
                swsl = swpool.take()
                ps_sw, swwar = swsl
                if swwar is not None:
                    p.wait_tok("tensor", swwar)
                tsw = p.op("tensor", "matmul", ps_sw[:, :], psw[:, :], ch128,
                           start=True, stop=True, sem=ctx.pe, after=(tg2, c_ready))
                tu = p.op("vector", "tensor_mul", uscr[:, :], ch128, Ct, sem=ctx.dv,
                          after=(tg2, c_ready))
                tw_ = p.op("vector", "tensor_mul", wscr[:, :], ps_sw[:, :], St, sem=ctx.dv,
                           after=(tsw,))
                swsl[1] = tw_
                to1 = p.op("vector", "tensor_add", ev[:, :], uscr[:, :], wscr[:, :],
                           sem=ctx.dv, after=(tu, tw_) + eww)
                last_to1[0] = to1
                es[1] = p.op("sync", "dma_start", sem=ctx.st, n=16,
                             out=o_fm[m * 128:(m + 1) * 128, t * 512:(t + 1) * 512],
                             in_=ev[:, :], after=(to1,))
        return p.tok(ctx.pe)

    if stop_after == "ln":
        ctx.al.end_stage()
        return ctx
    qtok = qk_tensor(0, qg, o_q, o_qn2, None)
    if stop_after == "q":
        ctx.al.end_stage()
        return ctx
    ktok = qk_tensor(1, kg, o_k, o_kn2, qtok)
    if stop_after == "k":
        ctx.al.end_stage()
        return ctx
    if mega:
        io = mega["io"]
        p.wait("gpsimd", ctx.st)
        p.op("gpsimd", "collective_compute", "AllReduce", ALU.add,
             replica_groups=[list(range(NCORES))],
             ins=[io["bar2"].opt()], outs=[io["bar2o"].opt()], sem=ctx.cc, n=1)
        p.wait("gpsimd", ctx.cc)
        for src, dst in (("K1loc", "K1AG"), ("kn21loc", "kn21AG")):
            p.op("gpsimd", "collective_compute", "AllGather", ALU.bypass,
                 replica_groups=[list(range(NCORES))],
                 ins=[io[src].opt()], outs=[io[dst].opt()], sem=ctx.cc, n=1)
        mega["ccK"] = p.tok(ctx.cc)

    tw = p.op("gpsimd", "dma_start", sem=ctx.ld, n=16,
              out=wsb[:, :].rearrange("p (k m) -> p k m", k=16),
              in_=wqkv_d[:, 4096:6144].rearrange("(k p) m -> p k m", p=128),
              after=(ktok,))
    qa_war = p.tok(ctx.gp)
    qa_war_v = p.tok(ctx.dv)
    vtm_war = [None, None]
    for t in range(2):
        vfm_tok = None
        for m in range(16):
            slot = pool.take()
            ps, war = slot
            p.wait_tok("tensor", tw)
            if war is not None:
                p.wait_tok("tensor", war)
            mm = None
            for k in range(16):
                mm = p.op("tensor", "matmul", ps[:, :], wslice(wsb, 0, 2048, k, m),
                          lnx_rhs(k, t), start=(k == 0), stop=(k == 15),
                          sem=(ctx.pe if k == 15 else None))
            vfm_tok = p.op("scalar", "activation",
                           qa[:, m * T + t * 512:m * T + (t + 1) * 512]
                           if False else qa[:, m * 512 + t * 8192:m * 512 + t * 8192 + 512],
                           ps[:, :], AF.Identity, sem=ctx.sc,
                           after=(mm, qa_war, qa_war_v))
            slot[1] = vfm_tok
        p.wait_tok("tensor", vfm_tok)
        for a in range(4):
            sl = a % 2
            if vtm_war[sl] is not None:
                p.wait_tok("vector", vtm_war[sl])
            for g4 in range(4):
                srcs = [qa[:, t * 8192 + (g4 * 4 + j) * 512 + a * 128:
                           t * 8192 + (g4 * 4 + j) * 512 + a * 128 + 128] for j in range(4)]
                dst = vtm[:, sl * 2048 + g4 * 512:sl * 2048 + (g4 + 1) * 512]
                tg = transpose_group(ctx, srcs, dst, trpool.take())
            vtm_war[sl] = p.op("sync", "dma_start", sem=ctx.st, n=16,
                               out=o_v[t * 512 + a * 128:t * 512 + (a + 1) * 128, :],
                               in_=vtm[:, sl * 2048:(sl + 1) * 2048], after=(tg,))
    if mega:
        io = mega["io"]
        p.wait("gpsimd", ctx.st)
        p.op("gpsimd", "collective_compute", "AllReduce", ALU.add,
             replica_groups=[list(range(NCORES))],
             ins=[io["bar3"].opt()], outs=[io["bar3o"].opt()], sem=ctx.cc, n=1)
        p.wait("gpsimd", ctx.cc)
        p.op("gpsimd", "collective_compute", "AllGather", ALU.bypass,
             replica_groups=[list(range(NCORES))],
             ins=[io["V1loc"].opt()], outs=[io["V1AG"].opt()], sem=ctx.cc, n=1)
        p.wait("gpsimd", ctx.cc)

        def nbcopy(g, io=io):
            pid = g.partition_id()
            o1 = g.snap((pid + 7) % 8 * 2048)
            g.dma_start(out=io["K1nb"][:, :],
                        in_=io["K1AG"][bass.ds(o1, 2048), :]).then_inc(ctx.ld, 16)
            o3 = g.snap((pid + 7) % 8 * 16)
            g.dma_start(out=io["kn21nb"][:, :],
                        in_=io["kn21AG"][bass.ds(o3, 16), :]).then_inc(ctx.ld, 16)
            o2 = g.snap((pid + 7) % 8 * 1024)
            g.dma_start(out=io["V1nb"][:, :],
                        in_=io["V1AG"][bass.ds(o2, 1024), :]).then_inc(ctx.ld, 16)
        p.emit("gpsimd", nbcopy)
        p.inc(ctx.ld, 48)
        mega["nb_tok"] = p.tok(ctx.ld)
        ctx.barrier()
    ctx.al.end_stage()
    return ctx


def run_l1pre(g, resid2_per_core):
    ctx = build_l1pre()
    nc = ctx.finish()
    cosb = _bf(g["cosT"])
    sinb = _bf(g["sinT"])
    in_maps = []
    for i in range(NCORES):
        m = {"Wqkv": g["Wqkv"], "qg1": g["qg1"], "kg1": g["kg1"], "ident": g["ident"],
             "pswap": g["pswap"], "resid2": resid2_per_core[i],
             "cosT_sl": np.ascontiguousarray(cosb[:, i * T:(i + 1) * T]),
             "sinT_sl": np.ascontiguousarray(sinb[:, i * T:(i + 1) * T])}
        in_maps.append(m)
    res = run_bass_kernel_spmd(nc, in_maps, core_ids=list(range(NCORES)))
    return res.results


ATT_SCALE1 = 1.0 / float(np.sqrt(128.0))


def build_l1attn(mega=None):
    ctx = mega["ctx"] if mega else Ctx("l1attn")
    ctx.al.begin_stage("e")
    p, al = ctx.p, ctx.al
    if mega:
        io = mega["io"]
        x = io["r2"]
        q_d, qn2_d = io["Q1loc"], io["qn21loc"]
        k1loc, v1loc, kn21loc = io["K1loc"], io["V1loc"], io["kn21loc"]
        k1nb, v1nb, kn21nb = io["K1nb"], io["V1nb"], io["kn21nb"]
        o_r3 = io["r3"]
        mscr = io["mscrE"]
    else:
        x = ctx.din("resid2", [T, D])
        q_d = ctx.din("Q1", [2048, T], BF16)
        k_d = ctx.din("K1in", [2048, 2 * 1536], BF16)
        v_d = ctx.din("V1in", [2 * 1536, 2048], BF16)
        qn2_d = ctx.din("qn21", [16, T])
        kn2_d = ctx.din("kn21in", [16, 2 * 1536])
        o_r3 = ctx.dout("resid3", [T, D])
        mscr = ctx.nc.dram_tensor("mscr", [16, 2], F32, kind="Internal").ap()
    km_d = ctx.din("keymask1", [128, 6])
    maskc_d = ctx.din("maskC", [128, 2048])
    maskw_d = ctx.din("maskW", [128, 2048])
    wo_d = ctx.din("Wo", [D, D])
    ctx.din("ident", [128, 128], BF16)

    if not hasattr(ctx, "ident"):
        ctx.common()
    wsb = al.sb("wsb", [128, 16 * 2048], BF16)
    q_sb = al.sb("q_sb", [128, 16 * T], BF16)
    pT = al.sb("pT", [128, 2 * 2048], BF16)
    attn = al.sb("attn", [128, 16 * 512], BF16)
    maskc = al.sb("sb_maskc", [128, 2048], F32)
    maskw = al.sb("sb_maskw", [128, 2048], F32)
    keymask = al.sb("sb_keymask", [128, 6], F32)
    biasC = al.sb("biasC", [128, 32 * 3], F32)
    smask = al.sb("smask", [128, 2048], F32)
    kn_t = al.sb("kn_t", [128, 3 * 512], BF16)
    v_t = al.sb("v_t", [128, 3 * 512], BF16)
    lsum = al.sb("lsum", [128, 2 * 512], BF16)
    lsum2 = al.sb("lsum2", [128, 512], BF16)
    linv = al.sb("linv", [128, 512], F32)
    rl = al.sb("rl", [1, 512], F32)
    xrt = al.sb("xrt", [128, 2 * 512], F32)
    rout = al.sb("rout", [128, 2 * 512], F32)
    qn2 = al.sb("sb_qn2", [16, T], F32)
    k2r = al.sb("k2r", [16, 2 * 1536], F32)
    kmax2 = al.sb("kmax2", [16, 2], F32)
    qmax2 = al.sb("qmax2", [16, 2], F32)
    negM = al.sb("negM", [16, 2], F32)
    mrow = al.sb("mrow", [1, 32], F32)
    Mb = al.sb("Mb", [128, 32], F32)
    ps_sa = al.ps("ps_sa", [128, 1024])
    ps_sb = al.ps("ps_sb", [128, 1024])
    ps_pv = al.ps("ps_pv", [128, 512])
    ps_l = al.ps("ps_l", [1, 512])
    ps_bc = al.ps("ps_bc", [128, 512])
    ps_o = al.ps("ps_o", [128, 512])

    tw = p.op("gpsimd", "dma_start", sem=ctx.ld, n=16,
              out=wsb[:, :].rearrange("p (k m) -> p k m", k=16),
              in_=wo_d.rearrange("(k p) m -> p k m", p=128))
    p.op("sync", "dma_start", sem=ctx.ls, n=16,
         out=q_sb[:, :].rearrange("p (h t) -> p h t", h=16),
         in_=q_d.rearrange("(h p) t -> p h t", p=128))
    p.op("sync", "dma_start", sem=ctx.ls, n=16, out=maskc[:, :], in_=maskc_d[:, :])
    p.op("sync", "dma_start", sem=ctx.ls, n=16, out=maskw[:, :], in_=maskw_d[:, :])
    tkm = p.op("sync", "dma_start", sem=ctx.ls, n=16, out=keymask[:, :], in_=km_d[:, :])
    tq3 = p.op("sync", "dma_start", sem=ctx.ls, n=16, out=qn2[:, :], in_=qn2_d[:, :])
    regs = {}
    if mega:

        tq5 = p.op("sync", "dma_start", sem=ctx.ls, n=16, out=k2r[:, 0:1024],
                   in_=kn21loc[:, :])
        tq5b = p.op("gpsimd", "dma_start", sem=ctx.ld, n=16, out=k2r[:, 1024:2048],
                    in_=kn21nb[:, :], after=(mega["nb_tok"],))
    else:
        tq5 = p.op("sync", "dma_start", sem=ctx.ls, n=16, out=k2r[:, :], in_=kn2_d[:, :])
    q_ready = p.tok(ctx.ls)

    tq4 = None
    if mega:
        ta_ = p.op("vector", "reduce_max", kmax2[:, 0:1], k2r[:, 0:1024], axis=AX.X,
                   sem=ctx.dv, after=(tq5,))
        tb_ = p.op("vector", "reduce_max", kmax2[:, 1:2], k2r[:, 1024:2048], axis=AX.X,
                   sem=ctx.dv, after=(tq5b, ta_))
        tkx = p.op("vector", "tensor_max", kmax2[:, 0:1], kmax2[:, 0:1], kmax2[:, 1:2],
                   sem=ctx.dv, after=(tb_,))
        for q in range(2):
            tq4 = p.op("vector", "reduce_max", qmax2[:, q:q + 1],
                       qn2[:, q * 512:(q + 1) * 512], axis=AX.X, sem=ctx.dv,
                       after=(tq3, tkx))
        t5 = p.op("vector", "tensor_scalar_mul", negM[:, :], qmax2[:, :], kmax2[:, 0:1],
                  sem=ctx.dv, after=(tq4,))
    else:
        for q in range(2):
            p.op("vector", "reduce_max", kmax2[:, q:q + 1], k2r[:, q * 1536:(q + 1) * 1536],
                 axis=AX.X, sem=ctx.dv, after=(tq5,))
            tq4 = p.op("vector", "reduce_max", qmax2[:, q:q + 1], qn2[:, q * 512:(q + 1) * 512],
                       axis=AX.X, sem=ctx.dv, after=(tq3,))
        t5 = p.op("vector", "tensor_mul", negM[:, :], qmax2[:, :], kmax2[:, :],
                  sem=ctx.dv, after=(tq4,))
    t6 = p.op("scalar", "activation", negM[:, :], negM[:, :], AF.Sqrt, sem=ctx.sc, after=(t5,))
    t7 = p.op("vector", "tensor_scalar_mul", negM[:, :], negM[:, :], -ATT_SCALE1,
              sem=ctx.dv, after=(t6,))
    t8 = p.op("sync", "dma_start", sem=ctx.st, n=16, out=mscr[:, :], in_=negM[:, :], after=(t7,))
    t9 = p.op("sync", "dma_start", sem=ctx.ls, n=16, out=mrow[:, :],
              in_=mscr.rearrange("a b -> (a b)")[None, :], after=(t8,))
    t10 = p.op("tensor", "matmul", ps_bc[:, 0:32], ctx.ones32[0:1, 0:128], mrow[0:1, :],
               start=True, stop=True, sem=ctx.pe, after=(t9,))
    t11 = p.op("vector", "tensor_copy", Mb[:, :], ps_bc[:, 0:32], sem=ctx.dv, after=(t10,))
    bias_tok = None
    for h in range(16):
        for q in range(2):
            idx = h * 2 + q
            bias_tok = p.op("vector", "tensor_scalar_add",
                            biasC[:, idx * 3:(idx + 1) * 3],
                            keymask[:, q * 3:(q + 1) * 3], Mb[:, idx:idx + 1],
                            sem=ctx.dv, after=(t11, tkm))

    NSLOT = 3
    st_k = {"war": [None, None, None], "exp_war": [None, None],
            "pT_war": [None, None], "smask_war": None, "psg": t11,
            "bc_war": t11, "attn_war": None}

    def emit_slot_loads(h, q, j):
        b = j
        war = (st_k["war"][b],) if st_k["war"][b] else ()
        if mega:
            rel = 512 * (q + j - 2)
            if rel < 0:
                loc = rel + 1024
                war = war + (mega["nb_tok"],)
                kn_src = k1nb[h * 128:(h + 1) * 128, loc:loc + 512]
                v_src = v1nb[loc:loc + 512, h * 128:(h + 1) * 128]
            else:
                kn_src = k1loc[h * 128:(h + 1) * 128, rel:rel + 512]
                v_src = v1loc[rel:rel + 512, h * 128:(h + 1) * 128]
        else:
            kn_src = k_d[h * 128:(h + 1) * 128, q * 1536 + j * 512:q * 1536 + (j + 1) * 512]
            v_src = v_d[q * 1536 + j * 512:q * 1536 + (j + 1) * 512, h * 128:(h + 1) * 128]
        t1 = p.op("sync", "dma_start", sem=ctx.ls, n=16,
                  out=kn_t[:, b * 512:(b + 1) * 512], in_=kn_src, after=war)
        t3 = p.op("gpsimd", "dma_start", sem=ctx.ld, n=16,
                  out=v_t[:, b * 512:(b + 1) * 512].rearrange("p (c d) -> p c d", c=4),
                  in_=v_src.rearrange("(c p) d -> p c d", p=128), after=war)
        return (t1, t3)

    SLOTS = [(q_, h_, j_) for q_ in range(2) for h_ in range(16) for j_ in range(NSLOT)]
    pump_state = {"i": 0}
    ld_q = {}

    def pump():
        i = pump_state["i"]
        if i < len(SLOTS):
            q_, h_, j_ = SLOTS[i]
            ld_q[(q_, h_, j_)] = emit_slot_loads(h_, q_, j_)
            pump_state["i"] = i + 1

    def flash_head(h, q):
        hq = h * 2 + q
        par = (q * 16 + h) % 2
        pv_cur = ps_pv if par == 0 else ps_o
        q_rhs = q_sb[:, h * T + q * 512:h * T + (q + 1) * 512]
        prev = None
        for j in range(NSLOT + 1):
            if j < NSLOT:
                ld_toks = ld_q.pop((q, h, j))
                b = j
                mm_a = mm_b = None
                for c in range(4):
                    sps = ps_sa if c < 2 else ps_sb
                    if c in (0, 2) and st_k["exp_war"][0 if c < 2 else 1] is not None:
                        p.wait_tok("tensor", st_k["exp_war"][0 if c < 2 else 1])
                    if c == 0:
                        for tt in ld_toks:
                            p.wait_tok("tensor", tt)
                        p.wait_tok("tensor", q_ready)
                    mm = p.op("tensor", "matmul", sps[:, (c % 2) * 512:(c % 2 + 1) * 512],
                              kn_t[:, b * 512 + c * 128:b * 512 + (c + 1) * 128],
                              q_rhs, start=True, stop=True,
                              sem=(ctx.pe if c in (1, 3) else None))
                    if c == 1:
                        mm_a = mm
                mm_b = mm
                pb = j % 2
                pw = (st_k["pT_war"][pb],) if st_k["pT_war"][pb] else ()
                bias_ap = biasC[:, hq * 3 + j:hq * 3 + j + 1]
                mk = maskw if j == 0 else (maskc if j == 2 else None)
                if mk is not None:
                    smw = (st_k["smask_war"],) if st_k["smask_war"] else ()
                    tm1 = p.op("vector", "tensor_add", smask[:, 0:1024],
                               ps_sa[:, :], mk[:, 0:1024], sem=ctx.dv,
                               after=(mm_a, bias_tok) + smw)
                    tm2 = p.op("vector", "tensor_add", smask[:, 1024:2048],
                               ps_sb[:, :], mk[:, 1024:2048], sem=ctx.dv,
                               after=(mm_b,))
                    e_a = p.op("scalar", "activation", pT[:, pb * 2048:pb * 2048 + 1024],
                               smask[:, 0:1024], AF.Exp, bias=bias_ap,
                               scale=ATT_SCALE1, sem=ctx.sc, after=(tm1,) + pw)
                    e_b = p.op("scalar", "activation",
                               pT[:, pb * 2048 + 1024:pb * 2048 + 2048],
                               smask[:, 1024:2048], AF.Exp, bias=bias_ap,
                               scale=ATT_SCALE1, sem=ctx.sc, after=(tm2,))
                    st_k["smask_war"] = e_b
                else:
                    e_a = p.op("scalar", "activation", pT[:, pb * 2048:pb * 2048 + 1024],
                               ps_sa[:, :], AF.Exp, bias=bias_ap, scale=ATT_SCALE1,
                               sem=ctx.sc, after=(mm_a, bias_tok) + pw)
                    e_b = p.op("scalar", "activation",
                               pT[:, pb * 2048 + 1024:pb * 2048 + 2048],
                               ps_sb[:, :], AF.Exp, bias=bias_ap, scale=ATT_SCALE1,
                               sem=ctx.sc, after=(mm_b,))
                st_k["exp_war"][0] = e_a
                st_k["exp_war"][1] = e_b
            if prev is not None:
                jp, pbp, ea_p, eb_p = prev
                c0 = pbp * 2048
                lw = st_k.get("lsum_war_%d" % pbp)
                tl1 = p.op("vector", "tensor_add", lsum[:, pbp * 512:(pbp + 1) * 512],
                           pT[:, c0:c0 + 512], pT[:, c0 + 512:c0 + 1024],
                           sem=ctx.dv, after=(ea_p,) + ((lw,) if lw else ()))
                tl2 = p.op("vector", "tensor_add", lsum2[:, :],
                           pT[:, c0 + 1024:c0 + 1536], pT[:, c0 + 1536:c0 + 2048],
                           sem=ctx.dv, after=(eb_p,))
                tl3 = p.op("vector", "tensor_add", lsum[:, pbp * 512:(pbp + 1) * 512],
                           lsum[:, pbp * 512:(pbp + 1) * 512], lsum2[:, :],
                           sem=ctx.dv, after=(tl2, tl1))
                lmm = p.op("tensor", "matmul", ps_l[0:1, :], ctx.ones[:, 0:1],
                     lsum[:, pbp * 512:(pbp + 1) * 512], start=(jp == 0),
                     stop=(jp == NSLOT - 1), sem=ctx.pe,
                     after=(tl3,) + ((st_k.get("psg_l"),) if (jp == 0 and st_k.get("psg_l")) else ()))
                st_k["lsum_war_%d" % pbp] = lmm
                pv_tok = None
                for c in range(4):
                    pv_tok = p.op("tensor", "matmul", pv_cur[:, :],
                                  v_t[:, jp * 512 + c * 128:jp * 512 + (c + 1) * 128],
                                  pT[:, c0 + c * 512:c0 + (c + 1) * 512],
                                  start=(jp == 0 and c == 0),
                                  stop=(jp == NSLOT - 1 and c == 3),
                                  after=((ea_p,) if c < 2 else (eb_p,)) +
                                        ((st_k.get("psg_pv_%d" % par),) if (jp == 0 and c == 0 and st_k.get("psg_pv_%d" % par)) else ()),
                                  sem=ctx.pe)
                st_k["war"][jp] = pv_tok
                st_k["pT_war"][pbp] = (pv_tok, tl3)
                pump()
            if j < NSLOT:
                prev = (j, j % 2, e_a, e_b)
        lpv_done = p.tok(ctx.pe)
        trc = p.op("vector", "reciprocal", rl[:, :], ps_l[0:1, :], sem=ctx.dv,
                   after=(lpv_done,))
        tb = p.op("tensor", "matmul", ps_bc[:, :], ctx.ones32[0:1, 0:128], rl[0:1, :],
                  start=True, stop=True, sem=ctx.pe,
                  after=(trc,) + ((st_k["bc_war"],) if st_k["bc_war"] else ()))
        tc = p.op("vector", "tensor_copy", linv[:, :], ps_bc[:, :], sem=ctx.dv, after=(tb,))
        st_k["bc_war"] = tc
        aw = (st_k["attn_war"],) if st_k["attn_war"] else ()
        ta = p.op("vector", "tensor_mul", attn[:, h * 512:(h + 1) * 512],
                  pv_cur[:, :], linv[:, :], sem=ctx.dv, after=(tc,) + aw)
        st_k["psg_pv_%d" % par] = ta
        st_k["psg_l"] = trc
        return ta

    pump(); pump(); pump()
    for q in range(2):
        attn_tok = None
        for h in range(16):
            attn_tok = flash_head(h, q)
        po_war = None
        for mq in range(4):
            for n in range(4):
                mm = None
                p.wait_tok("tensor", tw)
                p.wait_tok("tensor", attn_tok)
                if po_war is not None:
                    p.wait_tok("tensor", po_war)
                for k in range(16):
                    mm = p.op("tensor", "matmul", ps_o[:, :],
                              attn[:, k * 512 + mq * 128:k * 512 + (mq + 1) * 128],
                              wsb[:, k * 2048 + n * 512:k * 2048 + (n + 1) * 512],
                              start=(k == 0), stop=(k == 15),
                              sem=(ctx.pe if k == 15 else None))
                sl = (mq * 4 + n) % 2
                xa = xrt[:, sl * 512:(sl + 1) * 512]
                ra = rout[:, sl * 512:(sl + 1) * 512]
                war = st_k.get("oslot_war_%d" % sl)
                tx = p.op("sync", "dma_start", sem=ctx.ls, n=16, out=xa,
                          in_=x[q * 512 + mq * 128:q * 512 + (mq + 1) * 128,
                                n * 512:(n + 1) * 512],
                          after=((war,) if war else ()))
                tadd = p.op("vector", "tensor_add", ra, ps_o[:, :], xa, sem=ctx.dv,
                            after=(mm, tx) + ((war,) if war else ()))
                po_war = tadd
                st_k["oslot_war_%d" % sl] = p.op(
                    "sync", "dma_start", sem=ctx.st, n=16,
                    out=o_r3[q * 512 + mq * 128:q * 512 + (mq + 1) * 128,
                             n * 512:(n + 1) * 512],
                    in_=ra, after=(tadd,))
        st_k["attn_war"] = p.tok(ctx.pe)
        st_k["psg_pv_1"] = (ctx.dv, ctx.p.val(ctx.dv))
        st_k["psg_pv_0"] = (ctx.dv, ctx.p.val(ctx.dv))
    if mega:
        ctx.barrier()
    ctx.al.end_stage()
    return ctx


def run_l1attn(g, r2_cores, d_res):
    ctx = build_l1attn()
    nc = ctx.finish()
    K1_all = np.concatenate([d_res[i]["K1"] for i in range(NCORES)], axis=1)
    V1_all = np.concatenate([d_res[i]["V1"] for i in range(NCORES)], axis=0)
    kn21_all = np.concatenate([d_res[i]["kn21"] for i in range(NCORES)], axis=1)
    in_maps = []
    for i in range(NCORES):
        kin = np.zeros((2048, 2 * 1536), K1_all.dtype)
        vin = np.zeros((2 * 1536, 2048), V1_all.dtype)
        k2in = np.zeros((16, 2 * 1536), np.float32)
        km = np.zeros((128, 6), np.float32)
        for q in range(2):
            q0 = 1024 * i + 512 * q
            lo = q0 - 1024
            for s in range(3):
                a, b_ = lo + s * 512, lo + (s + 1) * 512
                if b_ <= 0:
                    km[:, q * 3 + s] = NEG
                    continue
                aa = max(a, 0)
                pad = aa - a
                kin[:, q * 1536 + s * 512 + pad:q * 1536 + (s + 1) * 512] = K1_all[:, aa:b_]
                vin[q * 1536 + s * 512 + pad:q * 1536 + (s + 1) * 512, :] = V1_all[aa:b_, :]
                k2in[:, q * 1536 + s * 512 + pad:q * 1536 + (s + 1) * 512] = kn21_all[:, aa:b_]
                if pad > 0 and s != 2:
                    pass
        m = {"resid2": r2_cores[i], "Q1": d_res[i]["Q1"], "qn21": d_res[i]["qn21"],
             "K1in": kin, "V1in": vin, "kn21in": k2in, "keymask1": km,
             "maskC": g["maskC"], "maskW": g["maskW"], "Wo": g["Wo"], "ident": g["ident"]}
        in_maps.append(m)
    res = run_bass_kernel_spmd(nc, in_maps, core_ids=list(range(NCORES)))
    return res.results


def build_l1moe(mega=None):
    ctx = mega["ctx"] if mega else Ctx("l1moe")
    ctx.al.begin_stage("f")
    ctx.psg = None
    ctx.strowg = {0: None, 1: None}
    p, al = ctx.p, ctx.al
    r3 = mega["io"]["r3"] if mega else ctx.din("resid3", [T, D])
    rw_d = ctx.din("rwT", [D, 8])
    rb_d = ctx.din("rbrow", [1, 8])
    gw_d = ctx.din("gw", [8, D, D])
    uw_d = ctx.din("uw", [8, D, D])
    gb_d = ctx.din("gbcol", [8, 128, 16])
    ub_d = ctx.din("ubcol", [8, 128, 16])
    dw_d = ctx.din("dw", [8, D, D])
    db_d = ctx.din("dbrow", [8, D])
    ctx.din("ident", [128, 128], BF16)
    o_out = ctx.dout("out_sl", [T, D])

    if not hasattr(ctx, "ident"):
        ctx.common()
    resid = al.sb("resid", [128, 8 * D], F32)
    lnx = al.sb("lnx", [128, 16 * T], BF16)
    lnx_tm = al.sb("lnx_tm", [128, D], BF16)
    xt = al.sb("xt", [128, D], F32)
    WQ = 8192
    wsb = al.sb("wsb", [128, 2 * WQ], BF16)
    gsb = al.sb("gsb", [128, 16 * 512], BF16)
    act = al.sb("act", [128, 16 * 1024], BF16)
    u2s = al.sb("u2s", [128, 512], BF16)
    sgs = al.sb("sgs", [128, 512], BF16)
    rw_sb = al.sb("sb_rw", [128, 16 * 8], BF16)
    rb_sb = al.sb("sb_rb", [1, 8], BF16)
    gb = al.sb("sb_gb", [128, 16], F32)
    ub = al.sb("sb_ub", [128, 16], F32)
    db = al.sb("sb_db", [1, D], BF16)
    logits = al.sb("logits", [128, 8 * 8], F32)
    scores = al.sb("scores", [128, 8 * 8], F32)
    m1 = al.sb("sb_m1", [128, 1], F32)
    m2 = al.sb("sb_m2", [128, 1], F32)
    w1 = al.sb("sb_w1", [128, 1], F32)
    w2 = al.sb("sb_w2", [128, 1], F32)
    msk = al.sb("sb_msk", [128, 8], F32)
    msk2 = al.sb("sb_msk2", [128, 8], F32)
    scr_sq = al.sb("scr_sq", [128, D], BF16)
    stats = {k: al.sb("st_" + k, [128, 1], F32) for k in ["ssq", "smean", "m", "var", "rstd"]}
    ps_a = al.ps("ps_a", [128, 512])
    ps_b = al.ps("ps_b", [128, 512])
    ps_tr = al.ps("ps_tr", [128, 2048], BF16)
    ps_rt = al.ps("ps_rt", [128, 64])
    ps_d = al.ps("ps_d", [128, 512])
    ps_d2 = al.ps("ps_d2", [128, 512])
    pool = SlotPool([ps_a, ps_b])
    dpool = SlotPool([ps_d, ps_d2])
    trpool = SlotPool([ps_tr[:, 0:512], ps_tr[:, 1024:1536]])

    trw = p.op("gpsimd", "dma_start", sem=ctx.ld, n=16,
               out=rw_sb[:, :].rearrange("p (k e) -> p k e", k=16),
               in_=rw_d.rearrange("(k p) e -> p k e", p=128))
    trb = p.op("gpsimd", "dma_start", sem=ctx.ld, n=16, out=rb_sb[:, :], in_=rb_d[:, :])

    res_toks = []
    tm_war = [None]
    lnx_ready = None
    for a in range(8):
        tx = p.op("sync", "dma_start", sem=ctx.ls, n=16, out=resid[:, a * D:(a + 1) * D],
                  in_=r3[a * 128:(a + 1) * 128, :])
        res_toks.append(tx)
        tn = emit_ln(ctx, resid[:, a * D:(a + 1) * D], lnx_tm[:, :], scr_sq, stats,
                     (tx,) + ((tm_war[0],) if tm_war[0] is not None else ()))
        p.wait_tok("tensor", tn)
        for g4 in range(4):
            srcs = [lnx_tm[:, (g4 * 4 + j) * 128:(g4 * 4 + j) * 128 + 128]
                    for j in range(4)]
            dst = lnx[:, :].rearrange("p (k t) -> p k t", k=16)[:, g4 * 4:(g4 + 1) * 4,
                                                               a * 128:(a + 1) * 128]
            lnx_ready = transpose_group(ctx, srcs, dst, trpool.take())
        tm_war[0] = p.tok(ctx.pe)

    sc_toks = []
    rt_guard = None
    for a in range(8):
        mm = None
        p.wait_tok("tensor", trw)
        p.wait_tok("tensor", lnx_ready)
        if rt_guard is not None:
            p.wait_tok("tensor", rt_guard)
        for k in range(16):
            mm = p.op("tensor", "matmul", ps_rt[:, a * 8:(a + 1) * 8],
                      lnx[:, k * T + a * 128:k * T + (a + 1) * 128],
                      rw_sb[:, k * 8:(k + 1) * 8],
                      start=(k == 0), stop=False)
        mm = p.op("tensor", "matmul", ps_rt[:, a * 8:(a + 1) * 8], ctx.ones[0:1, 0:128],
                  rb_sb[0:1, :], start=False, stop=True, sem=ctx.pe, after=(trb,))
        la = logits[:, a * 8:(a + 1) * 8]
        sa = scores[:, a * 8:(a + 1) * 8]
        t1 = p.op("vector", "tensor_copy", la, ps_rt[:, a * 8:(a + 1) * 8], sem=ctx.dv,
                  after=(mm,))
        rt_guard = t1
        t2 = p.op("vector", "reduce_max", m1[:, :], la, axis=AX.X, sem=ctx.dv, after=(t1,))
        t3 = p.op("vector", "tensor_single_scalar", msk[:, :], la, m1[:, :],
                  ALU.is_equal, sem=ctx.dv, after=(t2,))
        t4 = p.op("vector", "scalar_tensor_tensor", msk2[:, :], msk[:, :], -1.0e30, la,
                  ALU.mult, ALU.add, sem=ctx.dv, after=(t3,))
        t5 = p.op("vector", "reduce_max", m2[:, :], msk2[:, :], axis=AX.X, sem=ctx.dv,
                  after=(t4,))
        t6 = p.op("vector", "tensor_sub", w1[:, :], m1[:, :], m2[:, :], sem=ctx.dv,
                  after=(t5,))
        t7 = p.op("scalar", "activation", w1[:, :], w1[:, :], AF.Sigmoid, sem=ctx.sc,
                  after=(t6,))
        t8 = p.op("vector", "tensor_scalar", w2[:, :], w1[:, :], -1.0, 1.0,
                  ALU.mult, ALU.add, sem=ctx.dv, after=(t7,))
        t9 = p.op("vector", "tensor_single_scalar", msk2[:, :], la, m2[:, :],
                  ALU.is_equal, sem=ctx.dv, after=(t8,))
        t10 = p.op("vector", "tensor_scalar_mul", sa, msk[:, :], w1[:, :], sem=ctx.dv,
                   after=(t9,))
        t11 = p.op("vector", "scalar_tensor_tensor", sa, msk2[:, :], w2[:, :], sa,
                   ALU.mult, ALU.add, sem=ctx.dv, after=(t10,))
        sc_toks.append(t11)

    wpool = SlotPool([wsb[:, i * 4096:(i + 1) * 4096] for i in range(4)])
    gch = al.sb("gch", [128, 512], BF16)
    sgch = al.sb("sgch", [128, 512], BF16)
    u2ch = al.sb("u2ch", [128, 512], BF16)
    glch = al.sb("glch", [128, 512], BF16)
    act_war = None
    bias_war = None
    db_war = None
    dpool2 = SlotPool([wsb[:, 0:8192], wsb[:, 8192:16384]])
    sc_tok = sc_toks[-1]
    res_out_tok = {}
    for e in range(8):
        tgb = p.op("sync", "dma_start", sem=ctx.ls, n=16, out=gb[:, :], in_=gb_d[e, :, :],
                   after=(bias_war,) if bias_war else ())
        tub = p.op("sync", "dma_start", sem=ctx.ls, n=16, out=ub[:, :], in_=ub_d[e, :, :],
                   after=(bias_war,) if bias_war else ())
        tdb = p.op("gpsimd", "dma_start", sem=ctx.ld, n=16, out=db[:, :],
                   in_=db_d[e:e + 1, :], after=(db_war,) if db_war else ())
        last_act = None
        last_ts = None
        for pd in range(8):
            wg_s = wpool.take()
            wg, wgw = wg_s
            twg = p.op("gpsimd", "dma_start", sem=ctx.ld, n=16,
                       out=wg[:, :].rearrange("p (k m) -> p k m", k=16),
                       in_=gw_d[e, :, pd * 256:(pd + 1) * 256].rearrange(
                           "(k p) m -> p k m", p=128),
                       after=(wgw,) if wgw else ())
            wu_s = wpool.take()
            wu, wuw = wu_s
            twu = p.op("gpsimd", "dma_start", sem=ctx.ld, n=16,
                       out=wu[:, :].rearrange("p (k m) -> p k m", k=16),
                       in_=uw_d[e, :, pd * 256:(pd + 1) * 256].rearrange(
                           "(k p) m -> p k m", p=128),
                       after=(wuw,) if wuw else ())
            for mi in range(2):
                m = pd * 2 + mi
                for t in range(2):
                    gslot = pool.take()
                    gps, gwar = gslot
                    p.wait_tok("tensor", twg)
                    p.wait_tok("tensor", lnx_ready)
                    if act_war is not None:
                        p.wait_tok("tensor", act_war)
                    if gwar is not None:
                        p.wait_tok("tensor", gwar)
                    gmm = None
                    for k in range(16):
                        gmm = p.op("tensor", "matmul", gps[:, :],
                                   wg[:, k * 256 + mi * 128:k * 256 + (mi + 1) * 128],
                                   lnx[:, k * T + t * 512:k * T + (t + 1) * 512],
                                   start=(k == 0), stop=(k == 15),
                                   sem=(ctx.pe if k == 15 else None))
                    uslot = pool.take()
                    ups, uwar = uslot
                    p.wait_tok("tensor", twu)
                    if uwar is not None:
                        p.wait_tok("tensor", uwar)
                    umm = None
                    for k in range(16):
                        umm = p.op("tensor", "matmul", ups[:, :],
                                   wu[:, k * 256 + mi * 128:k * 256 + (mi + 1) * 128],
                                   lnx[:, k * T + t * 512:k * T + (t + 1) * 512],
                                   start=(k == 0), stop=(k == 15),
                                   sem=(ctx.pe if k == 15 else None))
                    tg1 = p.op("vector", "tensor_scalar", gch[:, :], gps[:, :],
                               gb[:, m:m + 1], 7.0, ALU.add, ALU.min, sem=ctx.dv,
                               after=(gmm, tgb) + ((last_ts,) if last_ts else ()))
                    gslot[1] = tg1
                    tsg = p.op("scalar", "activation", sgch[:, :], gch[:, :], AF.Sigmoid,
                               scale=1.702, sem=ctx.sc, after=(tg1,))
                    tu1 = p.op("vector", "tensor_scalar", u2ch[:, :], ups[:, :],
                               ub[:, m:m + 1], -7.0, ALU.add, ALU.max, sem=ctx.dv,
                               after=(umm, tub))
                    uslot[1] = tu1
                    tu2 = p.op("vector", "tensor_scalar", u2ch[:, :], u2ch[:, :],
                               7.0, 1.0, ALU.min, ALU.add, sem=ctx.dv, after=(tu1,))
                    tgl = p.op("vector", "tensor_mul", glch[:, :], gch[:, :], sgch[:, :],
                               sem=ctx.dv, after=(tsg,))
                    last_act = p.op("vector", "tensor_mul",
                                    act[:, m * 1024 + t * 512:m * 1024 + (t + 1) * 512],
                                    u2ch[:, :], glch[:, :], sem=ctx.dv, after=(tgl, tu2))
                    last_ts = last_act
            piece_done = p.tok(ctx.pe)
            wg_s[1] = piece_done
            wu_s[1] = piece_done
        bias_war = last_act
        dn_last = None
        for nd in range(4):
            wd_s = dpool2.take()
            wd, wdw = wd_s
            p.wait_tok("gpsimd", last_act)
            twd = p.op("gpsimd", "dma_start", sem=ctx.ld, n=16,
                       out=wd[:, :].rearrange("p (k m) -> p k m", k=16),
                       in_=dw_d[e, :, nd * 512:(nd + 1) * 512].rearrange(
                           "(k p) m -> p k m", p=128),
                       after=((wdw,) if wdw else ()) + ((act_war,) if act_war else ()))
            for t in range(2):
                for mq in range(4):
                    dslot = dpool.take()
                    dps, dwar = dslot
                    p.wait_tok("tensor", twd)
                    p.wait_tok("tensor", last_act)
                    if dwar is not None:
                        p.wait_tok("tensor", dwar)
                    mm = None
                    for k in range(16):
                        mm = p.op("tensor", "matmul", dps[:, :],
                                  act[:, k * 1024 + t * 512 + mq * 128:
                                      k * 1024 + t * 512 + (mq + 1) * 128],
                                  wd[:, k * 512:(k + 1) * 512],
                                  start=(k == 0), stop=False)
                    mm = p.op("tensor", "matmul", dps[:, :], ctx.ones[0:1, 0:128],
                              db[0:1, nd * 512:(nd + 1) * 512], start=False, stop=True,
                              sem=ctx.pe, after=(tdb,))
                    dn_last = mm
                    a = t * 4 + mq
                    rkey = (a, nd)
                    tacc = p.op("vector", "scalar_tensor_tensor",
                                resid[:, a * D + nd * 512:a * D + (nd + 1) * 512],
                                dps[:, :], scores[:, a * 8 + e:a * 8 + e + 1],
                                resid[:, a * D + nd * 512:a * D + (nd + 1) * 512],
                                ALU.mult, ALU.add, sem=ctx.dv,
                                after=(mm, sc_tok, res_toks[a]) +
                                      ((res_out_tok.get(rkey),) if res_out_tok.get(rkey) else ()))
                    dslot[1] = tacc
                    res_out_tok[rkey] = tacc
            wd_s[1] = dn_last
        act_war = dn_last
        db_war = dn_last
        for tile in wpool.tiles:
            tile[1] = dn_last

    for a in range(8):
        toks = tuple(res_out_tok[(a, nd)] for nd in range(4))
        p.op("sync", "dma_start", sem=ctx.st, n=16, out=o_out[a * 128:(a + 1) * 128, :],
             in_=resid[:, a * D:(a + 1) * D], after=toks)
    ctx.al.end_stage()
    return ctx


def run_l1moe(g, r3_cores):
    ctx = build_l1moe()
    nc = ctx.finish()
    in_maps = []
    for i in range(NCORES):
        m = {"rwT": g["rwT"], "rbrow": g["rbrow"], "gw": g["gw"], "uw": g["uw"],
             "gbcol": g["gbcol"], "ubcol": g["ubcol"], "dw": g["dw"], "dbrow": g["dbrow"],
             "ident": g["ident"], "resid3": r3_cores[i]}
        in_maps.append(m)
    res = run_bass_kernel_spmd(nc, in_maps, core_ids=list(range(NCORES)))
    return res.results


def kernel(**inputs):
    return kernel_mega(**inputs)


def kernel_unfused(**inputs):
    g = host_prep(inputs)
    a_res = run_l0pre(g)
    b_res = run_l0attn(g, a_res)
    r1 = [b_res[i]["resid1"] for i in range(NCORES)]
    c_res = run_l0ffn(g, r1)
    r2 = [c_res[i]["resid2"] for i in range(NCORES)]
    d_res = run_l1pre(g, r2)
    e_res = run_l1attn(g, r2, d_res)
    r3 = [e_res[i]["resid3"] for i in range(NCORES)]
    f_res = run_l1moe(g, r3)
    out = np.concatenate([f_res[i]["out_sl"] for i in range(NCORES)], axis=0)
    return out.reshape(1, SEQ, D).astype(np.float32)


def build_mega(debug=False):
    ctx = Ctx("mega")
    io = {}
    dt = ctx.dint
    io["Qn0"] = dt("i_Qn0", [2048, T], BF16)
    io["Qr0"] = dt("i_Qr0", [1024, T], BF16)
    io["Kn0"] = dt("i_Kn0", [2048, T], BF16)
    io["Kr0"] = dt("i_Kr0", [1024, T], BF16)
    io["V0"] = dt("i_V0", [T, 2048], BF16)
    io["qn20"] = dt("i_qn20", [16, T], F32)
    io["kn20"] = dt("i_kn20", [16, T], F32)
    io["KnAG"] = dt("i_KnAG", [8 * 2048, T], BF16)
    io["KrAG"] = dt("i_KrAG", [8 * 1024, T], BF16)
    io["VAG"] = dt("i_VAG", [8 * T, 2048], BF16)
    io["kn2AG"] = dt("i_kn2AG", [8 * 16, T], F32)
    io["r1"] = dt("i_r1", [T, D], F32)
    io["r2"] = dt("i_r2", [T, D], F32)
    io["r3"] = dt("i_r3", [T, D], F32)
    io["Q1loc"] = dt("i_Q1loc", [2048, T], BF16)
    io["K1loc"] = dt("i_K1loc", [2048, T], BF16)
    io["V1loc"] = dt("i_V1loc", [T, 2048], BF16)
    io["qn21loc"] = dt("i_qn21loc", [16, T], F32)
    io["kn21loc"] = dt("i_kn21loc", [16, T], F32)
    io["K1AG"] = dt("i_K1AG", [8 * 2048, T], BF16)
    io["V1AG"] = dt("i_V1AG", [8 * T, 2048], BF16)
    io["kn21AG"] = dt("i_kn21AG", [8 * 16, T], F32)
    io["K1nb"] = dt("i_K1nb", [2048, T], BF16)
    io["V1nb"] = dt("i_V1nb", [T, 2048], BF16)
    io["kn21nb"] = dt("i_kn21nb", [16, T], F32)
    io["mscrB"] = dt("i_mscrB", [16, 2], F32)
    io["mscrE"] = dt("i_mscrE", [16, 2], F32)
    for bn in ("bar1", "bar1o", "bar2", "bar2o", "bar3", "bar3o"):
        io[bn] = dt("i_" + bn, [128, 16], F32)
    mega = {"ctx": ctx, "io": io}
    p = ctx.p

    build_l0pre(mega=mega)
    groups = [list(range(NCORES))]
    build_l0attn(mega=mega)
    build_l0ffn(mega=mega)
    build_l1pre(mega=mega)
    build_l1attn(mega=mega)
    build_l1moe(mega=mega)
    if debug:
        ctx.barrier()
        p = ctx.p
        for nm, shape, dt_ in (("r1", [T, D], F32), ("r2", [T, D], F32), ("r3", [T, D], F32),
                               ("Qn0", [2048, T], BF16), ("Kn0", [2048, T], BF16),
                               ("qn20", [16, T], F32), ("kn20", [16, T], F32),
                               ("kn2AG", [128, T], F32), ("K1loc", [2048, T], BF16),
                               ("K1nb", [2048, T], BF16), ("V1nb", [T, 2048], BF16),
                               ("qn21loc", [16, T], F32)):
            o = ctx.dout("dbg_" + nm, shape, dt_)
            sl = tuple(slice(None) for _ in shape)
            p.op("sync", "dma_start", sem=ctx.st, n=16, out=o[sl], in_=io[nm][sl])
        kslice = ctx.dout("dbg_KnAG", [2048, T], BF16)
        p.op("sync", "dma_start", sem=ctx.st, n=16, out=kslice[:, :], in_=io["KnAG"][2 * 2048:3 * 2048, :])
        for rr in range(8):
            vsl = ctx.dout("dbg_V1AG%d" % rr, [T, 2048], BF16)
            p.op("sync", "dma_start", sem=ctx.st, n=16, out=vsl[:, :],
                 in_=io["V1AG"][rr * T:(rr + 1) * T, :])
    return ctx


def mega_in_map(g, i):
    km = np.zeros((128, 2 * 15), np.float32)
    for q in range(2):
        dq = 2 * i + q
        km[:, q * 15 + dq:(q + 1) * 15] = NEG
    km1 = np.zeros((128, 6), np.float32)
    if i == 0:
        km1[:, 0] = NEG
        km1[:, 1] = NEG
        km1[:, 3] = NEG
    m = {
        "x_sl": g["x"][i * T:(i + 1) * T],
        "w_cq": g["w_cq"], "w_dq_nope": g["w_dq_nope"], "w_dq_rope_p": g["w_dq_rope_p"],
        "w_ckv": g["w_ckv"], "w_dk_nope": g["w_dk_nope"], "w_dv": g["w_dv"],
        "w_k_rope_p": g["w_k_rope_p"], "qg": g["qg"], "kvg": g["kvg"],
        "crow": g["crow"], "srow": g["srow"], "ident": g["ident"], "i64": g["i64"],
        "mswap": g["mswap"], "keymask": km, "maskC": g["maskC"], "maskW": g["maskW"],
        "w_proj": g["w_proj"], "W0T": g["W0T"], "b0col": g["b0col"], "W1T": g["W1T"],
        "b1row": g["b1row"], "Wqkv": g["Wqkv"], "qg1": g["qg1"], "kg1": g["kg1"],
        "pswap": g["pswap"],
        "cosT_sl": np.ascontiguousarray(_bf(g["cosT"])[:, i * T:(i + 1) * T]),
        "sinT_sl": np.ascontiguousarray(_bf(g["sinT"])[:, i * T:(i + 1) * T]),
        "keymask1": km1, "Wo": g["Wo"], "rwT": g["rwT"], "rbrow": g["rbrow"],
        "gw": g["gw"], "uw": g["uw"], "gbcol": g["gbcol"], "ubcol": g["ubcol"],
        "dw": g["dw"], "dbrow": g["dbrow"],
    }
    return m


def kernel_mega(**inputs):
    g = host_prep(inputs)
    ctx = build_mega()
    nc = ctx.finish()
    in_maps = [mega_in_map(g, i) for i in range(NCORES)]
    res = run_bass_kernel_spmd(nc, in_maps, core_ids=list(range(NCORES)))
    out = np.concatenate([res.results[i]["out_sl"] for i in range(NCORES)], axis=0)
    return out.reshape(1, SEQ, D).astype(np.float32)
